# version 06
# baseline (speedup 1.0000x reference)
import numpy as np
from contextlib import ExitStack

import concourse.bass as bass
import concourse.mybir as mybir
from concourse.bass_utils import run_bass_kernel_spmd

F32 = mybir.dt.float32
BF16 = mybir.dt.bfloat16
AF = mybir.ActivationFunctionType
ALU = mybir.AluOpType
AX = mybir.AxisListType.X

P = 128
TOKS = 1024
NCH = 16
NH = 16
HG = 4
NG = NH // HG
BLK = 256
NBLK = TOKS // BLK
DH = 128
FFN_SCALE = 0.5 / 64.0
NEG_BIG = -1.0e9

SB0 = 16512
X_BASE = SB0
X_SLOT = 4096
NSLOT = 17
XN_BASE = SB0 + 69632
OB_BASE = SB0 + 102400
WK_BASE = SB0 + 135168
CONST_BASE = SB0 + 196608

KVQ = [X_BASE, X_BASE + 24576]
WSTR = [X_BASE + 49152, X_BASE + 57344]

WK_S = [WK_BASE, WK_BASE + 8192]
WK_T = [WK_BASE + 16384, WK_BASE + 20480]
WK_KVI = [WK_BASE + 24576, WK_BASE + 26624]
WK_D = WK_BASE + 28672
WK_QDB = WK_BASE + 36864
WK_KS = [WK_BASE + 40960, WK_BASE + 43008]
WK_QS = [WK_BASE + 45056, WK_BASE + 45568]
WK_QKD = [WK_BASE + 46080, WK_BASE + 47104]
WK_GTMP = WK_BASE + 48128
WK_OSQ = [WK_BASE + 50176, WK_BASE + 50688]

WK_ABF = WK_BASE
WK_BBF = WK_BASE + 16384
WK_WRBF = WK_BASE + 32768
WK_XB = WK_BASE + 33280
WK_SQ = WK_BASE + 41472
WK_TMP = WK_BASE + 49664
WK_ROWA = WK_BASE + 53760
WK_LGS = WK_BASE + 57856
WK_EXB = WK_BASE + 58112
WK_OHB = WK_BASE + 58368
WK_MXB = WK_BASE + 58624
WK_NMX = WK_BASE + 58656
WK_SEB = WK_BASE + 58688
WK_WB = WK_BASE + 58720
WK_GST = [WK_BASE, WK_BASE + 8192]
WK_SG = [WK_BASE + 16384, WK_BASE + 18432]
WK_T2 = [WK_BASE + 20480, WK_BASE + 22528]

OB_MASKB = OB_BASE
OB_HM = OB_BASE + 16384
OB_MWTM = OB_BASE + 24576
OB_MWFM = OB_BASE + 24832

_CO = [CONST_BASE]


def _co(nbytes, align=64):
    _CO[0] = (_CO[0] + align - 1) // align * align
    off = _CO[0]
    _CO[0] += nbytes
    return off


class Builder:
    def __init__(self, taps=()):
        self.taps = set(taps)
        nc = bass.Bass("TRN2", target_bir_lowering=False)
        self.nc = nc
        self.es = ExitStack()

        dt = nc.dram_tensor
        self.x_in = dt("x_in", [2048, TOKS], F32, kind="ExternalInput")
        self.wqkv = dt("wqkv", [2048, 6144], F32, kind="ExternalInput")
        self.wgate = dt("wgate", [2048, 2048], F32, kind="ExternalInput")
        self.woutw = dt("woutw", [2048, 2048], F32, kind="ExternalInput")
        self.a_t = dt("a_t", [2048, 512], F32, kind="ExternalInput")
        self.b_t = dt("b_t", [512, 2048], F32, kind="ExternalInput")
        self.wr_t = dt("wr_t", [2048, 8], F32, kind="ExternalInput")
        self.br_in = dt("br_in", [1, 8], F32, kind="ExternalInput")
        self.rmsw_in = dt("rmsw_in", [P, NCH], F32, kind="ExternalInput")
        self.s_col_in = dt("s_col_in", [NH, 1], F32, kind="ExternalInput")
        self.flag_in = dt("flag_in", [P, 1], F32, kind="ExternalInput")
        self.idx_in = dt("idx_in", [P, 512], F32, kind="ExternalInput")
        self.gq_in = dt("gq_in", [NH, BLK], F32, kind="ExternalInput")
        self.gk_in = dt("gk_in", [NH, BLK], F32, kind="ExternalInput")
        self.gbd_in = dt("gbd_in", [NH, NBLK], F32, kind="ExternalInput")
        self.eexp_in = dt("eexp_in", [8, 512], F32, kind="ExternalInput")
        self.out_t = dt("out_t", [2048, TOKS], F32, kind="ExternalOutput")
        self.xspill = dt("xspill", [P, NSLOT * TOKS], F32)
        self.wqkv_bf = dt("wqkv_bf", [2048, 6144], BF16)
        self.wg_bf = dt("wg_bf", [2048, 2048], BF16)
        self.wo_bf = dt("wo_bf", [2048, 2048], BF16)
        self.qd_dram = dt("qd_dram", [NH, BLK], F32)
        self.bdp_dram = dt("bdp_dram", [NH, NBLK], F32)
        self.cc_in = [dt(f"cc_in{g}", [HG * P, DH], F32) for g in range(NG)]
        self.cc_out = [dt(f"cc_out{g}", [2 * HG * P, DH], F32) for g in range(NG)]

        at = nc.alloc_sbuf_tensor_at
        self.xslot = [at(f"xslot{i}", [P, TOKS], F32, offset=X_BASE + i * X_SLOT)
                      for i in range(NSLOT)]
        self.xall = at("xall", [P, NSLOT * TOKS], F32, offset=X_BASE)
        self.slotmap = list(range(NCH))
        self.spare = NCH
        self.xn = at("xn", [P, NCH * TOKS], BF16, offset=XN_BASE)
        self.o_bf = at("o_bf", [P, NH * TOKS], BF16, offset=OB_BASE)
        self.k_g = [at(f"k_g{i}", [P, HG * TOKS], BF16, offset=KVQ[i]) for i in range(2)]
        self.v_g = [at(f"v_g{i}", [P, HG * TOKS], BF16, offset=KVQ[i] + 8192)
                    for i in range(2)]
        self.q_g = [at(f"q_g{i}", [P, HG * TOKS], BF16, offset=KVQ[i] + 16384)
                    for i in range(2)]
        self.wstr = [at(f"wstr{i}", [P, NCH * 256], BF16, offset=WSTR[i]) for i in range(2)]
        self.S_g = [at(f"S_g{i}", [P, HG * 4 * DH], F32, offset=WK_S[i]) for i in range(2)]
        self.T_g = [at(f"T_g{i}", [P, HG * 4 * DH], BF16, offset=WK_T[i]) for i in range(2)]
        self.kvi_g = [at(f"kvi_g{i}", [P, HG * DH], F32, offset=WK_KVI[i]) for i in range(2)]
        self.D_g = at("D_g", [P, HG * 512], F32, offset=WK_D)
        self.qdB_g = at("qdB_g", [P, HG * BLK], F32, offset=WK_QDB)
        self.ks_h = [at(f"ks_h{i}", [P, TOKS], BF16, offset=WK_KS[i]) for i in range(2)]
        self.qs_b = [at(f"qs_b{i}", [P, BLK], BF16, offset=WK_QS[i]) for i in range(2)]
        self.qkD = [at(f"qkD{i}", [P, 512], BF16, offset=WK_QKD[i]) for i in range(2)]
        self.gtmp = at("gtmp", [P, 512], F32, offset=WK_GTMP)
        self.osq = [at(f"osq{i}", [P, BLK], BF16, offset=WK_OSQ[i]) for i in range(2)]
        self.a_bf = at("a_bf", [P, NCH * 512], BF16, offset=WK_ABF)
        self.b_bf = at("b_bf", [P, 4 * 2048], BF16, offset=WK_BBF)
        self.wr_bf = at("wr_bf", [P, NCH * 8], BF16, offset=WK_WRBF)
        self.xb = at("xb", [P, 4 * TOKS], BF16, offset=WK_XB)
        self.sq = at("sq", [P, 4 * TOKS], BF16, offset=WK_SQ)
        self.tmp = at("tmp", [P, TOKS], F32, offset=WK_TMP)
        self.rows_a = at("rows_a", [P, TOKS], F32, offset=WK_ROWA)
        self.rstd_sb = at("rstd_sb", [P, TOKS], F32, offset=WK_XB)
        self.nm_r = at("nm_r", [1, TOKS], F32, offset=WK_XB + 4096)
        self.m2_r = at("m2_r", [1, TOKS], F32, offset=WK_TMP)
        self.rstd_r = at("rstd_r", [1, TOKS], F32, offset=WK_SQ)
        self.nmr_r = at("nmr_r", [1, TOKS], F32, offset=WK_SQ + 4096)
        self.lgs = at("lgs", [P, 64], F32, offset=WK_LGS)
        self.exb = at("exb", [P, 64], F32, offset=WK_EXB)
        self.ohb = at("ohb", [P, 64], F32, offset=WK_OHB)
        self.mxb = at("mxb", [P, 8], F32, offset=WK_MXB)
        self.nmxb = at("nmxb", [P, 8], F32, offset=WK_NMX)
        self.seb = at("seb", [P, 8], F32, offset=WK_SEB)
        self.wb = at("wb", [P, 8], F32, offset=WK_WB)
        self.mask_b = at("mask_b", [P, 4 * TOKS], F32, offset=OB_MASKB)
        self.hm = at("hm", [P, 4 * TOKS], BF16, offset=OB_HM)
        self.mw_tm = at("mw_tm", [P, 64], F32, offset=OB_MWTM)
        self.mw_fm = at("mw_fm", [8, TOKS], F32, offset=OB_MWFM)
        self.gst = [at(f"gst{i}", [P, NCH * 256], BF16, offset=WK_GST[i]) for i in range(2)]
        self.sg_b = [at(f"sg_b{i}", [P, TOKS], F32, offset=WK_BASE + 16384 + i * 4096)
                     for i in range(2)]
        self.t2_b = [at(f"t2_b{i}", [P, TOKS], F32, offset=WK_BASE + 24576 + i * 4096)
                     for i in range(2)]
        self.idBF = at("idBF", [P, P], BF16, offset=_co(256))
        self.idF32 = at("idF32", [P, P], F32, offset=_co(512))
        self.ones_row_bf = at("ones_row_bf", [1, P], BF16, offset=_co(256))
        self.ones_row_f = at("ones_row_f", [1, P], F32, offset=_co(512))
        self.onesP_bf = at("onesP_bf", [P, 1], BF16, offset=_co(32))
        self.eps_ln = at("eps_ln", [1, 1], F32, offset=_co(32))
        self.eps_rms = at("eps_rms", [1, 1], F32, offset=_co(32))
        self.s_col = at("s_col", [NH, 1], F32, offset=_co(32))
        self.gq = at("gq", [NH, BLK], F32, offset=_co(1024))
        self.gk = at("gk", [NH, BLK], F32, offset=_co(1024))
        self.gbd = at("gbd", [NH, NBLK], F32, offset=_co(32))
        self.qd16 = at("qd16", [NH, BLK], F32, offset=_co(1024))
        self.kd16 = at("kd16", [NH, BLK], F32, offset=_co(1024))
        self.bdp16 = at("bdp16", [NH, NBLK], F32, offset=_co(32))
        self.idx_c = at("idx_c", [P, 512], F32, offset=_co(2048))
        self.eexp = at("eexp", [8, 512], F32, offset=_co(2048))
        self.br_b = at("br_b", [P, 8], F32, offset=_co(32))
        self.rmsw = at("rmsw", [P, NCH], F32, offset=_co(64))
        self.s_bc = at("s_bc", [P, NH], F32, offset=_co(64))
        self.bdp_b = at("bdp_b", [P, NH * NBLK], F32, offset=_co(256))
        self.kd_colsT = at("kd_colsT", [P, 32], F32, offset=_co(128))
        self.flag_c = at("flag_c", [P, 1], F32, offset=_co(32))
        assert _CO[0] <= 229000, _CO[0]

        ec = self.es.enter_context
        self.pA = ec(nc.psum_tensor("pA", [P, 512], F32))
        self.pB = ec(nc.psum_tensor("pB", [P, 512], F32))
        self.pQK = ec(nc.psum_tensor("pQK", [P, 512], F32))
        self.pU = ec(nc.psum_tensor("pU", [P, 512], F32))
        self.pO = ec(nc.psum_tensor("pO", [P, 512], F32))
        self.pT = ec(nc.psum_tensor("pT", [P, 1024], BF16))
        self.pV0 = ec(nc.psum_tensor("pV0", [1, 1024], F32))

        self.sem_g = ec(nc.semaphore("sem_g"))
        self.sem_y = ec(nc.semaphore("sem_y"))
        self.sem_cc = ec(nc.semaphore("sem_cc"))
        self.sem_w = [ec(nc.semaphore("sem_w0")), ec(nc.semaphore("sem_w1"))]
        self.exp_w = [0, 0]
        self.sem_t = ec(nc.semaphore("sem_t"))
        self.sem_v = ec(nc.semaphore("sem_v"))
        self.sem_s = ec(nc.semaphore("sem_s"))
        self.exp_g = 0
        self.exp_y = 0
        self.exp_cc = 0
        self.c_t = 0
        self.c_v = 0
        self.c_s = 0

    def block(self, **bodies):
        with self.nc.Block() as b:
            for eng in ("sync", "gpsimd", "tensor", "vector", "scalar"):
                if eng in bodies:
                    getattr(b, eng)(bodies[eng])

    def gdma(self, g, out, in_):
        g.dma_start(out=out, in_=in_).then_inc(self.sem_g, 16)
        self.exp_g += 16

    def wdma(self, g, slot, out, in_):
        g.dma_start(out=out, in_=in_).then_inc(self.sem_w[slot], 16)
        self.exp_w[slot] += 16
        return self.exp_w[slot]

    def xall16(self):
        return self.xall[:, 0:NCH * TOKS].rearrange("p (c t) -> p c t", c=NCH)

    def ydma(self, y, out, in_):
        y.dma_start(out=out, in_=in_).then_inc(self.sem_y, 16)
        self.exp_y += 16

    def tap(self, name, tensor, rows, cols):
        if name not in self.taps:
            return
        t = self.nc.dram_tensor(f"tap_{name}", [rows, cols], tensor.dtype,
                                kind="ExternalOutput")

        def _s(y):
            self.ydma(y, t[:, :], tensor[0:rows, 0:cols])
            y.wait_ge(self.sem_y, self.exp_y)
        self.block(sync=_s)

    def mark_t(self, ins):
        ins.then_inc(self.sem_t, 1)
        self.c_t += 1
        return self.c_t

    def mark_v(self, ins):
        ins.then_inc(self.sem_v, 1)
        self.c_v += 1
        return self.c_v

    def mark_s(self, ins):
        ins.then_inc(self.sem_s, 1)
        self.c_s += 1
        return self.c_s

    def xs(self, c):
        return self.xslot[self.slotmap[c]]

    def rotate(self, c):
        old = self.slotmap[c]
        self.slotmap[c] = self.spare
        self.spare = old

    def emit_setup(self):
        b = self

        def g_pre(g):
            g.memset(b.idBF[:, :], 1.0)
            g.memset(b.idF32[:, :], 1.0)
        self.block(gpsimd=g_pre)

        def g_body(g):
            g.affine_select(b.idBF[:, :], b.idBF[:, :], [[-1, P]], ALU.is_equal, 0.0,
                            base=0, channel_multiplier=1)
            g.affine_select(b.idF32[:, :], b.idF32[:, :], [[-1, P]], ALU.is_equal, 0.0,
                            base=0, channel_multiplier=1)
            g.memset(b.ones_row_bf[:, :], 1.0)
            g.memset(b.ones_row_f[:, :], 1.0)
            g.memset(b.onesP_bf[:, :], 1.0)
            g.memset(b.eps_ln[:, :], 1e-5)
            g.memset(b.eps_rms[:, :], 1e-6)
            b.gdma(g, b.s_col[:, :], b.s_col_in[:, :])
            b.gdma(g, b.gq[:, :], b.gq_in[:, :])
            b.gdma(g, b.gk[:, :], b.gk_in[:, :])
            b.gdma(g, b.gbd[:, :], b.gbd_in[:, :])
            b.gdma(g, b.idx_c[:, :], b.idx_in[:, :])
            b.gdma(g, b.eexp[:, :], b.eexp_in[:, :])
            b.gdma(g, b.rmsw[:, :], b.rmsw_in[:, :])
            b.gdma(g, b.flag_c[:, :], b.flag_in[:, :])
            b.gdma(g, b.br_b[:, :], bass.AP(b.br_in, 0, [[0, P], [1, 8]]))
            b.gdma(g, b.s_bc[:, :], bass.AP(b.s_col_in, 0, [[0, P], [1, NH]]))
            b.gdma(g, b.wqkv_bf.ap(), b.wqkv.ap())
            b.gdma(g, b.wg_bf.ap(), b.wgate.ap())
            b.gdma(g, b.wo_bf.ap(), b.woutw.ap())
            g.wait_ge(b.sem_g, b.exp_g)

        def y_body(y):
            b.ydma(y, b.xall16(),
                   b.x_in.ap().rearrange("(c p) t -> p c t", p=P))
            y.wait_ge(b.sem_y, b.exp_y)

        self.block(gpsimd=g_body, sync=y_body)
        self.cast_cnt = self.exp_g

        def v_body(v):
            v.tensor_scalar(b.qd16[:, :], b.gq[:, :], b.s_col[:, 0:1], None, ALU.mult)
            v.tensor_scalar(b.kd16[:, :], b.gk[:, :], b.s_col[:, 0:1], None, ALU.mult)
            v.tensor_scalar(b.bdp16[:, :], b.gbd[:, :], b.s_col[:, 0:1], None, ALU.mult)
        self.block(vector=v_body)

        def s_body(s):
            s.activation(b.qd16[:, :], b.qd16[:, :], AF.Exp)
            s.activation(b.kd16[:, :], b.kd16[:, :], AF.Exp)
            s.activation(b.bdp16[:, :], b.bdp16[:, :], AF.Exp)
        self.block(scalar=s_body)

        def g2(g):
            b.gdma(g, b.qd_dram[:, :], b.qd16[:, :])
            b.gdma(g, b.bdp_dram[:, :], b.bdp16[:, :])
            g.wait_ge(b.sem_g, b.exp_g)
            b.gdma(g, b.bdp_b[:, :],
                   bass.AP(b.bdp_dram, 0, [[0, P], [NBLK, NH], [1, NBLK]]))
            g.wait_ge(b.sem_g, b.exp_g)
        self.block(gpsimd=g2)

        def t_body(t):
            t.transpose(b.pQK[:, 0:16], b.kd16[:, 0:128], b.idF32[0:16, 0:16])
            t.transpose(b.pQK[:, 16:32], b.kd16[:, 128:256], b.idF32[0:16, 0:16])
        self.block(tensor=t_body)

        def v2(v):
            v.tensor_copy(b.kd_colsT[:, 0:32], b.pQK[:, 0:32])
        self.block(vector=v2)

    def load_ffn_weights(self):
        b = self

        def g_body(g):
            b.gdma(g, b.a_bf.ap().rearrange("p (c r) -> p c r", c=NCH),
                   b.a_t.ap().rearrange("(c p) r -> p c r", p=P))
            b.gdma(g, b.b_bf.ap().rearrange("p (c o) -> p c o", c=4),
                   b.b_t.ap().rearrange("(c p) o -> p c o", p=P))
            b.gdma(g, b.wr_bf.ap().rearrange("p (c e) -> p c e", c=NCH),
                   b.wr_t.ap().rearrange("(c p) e -> p c e", p=P))
            g.wait_ge(b.sem_g, b.exp_g)
        self.block(gpsimd=g_body)

    def emit_ln(self, tag):
        b = self
        nb = 8
        t0, s0, v0 = self.c_t, self.c_s, self.c_v
        xsrc = [b.xs(c) for c in range(NCH)]

        def s_body(s):
            for k in range(nb):
                off = (k % 2) * 2 * TOKS
                if k >= 2:
                    s.wait_ge(b.sem_t, t0 + k - 1)
                for i, c in enumerate([2 * k, 2 * k + 1]):
                    ins = s.activation(b.sq[:, off + i * TOKS:off + (i + 1) * TOKS],
                                       xsrc[c][:, :], AF.Square)
                b.c_s += 1
                ins.then_inc(b.sem_s, 1)

        def v_body(v):
            for k in range(nb):
                off = (k % 2) * 2 * TOKS
                if k >= 2:
                    v.wait_ge(b.sem_t, t0 + k - 1)
                for i, c in enumerate([2 * k, 2 * k + 1]):
                    ins = v.tensor_copy(b.xb[:, off + i * TOKS:off + (i + 1) * TOKS],
                                        xsrc[c][:, :])
                b.c_v += 1
                ins.then_inc(b.sem_v, 1)

        def t_body(t):
            for k in range(nb):
                off = (k % 2) * 2 * TOKS
                t.wait_ge(b.sem_s, s0 + k + 1)
                t.wait_ge(b.sem_v, v0 + k + 1)
                for i, c in enumerate([2 * k, 2 * k + 1]):
                    st = (c == 0)
                    sp = (c == NCH - 1)
                    xbv = b.xb[:, off + i * TOKS:off + (i + 1) * TOKS]
                    sqv = b.sq[:, off + i * TOKS:off + (i + 1) * TOKS]
                    t.matmul(b.pV0[0:1, 0:512], b.onesP_bf[:, :], xbv[:, 0:512],
                             start=st, stop=sp, skip_group_check=True)
                    t.matmul(b.pV0[0:1, 512:1024], b.onesP_bf[:, :], xbv[:, 512:1024],
                             start=st, stop=sp, skip_group_check=True)
                    t.matmul(b.pU[0:1, 0:512], b.onesP_bf[:, :], sqv[:, 0:512],
                             start=st, stop=sp, skip_group_check=True)
                    ins = t.matmul(b.pQK[0:1, 0:512], b.onesP_bf[:, :], sqv[:, 512:1024],
                                   start=st, stop=sp, skip_group_check=True)
                b.c_t += 1
                ins.then_inc(b.sem_t, 1)

        self.block(scalar=s_body, vector=v_body, tensor=t_body)

        def s_nm(s):
            s.activation(b.nm_r[0:1, :], b.pV0[0:1, :], AF.Copy, scale=-1.0 / 2048.0)
        self.block(scalar=s_nm)

        def s_m2(s):
            s.activation(b.m2_r[0:1, :], b.nm_r[0:1, :], AF.Square)
        self.block(scalar=s_m2)

        def v_var(v):
            v.scalar_tensor_tensor(b.rstd_r[0:1, 0:512], b.pU[0:1, 0:512],
                                   1.0 / 2048.0, b.m2_r[0:1, 0:512],
                                   ALU.mult, ALU.subtract)
            v.scalar_tensor_tensor(b.rstd_r[0:1, 512:1024], b.pQK[0:1, 0:512],
                                   1.0 / 2048.0, b.m2_r[0:1, 512:1024],
                                   ALU.mult, ALU.subtract)
        self.block(vector=v_var)

        def s_rstd(s):
            s.activation(b.rstd_r[0:1, :], b.rstd_r[0:1, :], AF.Ln,
                         bias=b.eps_ln[0:1, 0:1])
            s.activation(b.rstd_r[0:1, :], b.rstd_r[0:1, :], AF.Exp, scale=-0.5)
        self.block(scalar=s_rstd)

        def v_nmr(v):
            v.tensor_tensor(b.nmr_r[0:1, :], b.nm_r[0:1, :], b.rstd_r[0:1, :],
                            ALU.mult)
        self.block(vector=v_nmr)

        def t_bcast(t):
            t.matmul(b.pA[:, :], b.ones_row_f[:, :], b.rstd_r[0:1, 0:512],
                     start=True, stop=True)
            t.matmul(b.pB[:, :], b.ones_row_f[:, :], b.rstd_r[0:1, 512:1024],
                     start=True, stop=True)
            t.matmul(b.pQK[:, :], b.ones_row_f[:, :], b.nmr_r[0:1, 0:512],
                     start=True, stop=True)
            t.matmul(b.pO[:, :], b.ones_row_f[:, :], b.nmr_r[0:1, 512:1024],
                     start=True, stop=True)
        self.block(tensor=t_bcast)

        def v_apply(v):
            for c in range(NCH):
                x = b.xs(c)
                xnv = b.xn[:, c * TOKS:(c + 1) * TOKS]
                v.tensor_tensor(b.tmp[:, 0:512], x[:, 0:512], b.pA[:, :], ALU.mult)
                v.tensor_tensor(b.tmp[:, 512:1024], x[:, 512:1024], b.pB[:, :], ALU.mult)
                v.tensor_tensor(xnv[:, 0:512], b.tmp[:, 0:512], b.pQK[:, :], ALU.add)
                v.tensor_tensor(xnv[:, 512:1024], b.tmp[:, 512:1024], b.pO[:, :], ALU.add)
        self.block(vector=v_apply)
        self.tap(f"xn_{tag}", b.xn, P, NCH * TOKS)

    def emit_ffn(self, tag):
        b = self
        banks4 = [b.pA, b.pB, b.pQK, b.pO]

        def t_lg(t):
            for tt in range(8):
                for c in range(NCH):
                    t.matmul(b.pA[:, tt * 16:tt * 16 + 8],
                             b.xn[:, c * TOKS + tt * P:c * TOKS + (tt + 1) * P],
                             b.wr_bf[:, c * 8:(c + 1) * 8],
                             start=(c == 0), stop=(c == NCH - 1), skip_group_check=True)
        self.block(tensor=t_lg)

        def v_sm1(v):
            for tt in range(8):
                v.tensor_tensor(b.lgs[:, tt * 8:(tt + 1) * 8],
                                b.pA[:, tt * 16:tt * 16 + 8], b.br_b[:, :], ALU.add)
            for tt in range(8):
                v.reduce_max(b.mxb[:, tt:tt + 1], b.lgs[:, tt * 8:(tt + 1) * 8], AX)
            for tt in range(8):
                v.tensor_scalar(b.exb[:, tt * 8:(tt + 1) * 8], b.lgs[:, tt * 8:(tt + 1) * 8],
                                b.mxb[:, tt:tt + 1], None, ALU.subtract)
        self.block(vector=v_sm1)

        def s_exp(s):
            s.activation(b.exb[:, :], b.exb[:, :], AF.Exp)
        self.block(scalar=s_exp)

        def v_sm2a(v):
            for tt in range(8):
                v.reduce_sum(b.seb[:, tt:tt + 1], b.exb[:, tt * 8:(tt + 1) * 8], AX)
        self.block(vector=v_sm2a)

        def s_recip(s):
            s.activation(b.wb[:, :], b.seb[:, :], AF.Ln)
            s.activation(b.wb[:, :], b.wb[:, :], AF.Exp, scale=-1.0)
        self.block(scalar=s_recip)

        def v_sm2(v):
            for tt in range(8):
                v.tensor_scalar(b.ohb[:, tt * 8:(tt + 1) * 8], b.lgs[:, tt * 8:(tt + 1) * 8],
                                b.mxb[:, tt:tt + 1], None, ALU.is_equal)
            for tt in range(8):
                v.tensor_scalar(b.mw_tm[:, tt * 8:(tt + 1) * 8],
                                b.ohb[:, tt * 8:(tt + 1) * 8],
                                b.wb[:, tt:tt + 1], FFN_SCALE, ALU.mult, ALU.mult)
        self.block(vector=v_sm2)
        self.tap(f"seb_{tag}", b.seb, P, 8)
        self.tap(f"lgs_{tag}", b.lgs, P, 64)
        self.tap(f"exb_{tag}", b.exb, P, 64)
        self.tap(f"ohb_{tag}", b.ohb, P, 64)
        self.tap(f"mxb_{tag}", b.mxb, P, 8)
        self.tap(f"wb_{tag}", b.wb, P, 8)
        self.tap(f"mwtm_{tag}", b.mw_tm, P, 64)

        def t_tr(t):
            for tt in range(8):
                dst = b.pO if tt < 4 else b.pU
                t.transpose(dst[0:8, (tt % 4) * P:(tt % 4 + 1) * P],
                            b.mw_tm[:, tt * 8:(tt + 1) * 8], b.idF32[:, :])
        self.block(tensor=t_tr)

        def v_mw(v):
            v.tensor_copy(b.mw_fm[0:8, 0:512], b.pO[0:8, :])
            v.tensor_copy(b.mw_fm[0:8, 512:1024], b.pU[0:8, :])
        self.block(vector=v_mw)

        for wave in range(2):
            def t_ex(t, wave=wave):
                for i in range(2):
                    rc = wave * 2 + i
                    for tt in range(2):
                        bank = banks4[i * 2 + tt]
                        t.matmul(bank[:, :], b.eexp[:, rc * P:(rc + 1) * P],
                                 b.mw_fm[0:8, tt * 512:(tt + 1) * 512],
                                 start=True, stop=True)
            self.block(tensor=t_ex)

            def v_ex(v, wave=wave):
                for i in range(2):
                    rc = wave * 2 + i
                    for tt in range(2):
                        bank = banks4[i * 2 + tt]
                        v.tensor_copy(
                            b.mask_b[:, rc * TOKS + tt * 512:rc * TOKS + (tt + 1) * 512],
                            bank[:, :])
            self.block(vector=v_ex)

        htiles = [(rc, tt) for rc in range(4) for tt in range(2)]
        bsets = [(b.pA, b.pB), (b.pQK, b.pO)]
        t0, v0 = self.c_t, self.c_v

        def t_h(t):
            for w in range(0, len(htiles), 2):
                if w >= 4:
                    t.wait_ge(b.sem_v, v0 + w // 2 - 1)
                for i, (rc, tt) in enumerate(htiles[w:w + 2]):
                    bank = bsets[(w // 2) % 2][i]
                    for c in range(NCH):
                        ins = t.matmul(bank[:, :],
                                       b.a_bf[:, c * 512 + rc * P:c * 512 + (rc + 1) * P],
                                       b.xn[:, c * TOKS + tt * 512:c * TOKS + (tt + 1) * 512],
                                       start=(c == 0), stop=(c == NCH - 1))
                b.c_t += 1
                ins.then_inc(b.sem_t, 1)

        def v_h(v):
            for w in range(0, len(htiles), 2):
                v.wait_ge(b.sem_t, t0 + w // 2 + 1)
                for i, (rc, tt) in enumerate(htiles[w:w + 2]):
                    bank = bsets[(w // 2) % 2][i]
                    sl = slice(rc * TOKS + tt * 512, rc * TOKS + (tt + 1) * 512)
                    ins = v.tensor_tensor(b.hm[:, sl], bank[:, :], b.mask_b[:, sl], ALU.mult)
                b.c_v += 1
                ins.then_inc(b.sem_v, 1)

        self.block(tensor=t_h, vector=v_h)
        self.tap(f"hm_{tag}", b.hm, P, 4 * TOKS)

        store = (tag == "ffn2")
        t0, v0 = self.c_t, self.c_v
        rotplan = []
        sp = b.spare
        sm = list(b.slotmap)
        for oc in range(NCH):
            rotplan.append((sm[oc], sp))
            sm[oc], sp = sp, sm[oc]

        def t_y(t):
            for oc in range(NCH):
                if oc >= 2:
                    t.wait_ge(b.sem_v, v0 + oc - 1)
                for tt in range(2):
                    bank = bsets[oc % 2][tt]
                    for rc in range(4):
                        ins = t.matmul(bank[:, :],
                                       b.b_bf[:, rc * 2048 + oc * P:rc * 2048 + (oc + 1) * P],
                                       b.hm[:, rc * TOKS + tt * 512:rc * TOKS + (tt + 1) * 512],
                                       start=(rc == 0), stop=(rc == 3))
                b.c_t += 1
                ins.then_inc(b.sem_t, 1)

        def v_y(v):
            for oc in range(NCH):
                v.wait_ge(b.sem_t, t0 + oc + 1)
                olds, news = rotplan[oc]
                for tt in range(2):
                    bank = bsets[oc % 2][tt]
                    ins = v.tensor_tensor(b.xslot[news][:, tt * 512:(tt + 1) * 512],
                                          bank[:, :],
                                          b.xslot[olds][:, tt * 512:(tt + 1) * 512], ALU.add)
                b.c_v += 1
                ins.then_inc(b.sem_v, 1)

        bodies = dict(tensor=t_y, vector=v_y)
        if store:
            def y_st(y):
                for oc in range(NCH):
                    y.wait_ge(b.sem_v, v0 + oc + 1)
                    self.ydma(y, b.out_t[oc * P:(oc + 1) * P, :],
                              b.xslot[rotplan[oc][1]][:, :])
                y.wait_ge(b.sem_y, b.exp_y)
            bodies["sync"] = y_st
        self.block(**bodies)
        for oc in range(NCH):
            b.rotate(oc)

    def emit_attn(self):
        b = self
        self.emit_ln("attn")

        spill_map = list(self.slotmap)

        def y_spill(y):
            self.ydma(y, b.xspill.ap(), b.xall[:, :])
        self.block(sync=y_spill)
        spill_cnt = b.exp_y

        self.emit_proj(0, 0, extra_wait=spill_cnt)
        self.emit_amath(0, 0)
        self.emit_cc_issue(0)
        for g in range(NG):
            gb = g % 2
            if g + 1 < NG:
                self.emit_proj(g + 1, (g + 1) % 2)
            self.emit_cc_fixup(g, gb)
            self.emit_bmath(g, gb)
            if g + 1 < NG:
                self.emit_amath(g + 1, (g + 1) % 2)
                self.emit_cc_issue(g + 1)

        self.tap("o", b.o_bf, P, NH * TOKS)

        def s_rv(s):
            s.activation(b.rstd_r[0:1, :], b.pV0[0:1, :], AF.Ln,
                         bias=b.eps_rms[0:1, 0:1], scale=1.0 / 2048.0)
        self.block(scalar=s_rv)

        def s_rv2(s):
            s.activation(b.rstd_r[0:1, :], b.rstd_r[0:1, :], AF.Exp, scale=-0.5)
        self.block(scalar=s_rv2)

        def t_rvb(t):
            t.matmul(b.pQK[:, :], b.ones_row_f[:, :], b.rstd_r[0:1, 0:512],
                     start=True, stop=True)
            t.matmul(b.pU[:, :], b.ones_row_f[:, :], b.rstd_r[0:1, 512:1024],
                     start=True, stop=True)
        self.block(tensor=t_rvb)

        def v_rvb(v):
            v.tensor_copy(b.rstd_sb[:, 0:512], b.pQK[:, :])
            v.tensor_copy(b.rstd_sb[:, 512:1024], b.pU[:, :])
        self.block(vector=v_rvb)

        def y_rest(y):
            self.ydma(y, b.xall[:, :], b.xspill.ap())
        self.block(sync=y_rest)
        restore_cnt = b.exp_y
        self.slotmap = spill_map
        self.spare = [i for i in range(NSLOT) if i not in spill_map][0]

        self.emit_gate()
        self.tap("os", b.o_bf, P, NH * TOKS)

        self.emit_wout(restore_cnt)
        self.tap("x2", self.xs(0), P, TOKS)

    def emit_proj(self, g, gb, extra_wait=0):
        b = self
        kg, vg, qg = b.k_g[gb], b.v_g[gb], b.q_g[gb]
        pieces = []
        for which, colbase in (("k", 2048 + g * 512), ("v", 4096 + g * 512),
                               ("q", 0 + g * 512)):
            for half in range(2):
                pieces.append((which, colbase + half * 256, half))

        waves = []
        for p, (which, cb, half) in enumerate(pieces):
            ws = b.wstr[p % 2]
            if which in ("k", "q"):
                dst = kg if which == "k" else qg
                for hi2 in range(2):
                    hi = half * 2 + hi2

                    def mm(t, banks, ws=ws, hi2=hi2):
                        for tt in range(2):
                            for c in range(NCH):
                                ins = t.matmul(
                                    banks[tt][:, :],
                                    ws[:, c * 256 + hi2 * P:c * 256 + (hi2 + 1) * P],
                                    b.xn[:, c * TOKS + tt * 512:c * TOKS + (tt + 1) * 512],
                                    start=(c == 0), stop=(c == NCH - 1))
                        return ins

                    def ev(s, banks, dst=dst, hi=hi):
                        for tt in range(2):
                            ins = s.activation(
                                dst[:, hi * TOKS + tt * 512:hi * TOKS + (tt + 1) * 512],
                                banks[tt][:, :], AF.Silu)
                        return ins
                    waves.append((p, mm, ev))
            else:
                for wv in range(4):
                    def mm(t, banks, ws=ws, wv=wv):
                        for i in range(2):
                            tt = wv * 2 + i
                            for c in range(NCH):
                                ins = t.matmul(
                                    banks[i][:, 0:256],
                                    b.xn[:, c * TOKS + tt * P:c * TOKS + (tt + 1) * P],
                                    ws[:, c * 256:(c + 1) * 256],
                                    start=(c == 0), stop=(c == NCH - 1))
                        return ins

                    def ev(s, banks, vg=vg, wv=wv, half=half):
                        for i in range(2):
                            tt = wv * 2 + i
                            ins = s.activation(
                                vg[:, tt * 512 + half * 256:tt * 512 + (half + 1) * 256],
                                banks[i][:, 0:256], AF.Silu)
                        return ins
                    waves.append((p, mm, ev))

        nw = len(waves)
        bsets = [(b.pA, b.pB), (b.pQK, b.pO)]
        t0, s0 = self.c_t, self.c_s
        piece_first = {}
        piece_last = {}
        for w, (p, _, _) in enumerate(waves):
            piece_first.setdefault(p, w)
            piece_last[p] = w
        piece_cnt = [self.exp_w[p % 2] + 16 * (p // 2 + 1) for p in range(len(pieces))]

        def g_body(gq):
            if extra_wait:
                gq.wait_ge(b.sem_y, extra_wait)
            for p, (which, cb, half) in enumerate(pieces):
                if p >= 2:
                    gq.wait_ge(b.sem_t, t0 + piece_last[p - 1] + 1)
                srcap = bass.AP(b.wqkv_bf, cb, [[6144, P], [6144 * P, NCH], [1, 256]])
                b.wdma(gq, p % 2, b.wstr[p % 2].ap().rearrange("p (c n) -> p c n", c=NCH), srcap)

        def t_body(t):
            for w, (p, mm, _) in enumerate(waves):
                if w == piece_first[p]:
                    t.wait_ge(b.sem_w[p % 2], piece_cnt[p])
                if w >= 2:
                    t.wait_ge(b.sem_s, s0 + w - 1)
                ins = mm(t, bsets[w % 2])
                b.c_t += 1
                ins.then_inc(b.sem_t, 1)

        def s_body(s):
            for w, (p, _, ev) in enumerate(waves):
                s.wait_ge(b.sem_t, t0 + w + 1)
                ins = ev(s, bsets[w % 2])
                b.c_s += 1
                ins.then_inc(b.sem_s, 1)

        self.block(sync=g_body, tensor=t_body, scalar=s_body)
        if g == 0:
            self.tap("k_g0", kg, P, HG * TOKS)
            self.tap("v_g0", vg, P, HG * TOKS)
            self.tap("q_g0", qg, P, HG * TOKS)

    def emit_amath(self, g, gb):
        b = self
        kg, vg = b.k_g[gb], b.v_g[gb]
        Sg = b.S_g[gb]
        ubank = [b.pU, b.pA]
        TTR = [0] * HG
        TU = [0] * HG
        VKS = [0] * HG
        VSC = [0] * HG
        ct = self.c_t
        for hi in range(HG):
            ct += 1; TTR[hi] = ct
            if hi >= 1:
                ct += 1; TU[hi - 1] = ct
        ct += 1; TU[HG - 1] = ct
        cv = self.c_v
        for hi in range(HG):
            cv += 1; VKS[hi] = cv
            if hi >= 1:
                cv += 1; VSC[hi - 1] = cv
        cv += 1; VSC[HG - 1] = cv

        def emit_tr(t, hi):
            for tt in range(8):
                ins = t.transpose(b.pT[:, tt * P:(tt + 1) * P],
                                  kg[:, hi * TOKS + tt * P:hi * TOKS + (tt + 1) * P],
                                  b.idBF[:, :])
            return ins

        def emit_u(t, hi):
            for j in range(NBLK):
                for mc in range(2):
                    tt = 2 * j + mc
                    ins = t.matmul(ubank[hi % 2][:, j * DH:(j + 1) * DH],
                                   b.ks_h[hi % 2][:, tt * P:(tt + 1) * P],
                                   vg[:, tt * 512 + hi * DH:tt * 512 + (hi + 1) * DH],
                                   start=(mc == 0), stop=(mc == 1))
            return ins

        def t_body(t):
            for hi in range(HG):
                if hi >= 1:
                    t.wait_ge(b.sem_v, VKS[hi - 1])
                ins = emit_tr(t, hi)
                ins.then_inc(b.sem_t, 1)
                if hi >= 1:
                    ins = emit_u(t, hi - 1)
                    ins.then_inc(b.sem_t, 1)
            t.wait_ge(b.sem_v, VKS[HG - 1])
            ins = emit_u(t, HG - 1)
            ins.then_inc(b.sem_t, 1)
            b.c_t = ct

        def emit_ks(v, hi):
            hh = g * HG + hi
            for tt in range(8):
                col = (tt % 2) * 16 + hh
                ins = v.tensor_scalar(b.ks_h[hi % 2][:, tt * P:(tt + 1) * P],
                                      b.pT[:, tt * P:(tt + 1) * P],
                                      b.kd_colsT[:, col:col + 1], None, ALU.mult)
            return ins

        def emit_scan(v, hi):
            hh = g * HG + hi
            base = hi * 4 * DH
            v.tensor_copy(Sg[:, base:base + DH], ubank[hi % 2][:, 0:DH])
            for j in range(1, NBLK):
                ins = v.scalar_tensor_tensor(Sg[:, base + j * DH:base + (j + 1) * DH],
                                             Sg[:, base + (j - 1) * DH:base + j * DH],
                                             b.bdp_b[:, hh * NBLK + 1:hh * NBLK + 2],
                                             ubank[hi % 2][:, j * DH:(j + 1) * DH],
                                             ALU.mult, ALU.add)
            return ins

        def v_body(v):
            for hi in range(HG):
                v.wait_ge(b.sem_t, TTR[hi])
                ins = emit_ks(v, hi)
                ins.then_inc(b.sem_v, 1)
                if hi >= 1:
                    v.wait_ge(b.sem_t, TU[hi - 1])
                    ins = emit_scan(v, hi - 1)
                    ins.then_inc(b.sem_v, 1)
            v.wait_ge(b.sem_t, TU[HG - 1])
            ins = emit_scan(v, HG - 1)
            ins.then_inc(b.sem_v, 1)
            b.c_v = cv

        def g_body(gq):
            for hi in range(HG):
                gq.wait_ge(b.sem_v, VSC[hi])
                base = hi * 4 * DH
                b.gdma(gq, b.cc_in[g][hi * P:(hi + 1) * P, :],
                       Sg[:, base + 3 * DH:base + 4 * DH])
            gq.wait_ge(b.sem_g, b.exp_g)

        self.block(tensor=t_body, vector=v_body, gpsimd=g_body)
        if g == 0:
            self.tap("S_g0", Sg, P, HG * 4 * DH)

    def emit_cc_issue(self, g):
        b = self

        def g_cc(gq):
            gq.collective_compute(
                "AllGather", ALU.bypass,
                replica_groups=[[0, 1], [2, 3], [4, 5], [6, 7]],
                ins=[b.cc_in[g].ap()], outs=[b.cc_out[g].ap()],
            ).then_inc(b.sem_cc)
            b.exp_cc += 1
        self.block(gpsimd=g_cc)

    def emit_cc_fixup(self, g, gb):
        b = self
        Sg, Tg, kvi = b.S_g[gb], b.T_g[gb], b.kvi_g[gb]

        def g_cc(gq):
            gq.wait_ge(b.sem_cc, g + 1)
            b.gdma(gq, b.gtmp.ap().rearrange("p (h e) -> p h e", h=HG),
                   bass.AP(b.cc_out[g], 0, [[DH, P], [P * DH, HG], [1, DH]]))
            gq.wait_ge(b.sem_g, b.exp_g)
        self.block(gpsimd=g_cc)

        def v_kvi(v):
            v.tensor_scalar(kvi[:, :], b.gtmp[:, :], b.flag_c[:, 0:1], None, ALU.mult)
        self.block(vector=v_kvi)

        def v_fix(v):
            for hi in range(HG):
                hh = g * HG + hi
                base = hi * 4 * DH
                v.tensor_copy(Tg[:, base:base + DH], kvi[:, hi * DH:(hi + 1) * DH])
                for j in range(1, NBLK):
                    v.scalar_tensor_tensor(Tg[:, base + j * DH:base + (j + 1) * DH],
                                           kvi[:, hi * DH:(hi + 1) * DH],
                                           b.bdp_b[:, hh * NBLK + j:hh * NBLK + j + 1],
                                           Sg[:, base + (j - 1) * DH:base + j * DH],
                                           ALU.mult, ALU.add)
        self.block(vector=v_fix)
        if g == 0:
            self.tap("T_g0", Tg, P, HG * 4 * DH)

    def emit_bmath(self, g, gb):
        b = self
        kg, vg, qg = b.k_g[gb], b.v_g[gb], b.q_g[gb]
        Tg = b.T_g[gb]

        def g_qd(gq):
            b.gdma(gq, b.qdB_g.ap().rearrange("p (h l) -> p h l", h=HG),
                   bass.AP(b.qd_dram, g * HG * BLK, [[0, P], [BLK, HG], [1, BLK]]))
            gq.wait_ge(b.sem_g, b.exp_g)
        self.block(gpsimd=g_qd)

        def v_d(v):
            for hi in range(HG):
                hh = g * HG + hi
                v.tensor_scalar(b.D_g[:, hi * 512:(hi + 1) * 512], b.idx_c[:, :],
                                b.s_bc[:, hh:hh + 1], None, ALU.mult)
        self.block(vector=v_d)

        def s_d(s):
            for hi in range(HG):
                s.activation(b.D_g[:, hi * 512:(hi + 1) * 512],
                             b.D_g[:, hi * 512:(hi + 1) * 512], AF.Exp)
        self.block(scalar=s_d)
        if g == 0:
            self.tap("D_g0", b.D_g, P, HG * 512)
            self.tap("qdB_g0", b.qdB_g, P, HG * BLK)

        NI = HG * NBLK
        qkbank = [b.pQK, b.pU]
        obank = [b.pA, b.pO]
        TQK = [0] * NI
        TO = [0] * NI
        TV = [0] * NI
        VOE = [0] * NI
        VQK = [0] * NI
        SQ = [0] * NI
        ct = self.c_t
        for i in range(NI):
            ct += 1; TQK[i] = ct
            if i >= 1:
                ct += 1; TO[i - 1] = ct
            if i >= 4:
                ct += 1; TV[i - 4] = ct
        ct += 1; TO[NI - 1] = ct
        for j in range(NI - 4, NI):
            ct += 1; TV[j] = ct
        cv = self.c_v
        for i in range(NI):
            if i >= 1:
                cv += 1; VOE[i - 1] = cv
            cv += 1; VQK[i] = cv
        cv += 1; VOE[NI - 1] = cv
        cs = self.c_s
        for j in range(NI):
            cs += 1; SQ[j] = cs

        def mm_qk(t, i):
            hi, blk = i // NBLK, i % NBLK
            for mc in range(2):
                tt = 2 * blk + mc
                ins = t.matmul(qkbank[i % 2][:, mc * 256:(mc + 1) * 256],
                               kg[:, hi * TOKS + tt * P:hi * TOKS + (tt + 1) * P],
                               qg[:, hi * TOKS + blk * BLK:hi * TOKS + (blk + 1) * BLK],
                               start=True, stop=True)
            return ins

        def mm_o(t, j):
            hi, blk = j // NBLK, j % NBLK
            for mc in range(2):
                tt = 2 * blk + mc
                t.matmul(obank[j % 2][:, 0:256],
                         vg[:, tt * 512 + hi * DH:tt * 512 + (hi + 1) * DH],
                         b.qkD[j % 2][:, mc * 256:(mc + 1) * 256],
                         start=(mc == 0), stop=False)
            base = hi * 4 * DH
            return t.matmul(obank[j % 2][:, 0:256],
                            Tg[:, base + blk * DH:base + (blk + 1) * DH],
                            b.qs_b[j % 2][:, :], start=False, stop=True)

        def mm_var(t, j):
            hi, blk = j // NBLK, j % NBLK
            hh = g * HG + hi
            return t.matmul(b.pV0[0:1, blk * BLK:(blk + 1) * BLK],
                            b.onesP_bf[:, :], b.osq[j % 2][:, :],
                            start=(hh == 0 and blk in (0, 2)),
                            stop=(hh == NH - 1 and blk in (1, 3)),
                            skip_group_check=True)

        def t_body(t):
            def do_o(j):
                t.wait_ge(b.sem_v, VQK[j])
                if j >= 2:
                    t.wait_ge(b.sem_v, VOE[j - 2])
                mm_o(t, j).then_inc(b.sem_t, 1)

            def do_var(j):
                t.wait_ge(b.sem_s, SQ[j])
                mm_var(t, j).then_inc(b.sem_t, 1)

            for i in range(NI):
                if i >= 2:
                    t.wait_ge(b.sem_v, VQK[i - 2])
                mm_qk(t, i).then_inc(b.sem_t, 1)
                if i >= 1:
                    do_o(i - 1)
                if i >= 4:
                    do_var(i - 4)
            do_o(NI - 1)
            for j in range(NI - 4, NI):
                do_var(j)
            b.c_t = ct

        def v_body(v):
            def do_oe(j):
                hi, blk = j // NBLK, j % NBLK
                hh = g * HG + hi
                v.wait_ge(b.sem_t, TO[j])
                ins = v.tensor_copy(
                    b.o_bf[:, hh * TOKS + blk * BLK:hh * TOKS + (blk + 1) * BLK],
                    obank[j % 2][:, 0:256])
                ins.then_inc(b.sem_v, 1)

            for i in range(NI):
                if i >= 1:
                    do_oe(i - 1)
                hi, blk = i // NBLK, i % NBLK
                v.wait_ge(b.sem_t, TQK[i])
                v.tensor_tensor(b.qkD[i % 2][:, :], qkbank[i % 2][:, :],
                                b.D_g[:, hi * 512:(hi + 1) * 512], ALU.mult)
                ins = v.tensor_tensor(
                    b.qs_b[i % 2][:, :],
                    qg[:, hi * TOKS + blk * BLK:hi * TOKS + (blk + 1) * BLK],
                    b.qdB_g[:, hi * BLK:(hi + 1) * BLK], ALU.mult)
                ins.then_inc(b.sem_v, 1)
            do_oe(NI - 1)
            b.c_v = cv

        def s_body(s):
            for j in range(NI):
                hi, blk = j // NBLK, j % NBLK
                hh = g * HG + hi
                s.wait_ge(b.sem_v, VOE[j])
                if j >= 2:
                    s.wait_ge(b.sem_t, TV[j - 2])
                ins = s.activation(
                    b.osq[j % 2][:, :],
                    b.o_bf[:, hh * TOKS + blk * BLK:hh * TOKS + (blk + 1) * BLK],
                    AF.Square)
                ins.then_inc(b.sem_s, 1)
            b.c_s = cs

        self.block(tensor=t_body, vector=v_body, scalar=s_body)

    def emit_gate(self):
        b = self
        bset = [(b.pA, b.pB), (b.pQK, b.pO)]
        t0, s0, v0 = self.c_t, self.c_s, self.c_v
        piece_cnt = [self.exp_w[p % 2] + 16 * (p // 2 + 1) for p in range(8)]

        def g_body(gq):
            for p in range(8):
                if p >= 2:
                    gq.wait_ge(b.sem_t, t0 + 2 * (p - 1))
                srcap = bass.AP(b.wg_bf, p * 256, [[2048, P], [2048 * P, NCH], [1, 256]])
                b.wdma(gq, p % 2, b.gst[p % 2].ap().rearrange("p (c n) -> p c n", c=NCH), srcap)

        def t_body(t):
            for oc in range(NCH):
                p, oc2 = oc // 2, oc % 2
                ws = b.gst[p % 2]
                if oc2 == 0:
                    t.wait_ge(b.sem_w[p % 2], piece_cnt[p])
                if oc >= 2:
                    t.wait_ge(b.sem_s, s0 + oc - 1)
                for tt in range(2):
                    bank = bset[oc % 2][tt]
                    for c in range(NCH):
                        ins = t.matmul(bank[:, :],
                                       ws[:, c * 256 + oc2 * P:c * 256 + (oc2 + 1) * P],
                                       b.xn[:, c * TOKS + tt * 512:c * TOKS + (tt + 1) * 512],
                                       start=(c == 0), stop=(c == NCH - 1))
                b.c_t += 1
                ins.then_inc(b.sem_t, 1)

        def s_body(s):
            for oc in range(NCH):
                s.wait_ge(b.sem_t, t0 + oc + 1)
                if oc >= 2:
                    s.wait_ge(b.sem_v, v0 + oc - 1)
                for tt in range(2):
                    ins = s.activation(b.sg_b[oc % 2][:, tt * 512:(tt + 1) * 512],
                                       bset[oc % 2][tt][:, :], AF.Sigmoid)
                b.c_s += 1
                ins.then_inc(b.sem_s, 1)

        def v_body(v):
            for oc in range(NCH):
                v.wait_ge(b.sem_s, s0 + oc + 1)
                for tt in range(2):
                    v.scalar_tensor_tensor(
                        b.t2_b[0][:, tt * 512:(tt + 1) * 512],
                        b.sg_b[oc % 2][:, tt * 512:(tt + 1) * 512], b.rmsw[:, oc:oc + 1],
                        b.o_bf[:, oc * TOKS + tt * 512:oc * TOKS + (tt + 1) * 512],
                        ALU.mult, ALU.mult)
                for tt in range(2):
                    ins = v.tensor_tensor(
                        b.o_bf[:, oc * TOKS + tt * 512:oc * TOKS + (tt + 1) * 512],
                        b.t2_b[0][:, tt * 512:(tt + 1) * 512],
                        b.rstd_sb[:, tt * 512:(tt + 1) * 512], ALU.mult)
                b.c_v += 1
                ins.then_inc(b.sem_v, 1)

        self.block(sync=g_body, tensor=t_body, scalar=s_body, vector=v_body)

    def emit_wout(self, restore_cnt=0):
        b = self
        bset = [(b.pA, b.pB), (b.pQK, b.pO)]
        t0, v0 = self.c_t, self.c_v
        piece_cnt = [self.exp_w[p % 2] + 16 * (p // 2 + 1) for p in range(8)]

        def g_body(gq):
            for p in range(8):
                if p >= 2:
                    gq.wait_ge(b.sem_t, t0 + 2 * (p - 1))
                srcap = bass.AP(b.wo_bf, p * 256, [[2048, P], [2048 * P, NCH], [1, 256]])
                b.wdma(gq, p % 2, b.gst[p % 2].ap().rearrange("p (c n) -> p c n", c=NCH), srcap)

        def t_body(t):
            for oc in range(NCH):
                p, oc2 = oc // 2, oc % 2
                ws = b.gst[p % 2]
                if oc2 == 0:
                    t.wait_ge(b.sem_w[p % 2], piece_cnt[p])
                if oc >= 2:
                    t.wait_ge(b.sem_v, v0 + oc - 1)
                for tt in range(2):
                    bank = bset[oc % 2][tt]
                    for c in range(NCH):
                        ins = t.matmul(bank[:, :],
                                       ws[:, c * 256 + oc2 * P:c * 256 + (oc2 + 1) * P],
                                       b.o_bf[:, c * TOKS + tt * 512:c * TOKS + (tt + 1) * 512],
                                       start=(c == 0), stop=(c == NCH - 1))
                b.c_t += 1
                ins.then_inc(b.sem_t, 1)

        def v_body(v):
            for oc in range(NCH):
                v.wait_ge(b.sem_t, t0 + oc + 1)
                if oc == 0 and restore_cnt:
                    v.wait_ge(b.sem_y, restore_cnt)
                new = b.xslot[b.spare]
                old = b.xs(oc)
                for tt in range(2):
                    bank = bset[oc % 2][tt]
                    ins = v.tensor_tensor(new[:, tt * 512:(tt + 1) * 512], bank[:, :],
                                          old[:, tt * 512:(tt + 1) * 512], ALU.add)
                b.rotate(oc)
                b.c_v += 1
                ins.then_inc(b.sem_v, 1)

        self.block(sync=g_body, tensor=t_body, vector=v_body)

    def emit_store(self):
        b = self

        def y_out(y):
            for c in range(NCH):
                self.ydma(y, b.out_t[c * P:(c + 1) * P, :], b.xs(c)[:, :])
            y.wait_ge(b.sem_y, b.exp_y)
        self.block(sync=y_out)

    def build(self, upto="all"):
        self.emit_setup()
        self.load_ffn_weights()
        self.emit_ln("ffn1")
        if upto == "ln1":
            self.emit_store(); self.es.close(); return self.nc
        self.emit_ffn("ffn1")
        self.tap("x1", self.xs(0), P, TOKS)
        if upto == "ffn1":
            self.emit_store(); self.es.close(); return self.nc
        self.emit_attn()
        if upto == "attn":
            self.emit_store(); self.es.close(); return self.nc
        self.load_ffn_weights()
        self.emit_ln("ffn2")
        self.emit_ffn("ffn2")
        self.es.close()
        return self.nc



def _host_consts():
    m = np.arange(BLK)
    l = np.arange(BLK)
    idx = (m[:, None] - l[None, :]).astype(np.float32)
    idx[m[:, None] > l[None, :]] = NEG_BIG
    idx_c = np.concatenate([idx[0:128, :], idx[128:256, :]], axis=1).astype(np.float32)
    gq = np.broadcast_to(-(l + 1).astype(np.float32), (NH, BLK)).copy()
    gk = np.broadcast_to(-(255 - m).astype(np.float32), (NH, BLK)).copy()
    gbd = np.broadcast_to((-256.0 * np.arange(NBLK)).astype(np.float32), (NH, NBLK)).copy()
    eexp = np.zeros((8, 512), np.float32)
    for j in range(512):
        eexp[j // 64, j] = 1.0
    return idx_c, gq, gk, gbd, eexp


def build_inputs(inputs):
    x = np.asarray(inputs["L0_attn_hidden_states"], np.float32)
    wqkv = np.ascontiguousarray(np.asarray(inputs["L0_attn_w_qkv"], np.float32))
    wgate = np.ascontiguousarray(np.asarray(inputs["L0_attn_w_gate"], np.float32))
    woutw = np.ascontiguousarray(np.asarray(inputs["L0_attn_w_out"], np.float32))
    rmsw = np.asarray(inputs["L0_attn_rms_weight"], np.float32)
    slope = np.asarray(inputs["L0_attn_slope_rate"], np.float32).reshape(NH)
    A = np.asarray(inputs["L0_ffn_A"], np.float32)
    B = np.asarray(inputs["L0_ffn_B"], np.float32)
    Wr = np.asarray(inputs["L0_ffn_Wr"], np.float32)
    br = np.asarray(inputs["L0_ffn_br"], np.float32)

    a_t = np.ascontiguousarray(A.reshape(512, 2048).T)
    b_t = np.ascontiguousarray(B.transpose(0, 2, 1).reshape(512, 2048))
    wr_t = np.ascontiguousarray(Wr.T)
    rmsw_fm = np.ascontiguousarray(rmsw.reshape(NCH, P).T)
    idx_c, gq, gk, gbd, eexp = _host_consts()

    common = dict(
        wqkv=wqkv, wgate=wgate, woutw=woutw, a_t=a_t, b_t=b_t, wr_t=wr_t,
        br_in=np.ascontiguousarray(br.reshape(1, 8)), rmsw_in=rmsw_fm,
        s_col_in=np.ascontiguousarray(slope.reshape(NH, 1)),
        idx_in=idx_c, gq_in=gq, gk_in=gk, gbd_in=gbd, eexp_in=eexp,
    )
    in_maps = []
    for c in range(8):
        bidx, half = c // 2, c % 2
        xc = np.ascontiguousarray(x[bidx, half * TOKS:(half + 1) * TOKS, :].T)
        flag = np.full((P, 1), 1.0 if half == 1 else 0.0, np.float32)
        m = dict(common)
        m["x_in"] = xc
        m["flag_in"] = flag
        in_maps.append(m)
    return in_maps


def kernel(**inputs):
    in_maps = build_inputs(inputs)
    bld = Builder()
    nc = bld.build()
    res = run_bass_kernel_spmd(nc, in_maps, core_ids=list(range(8)))
    out = np.empty((4, 2048, 2048), np.float32)
    for c in range(8):
        bidx, half = c // 2, c % 2
        out[bidx, half * TOKS:(half + 1) * TOKS, :] = res.results[c]["out_t"].T
    return out


# version 07
# speedup vs baseline: 1.0043x; 1.0043x over previous
import numpy as np
from contextlib import ExitStack

import concourse.bass as bass
import concourse.mybir as mybir
from concourse.bass_utils import run_bass_kernel_spmd

F32 = mybir.dt.float32
BF16 = mybir.dt.bfloat16
AF = mybir.ActivationFunctionType
ALU = mybir.AluOpType
AX = mybir.AxisListType.X

P = 128
TOKS = 1024
NCH = 16
NH = 16
HG = 4
NG = NH // HG
BLK = 256
NBLK = TOKS // BLK
DH = 128
FFN_SCALE = 0.5 / 64.0
NEG_BIG = -1.0e9

SB0 = 16512
X_BASE = SB0
X_SLOT = 4096
NSLOT = 17
XN_BASE = SB0 + 69632
OB_BASE = SB0 + 102400
WK_BASE = SB0 + 135168
CONST_BASE = SB0 + 196608

KVQ = [X_BASE, X_BASE + 24576]
WSTR = [X_BASE + 49152, X_BASE + 57344]

WK_S = [WK_BASE, WK_BASE + 8192]
WK_T = [WK_BASE + 16384, WK_BASE + 20480]
WK_KVI = [WK_BASE + 24576, WK_BASE + 26624]
WK_D = WK_BASE + 28672
WK_QDB = WK_BASE + 36864
WK_KS = [WK_BASE + 40960, WK_BASE + 43008]
WK_QS = [WK_BASE + 45056, WK_BASE + 45568]
WK_QKD = [WK_BASE + 46080, WK_BASE + 47104]
WK_GTMP = WK_BASE + 48128
WK_OSQ = [WK_BASE + 50176, WK_BASE + 50688]

WK_ABF = WK_BASE
WK_BBF = WK_BASE + 16384
WK_WRBF = WK_BASE + 32768
WK_XB = WK_BASE + 33280
WK_SQ = WK_BASE + 41472
WK_TMP = WK_BASE + 49664
WK_ROWA = WK_BASE + 53760
WK_LGS = WK_BASE + 57856
WK_EXB = WK_BASE + 58112
WK_OHB = WK_BASE + 58368
WK_MXB = WK_BASE + 58624
WK_NMX = WK_BASE + 58656
WK_SEB = WK_BASE + 58688
WK_WB = WK_BASE + 58720
WK_GST = [WK_BASE, WK_BASE + 8192]
WK_SG = [WK_BASE + 16384, WK_BASE + 18432]
WK_T2 = [WK_BASE + 20480, WK_BASE + 22528]

OB_MASKB = OB_BASE
OB_HM = OB_BASE + 16384
OB_MWTM = OB_BASE + 24576
OB_MWFM = OB_BASE + 24832

_CO = [CONST_BASE]


def _co(nbytes, align=64):
    _CO[0] = (_CO[0] + align - 1) // align * align
    off = _CO[0]
    _CO[0] += nbytes
    return off


class Builder:
    def __init__(self, taps=()):
        self.taps = set(taps)
        nc = bass.Bass("TRN2", target_bir_lowering=False)
        self.nc = nc
        self.es = ExitStack()

        dt = nc.dram_tensor
        self.x_in = dt("x_in", [2048, TOKS], F32, kind="ExternalInput")
        self.wqkv = dt("wqkv", [2048, 6144], F32, kind="ExternalInput")
        self.wgate = dt("wgate", [2048, 2048], F32, kind="ExternalInput")
        self.woutw = dt("woutw", [2048, 2048], F32, kind="ExternalInput")
        self.a_t = dt("a_t", [2048, 512], F32, kind="ExternalInput")
        self.b_t = dt("b_t", [512, 2048], F32, kind="ExternalInput")
        self.wr_t = dt("wr_t", [2048, 8], F32, kind="ExternalInput")
        self.br_in = dt("br_in", [1, 8], F32, kind="ExternalInput")
        self.rmsw_in = dt("rmsw_in", [P, NCH], F32, kind="ExternalInput")
        self.s_col_in = dt("s_col_in", [NH, 1], F32, kind="ExternalInput")
        self.flag_in = dt("flag_in", [P, 1], F32, kind="ExternalInput")
        self.idx_in = dt("idx_in", [P, 512], F32, kind="ExternalInput")
        self.gq_in = dt("gq_in", [NH, BLK], F32, kind="ExternalInput")
        self.gk_in = dt("gk_in", [NH, BLK], F32, kind="ExternalInput")
        self.gbd_in = dt("gbd_in", [NH, NBLK], F32, kind="ExternalInput")
        self.eexp_in = dt("eexp_in", [8, 512], F32, kind="ExternalInput")
        self.out_t = dt("out_t", [2048, TOKS], F32, kind="ExternalOutput")
        self.xspill = dt("xspill", [P, NSLOT * TOKS], F32)
        self.wqkv_bf = dt("wqkv_bf", [2048, 6144], BF16)
        self.wg_bf = dt("wg_bf", [2048, 2048], BF16)
        self.wo_bf = dt("wo_bf", [2048, 2048], BF16)
        self.qd_dram = dt("qd_dram", [NH, BLK], F32)
        self.bdp_dram = dt("bdp_dram", [NH, NBLK], F32)
        self.cc_in = [dt(f"cc_in{g}", [HG * P, DH], F32) for g in range(NG)]
        self.cc_out = [dt(f"cc_out{g}", [2 * HG * P, DH], F32) for g in range(NG)]

        at = nc.alloc_sbuf_tensor_at
        self.xslot = [at(f"xslot{i}", [P, TOKS], F32, offset=X_BASE + i * X_SLOT)
                      for i in range(NSLOT)]
        self.xall = at("xall", [P, NSLOT * TOKS], F32, offset=X_BASE)
        self.slotmap = list(range(NCH))
        self.spare = NCH
        self.xn = at("xn", [P, NCH * TOKS], BF16, offset=XN_BASE)
        self.o_bf = at("o_bf", [P, NH * TOKS], BF16, offset=OB_BASE)
        self.k_g = [at(f"k_g{i}", [P, HG * TOKS], BF16, offset=KVQ[i]) for i in range(2)]
        self.v_g = [at(f"v_g{i}", [P, HG * TOKS], BF16, offset=KVQ[i] + 8192)
                    for i in range(2)]
        self.q_g = [at(f"q_g{i}", [P, HG * TOKS], BF16, offset=KVQ[i] + 16384)
                    for i in range(2)]
        self.wstr = [at(f"wstr{i}", [P, NCH * 256], BF16, offset=WSTR[i]) for i in range(2)]
        self.S_g = [at(f"S_g{i}", [P, HG * 4 * DH], F32, offset=WK_S[i]) for i in range(2)]
        self.T_g = [at(f"T_g{i}", [P, HG * 4 * DH], BF16, offset=WK_T[i]) for i in range(2)]
        self.kvi_g = [at(f"kvi_g{i}", [P, HG * DH], F32, offset=WK_KVI[i]) for i in range(2)]
        self.D_g = at("D_g", [P, HG * 512], F32, offset=WK_D)
        self.qdB_g = at("qdB_g", [P, HG * BLK], F32, offset=WK_QDB)
        self.ks_h = [at(f"ks_h{i}", [P, TOKS], BF16, offset=WK_KS[i]) for i in range(2)]
        self.qs_b = [at(f"qs_b{i}", [P, BLK], BF16, offset=WK_QS[i]) for i in range(2)]
        self.qkD = [at(f"qkD{i}", [P, 512], BF16, offset=WK_QKD[i]) for i in range(2)]
        self.gtmp = at("gtmp", [P, 512], F32, offset=WK_GTMP)
        self.osq = [at(f"osq{i}", [P, BLK], BF16, offset=WK_OSQ[i]) for i in range(2)]
        self.a_bf = at("a_bf", [P, NCH * 512], BF16, offset=WK_ABF)
        self.b_bf = at("b_bf", [P, 4 * 2048], BF16, offset=WK_BBF)
        self.wr_bf = at("wr_bf", [P, NCH * 8], BF16, offset=WK_WRBF)
        self.xb = at("xb", [P, 4 * TOKS], BF16, offset=WK_XB)
        self.sq = at("sq", [P, 4 * TOKS], BF16, offset=WK_SQ)
        self.tmp = at("tmp", [P, TOKS], F32, offset=WK_TMP)
        self.rows_a = at("rows_a", [P, TOKS], F32, offset=WK_ROWA)
        self.rstd_sb = at("rstd_sb", [P, TOKS], F32, offset=WK_XB)
        self.nm_r = at("nm_r", [1, TOKS], F32, offset=WK_XB + 4096)
        self.m2_r = at("m2_r", [1, TOKS], F32, offset=WK_TMP)
        self.rstd_r = at("rstd_r", [1, TOKS], F32, offset=WK_SQ)
        self.nmr_r = at("nmr_r", [1, TOKS], F32, offset=WK_SQ + 4096)
        self.lgs = at("lgs", [P, 64], F32, offset=WK_LGS)
        self.exb = at("exb", [P, 64], F32, offset=WK_EXB)
        self.ohb = at("ohb", [P, 64], F32, offset=WK_OHB)
        self.mxb = at("mxb", [P, 8], F32, offset=WK_MXB)
        self.nmxb = at("nmxb", [P, 8], F32, offset=WK_NMX)
        self.seb = at("seb", [P, 8], F32, offset=WK_SEB)
        self.wb = at("wb", [P, 8], F32, offset=WK_WB)
        self.mask_b = at("mask_b", [P, 4 * TOKS], F32, offset=OB_MASKB)
        self.hm = at("hm", [P, 4 * TOKS], BF16, offset=OB_HM)
        self.mw_tm = at("mw_tm", [P, 64], F32, offset=OB_MWTM)
        self.mw_fm = at("mw_fm", [8, TOKS], F32, offset=OB_MWFM)
        self.gst = [at(f"gst{i}", [P, NCH * 256], BF16, offset=WK_GST[i]) for i in range(2)]
        self.sg_b = [at(f"sg_b{i}", [P, TOKS], F32, offset=WK_BASE + 16384 + i * 4096)
                     for i in range(2)]
        self.t2_b = [at(f"t2_b{i}", [P, TOKS], F32, offset=WK_BASE + 24576 + i * 4096)
                     for i in range(2)]
        self.idBF = at("idBF", [P, P], BF16, offset=_co(256))
        self.idF32 = at("idF32", [P, P], F32, offset=_co(512))
        self.ones_row_bf = at("ones_row_bf", [1, P], BF16, offset=_co(256))
        self.ones_row_f = at("ones_row_f", [1, P], F32, offset=_co(512))
        self.onesP_bf = at("onesP_bf", [P, 1], BF16, offset=_co(32))
        self.eps_ln = at("eps_ln", [1, 1], F32, offset=_co(32))
        self.eps_rms = at("eps_rms", [1, 1], F32, offset=_co(32))
        self.s_col = at("s_col", [NH, 1], F32, offset=_co(32))
        self.gq = at("gq", [NH, BLK], F32, offset=_co(1024))
        self.gk = at("gk", [NH, BLK], F32, offset=_co(1024))
        self.gbd = at("gbd", [NH, NBLK], F32, offset=_co(32))
        self.qd16 = at("qd16", [NH, BLK], F32, offset=_co(1024))
        self.kd16 = at("kd16", [NH, BLK], F32, offset=_co(1024))
        self.bdp16 = at("bdp16", [NH, NBLK], F32, offset=_co(32))
        self.idx_c = at("idx_c", [P, 512], F32, offset=_co(2048))
        self.eexp = at("eexp", [8, 512], F32, offset=_co(2048))
        self.br_b = at("br_b", [P, 8], F32, offset=_co(32))
        self.rmsw = at("rmsw", [P, NCH], F32, offset=_co(64))
        self.s_bc = at("s_bc", [P, NH], F32, offset=_co(64))
        self.bdp_b = at("bdp_b", [P, NH * NBLK], F32, offset=_co(256))
        self.kd_colsT = at("kd_colsT", [P, 32], F32, offset=_co(128))
        self.flag_c = at("flag_c", [P, 1], F32, offset=_co(32))
        assert _CO[0] <= 229000, _CO[0]

        ec = self.es.enter_context
        self.pA = ec(nc.psum_tensor("pA", [P, 512], F32))
        self.pB = ec(nc.psum_tensor("pB", [P, 512], F32))
        self.pQK = ec(nc.psum_tensor("pQK", [P, 512], F32))
        self.pU = ec(nc.psum_tensor("pU", [P, 512], F32))
        self.pO = ec(nc.psum_tensor("pO", [P, 512], F32))
        self.pT = ec(nc.psum_tensor("pT", [P, 1024], BF16))
        self.pV0 = ec(nc.psum_tensor("pV0", [1, 1024], F32))

        self.sem_g = ec(nc.semaphore("sem_g"))
        self.sem_y = ec(nc.semaphore("sem_y"))
        self.sem_cc = ec(nc.semaphore("sem_cc"))
        self.sem_cast = ec(nc.semaphore("sem_cast"))
        self.sem_w = [ec(nc.semaphore("sem_w0")), ec(nc.semaphore("sem_w1"))]
        self.exp_w = [0, 0]
        self.sem_t = ec(nc.semaphore("sem_t"))
        self.sem_v = ec(nc.semaphore("sem_v"))
        self.sem_s = ec(nc.semaphore("sem_s"))
        self.exp_g = 0
        self.exp_y = 0
        self.exp_cc = 0
        self.c_t = 0
        self.c_v = 0
        self.c_s = 0

    def block(self, **bodies):
        with self.nc.Block() as b:
            for eng in ("sync", "gpsimd", "tensor", "vector", "scalar"):
                if eng in bodies:
                    getattr(b, eng)(bodies[eng])

    def gdma(self, g, out, in_):
        g.dma_start(out=out, in_=in_).then_inc(self.sem_g, 16)
        self.exp_g += 16

    def wdma(self, g, slot, out, in_):
        g.dma_start(out=out, in_=in_).then_inc(self.sem_w[slot], 16)
        self.exp_w[slot] += 16
        return self.exp_w[slot]

    def xall16(self):
        return self.xall[:, 0:NCH * TOKS].rearrange("p (c t) -> p c t", c=NCH)

    def ydma(self, y, out, in_):
        y.dma_start(out=out, in_=in_).then_inc(self.sem_y, 16)
        self.exp_y += 16

    def tap(self, name, tensor, rows, cols):
        if name not in self.taps:
            return
        t = self.nc.dram_tensor(f"tap_{name}", [rows, cols], tensor.dtype,
                                kind="ExternalOutput")

        def _s(y):
            self.ydma(y, t[:, :], tensor[0:rows, 0:cols])
            y.wait_ge(self.sem_y, self.exp_y)
        self.block(sync=_s)

    def mark_t(self, ins):
        ins.then_inc(self.sem_t, 1)
        self.c_t += 1
        return self.c_t

    def mark_v(self, ins):
        ins.then_inc(self.sem_v, 1)
        self.c_v += 1
        return self.c_v

    def mark_s(self, ins):
        ins.then_inc(self.sem_s, 1)
        self.c_s += 1
        return self.c_s

    def xs(self, c):
        return self.xslot[self.slotmap[c]]

    def rotate(self, c):
        old = self.slotmap[c]
        self.slotmap[c] = self.spare
        self.spare = old

    def emit_setup(self):
        b = self

        def g_pre(g):
            g.memset(b.idBF[:, :], 1.0)
            g.memset(b.idF32[:, :], 1.0)
        self.block(gpsimd=g_pre)

        def g_body(g):
            g.affine_select(b.idBF[:, :], b.idBF[:, :], [[-1, P]], ALU.is_equal, 0.0,
                            base=0, channel_multiplier=1)
            g.affine_select(b.idF32[:, :], b.idF32[:, :], [[-1, P]], ALU.is_equal, 0.0,
                            base=0, channel_multiplier=1)
            g.memset(b.ones_row_bf[:, :], 1.0)
            g.memset(b.ones_row_f[:, :], 1.0)
            g.memset(b.onesP_bf[:, :], 1.0)
            g.memset(b.eps_ln[:, :], 1e-5)
            g.memset(b.eps_rms[:, :], 1e-6)
            b.gdma(g, b.s_col[:, :], b.s_col_in[:, :])
            b.gdma(g, b.gq[:, :], b.gq_in[:, :])
            b.gdma(g, b.gk[:, :], b.gk_in[:, :])
            b.gdma(g, b.gbd[:, :], b.gbd_in[:, :])
            b.gdma(g, b.idx_c[:, :], b.idx_in[:, :])
            b.gdma(g, b.eexp[:, :], b.eexp_in[:, :])
            b.gdma(g, b.rmsw[:, :], b.rmsw_in[:, :])
            b.gdma(g, b.flag_c[:, :], b.flag_in[:, :])
            b.gdma(g, b.br_b[:, :], bass.AP(b.br_in, 0, [[0, P], [1, 8]]))
            b.gdma(g, b.s_bc[:, :], bass.AP(b.s_col_in, 0, [[0, P], [1, NH]]))
            g.wait_ge(b.sem_g, b.exp_g)
            g.dma_start(out=b.wqkv_bf.ap(), in_=b.wqkv.ap()).then_inc(b.sem_cast, 16)
            g.dma_start(out=b.wg_bf.ap(), in_=b.wgate.ap()).then_inc(b.sem_cast, 16)
            g.dma_start(out=b.wo_bf.ap(), in_=b.woutw.ap()).then_inc(b.sem_cast, 16)

        def y_body(y):
            b.ydma(y, b.xall16(),
                   b.x_in.ap().rearrange("(c p) t -> p c t", p=P))
            y.wait_ge(b.sem_y, b.exp_y)

        self.block(gpsimd=g_body, sync=y_body)
        self.cast_cnt = self.exp_g

        def v_body(v):
            v.tensor_scalar(b.qd16[:, :], b.gq[:, :], b.s_col[:, 0:1], None, ALU.mult)
            v.tensor_scalar(b.kd16[:, :], b.gk[:, :], b.s_col[:, 0:1], None, ALU.mult)
            v.tensor_scalar(b.bdp16[:, :], b.gbd[:, :], b.s_col[:, 0:1], None, ALU.mult)
        self.block(vector=v_body)

        def s_body(s):
            s.activation(b.qd16[:, :], b.qd16[:, :], AF.Exp)
            s.activation(b.kd16[:, :], b.kd16[:, :], AF.Exp)
            s.activation(b.bdp16[:, :], b.bdp16[:, :], AF.Exp)
        self.block(scalar=s_body)

        def g2(g):
            b.gdma(g, b.qd_dram[:, :], b.qd16[:, :])
            b.gdma(g, b.bdp_dram[:, :], b.bdp16[:, :])
            g.wait_ge(b.sem_g, b.exp_g)
            b.gdma(g, b.bdp_b[:, :],
                   bass.AP(b.bdp_dram, 0, [[0, P], [NBLK, NH], [1, NBLK]]))
            g.wait_ge(b.sem_g, b.exp_g)
        self.block(gpsimd=g2)

        def t_body(t):
            t.transpose(b.pQK[:, 0:16], b.kd16[:, 0:128], b.idF32[0:16, 0:16])
            t.transpose(b.pQK[:, 16:32], b.kd16[:, 128:256], b.idF32[0:16, 0:16])
        self.block(tensor=t_body)

        def v2(v):
            v.tensor_copy(b.kd_colsT[:, 0:32], b.pQK[:, 0:32])
        self.block(vector=v2)

    def load_ffn_weights(self):
        b = self

        def g_body(g):
            b.gdma(g, b.a_bf.ap().rearrange("p (c r) -> p c r", c=NCH),
                   b.a_t.ap().rearrange("(c p) r -> p c r", p=P))
            b.gdma(g, b.b_bf.ap().rearrange("p (c o) -> p c o", c=4),
                   b.b_t.ap().rearrange("(c p) o -> p c o", p=P))
            b.gdma(g, b.wr_bf.ap().rearrange("p (c e) -> p c e", c=NCH),
                   b.wr_t.ap().rearrange("(c p) e -> p c e", p=P))
            g.wait_ge(b.sem_g, b.exp_g)
        self.block(gpsimd=g_body)

    def emit_ln(self, tag):
        b = self
        nb = 8
        t0, s0, v0 = self.c_t, self.c_s, self.c_v
        xsrc = [b.xs(c) for c in range(NCH)]

        def s_body(s):
            for k in range(nb):
                off = (k % 2) * 2 * TOKS
                if k >= 2:
                    s.wait_ge(b.sem_t, t0 + k - 1)
                for i, c in enumerate([2 * k, 2 * k + 1]):
                    ins = s.activation(b.sq[:, off + i * TOKS:off + (i + 1) * TOKS],
                                       xsrc[c][:, :], AF.Square)
                b.c_s += 1
                ins.then_inc(b.sem_s, 1)

        def v_body(v):
            for k in range(nb):
                off = (k % 2) * 2 * TOKS
                if k >= 2:
                    v.wait_ge(b.sem_t, t0 + k - 1)
                for i, c in enumerate([2 * k, 2 * k + 1]):
                    ins = v.tensor_copy(b.xb[:, off + i * TOKS:off + (i + 1) * TOKS],
                                        xsrc[c][:, :])
                b.c_v += 1
                ins.then_inc(b.sem_v, 1)

        def t_body(t):
            for k in range(nb):
                off = (k % 2) * 2 * TOKS
                t.wait_ge(b.sem_s, s0 + k + 1)
                t.wait_ge(b.sem_v, v0 + k + 1)
                for i, c in enumerate([2 * k, 2 * k + 1]):
                    st = (c == 0)
                    sp = (c == NCH - 1)
                    xbv = b.xb[:, off + i * TOKS:off + (i + 1) * TOKS]
                    sqv = b.sq[:, off + i * TOKS:off + (i + 1) * TOKS]
                    t.matmul(b.pV0[0:1, 0:512], b.onesP_bf[:, :], xbv[:, 0:512],
                             start=st, stop=sp, skip_group_check=True)
                    t.matmul(b.pV0[0:1, 512:1024], b.onesP_bf[:, :], xbv[:, 512:1024],
                             start=st, stop=sp, skip_group_check=True)
                    t.matmul(b.pU[0:1, 0:512], b.onesP_bf[:, :], sqv[:, 0:512],
                             start=st, stop=sp, skip_group_check=True)
                    ins = t.matmul(b.pQK[0:1, 0:512], b.onesP_bf[:, :], sqv[:, 512:1024],
                                   start=st, stop=sp, skip_group_check=True)
                b.c_t += 1
                ins.then_inc(b.sem_t, 1)

        self.block(scalar=s_body, vector=v_body, tensor=t_body)

        def s_nm(s):
            s.activation(b.nm_r[0:1, :], b.pV0[0:1, :], AF.Copy, scale=-1.0 / 2048.0)
        self.block(scalar=s_nm)

        def s_m2(s):
            s.activation(b.m2_r[0:1, :], b.nm_r[0:1, :], AF.Square)
        self.block(scalar=s_m2)

        def v_var(v):
            v.scalar_tensor_tensor(b.rstd_r[0:1, 0:512], b.pU[0:1, 0:512],
                                   1.0 / 2048.0, b.m2_r[0:1, 0:512],
                                   ALU.mult, ALU.subtract)
            v.scalar_tensor_tensor(b.rstd_r[0:1, 512:1024], b.pQK[0:1, 0:512],
                                   1.0 / 2048.0, b.m2_r[0:1, 512:1024],
                                   ALU.mult, ALU.subtract)
        self.block(vector=v_var)

        def s_rstd(s):
            s.activation(b.rstd_r[0:1, :], b.rstd_r[0:1, :], AF.Ln,
                         bias=b.eps_ln[0:1, 0:1])
            s.activation(b.rstd_r[0:1, :], b.rstd_r[0:1, :], AF.Exp, scale=-0.5)
        self.block(scalar=s_rstd)

        def v_nmr(v):
            v.tensor_tensor(b.nmr_r[0:1, :], b.nm_r[0:1, :], b.rstd_r[0:1, :],
                            ALU.mult)
        self.block(vector=v_nmr)

        def t_bcast(t):
            t.matmul(b.pA[:, :], b.ones_row_f[:, :], b.rstd_r[0:1, 0:512],
                     start=True, stop=True)
            t.matmul(b.pB[:, :], b.ones_row_f[:, :], b.rstd_r[0:1, 512:1024],
                     start=True, stop=True)
            t.matmul(b.pQK[:, :], b.ones_row_f[:, :], b.nmr_r[0:1, 0:512],
                     start=True, stop=True)
            t.matmul(b.pO[:, :], b.ones_row_f[:, :], b.nmr_r[0:1, 512:1024],
                     start=True, stop=True)
        self.block(tensor=t_bcast)

        def v_apply(v):
            for c in range(NCH):
                x = b.xs(c)
                xnv = b.xn[:, c * TOKS:(c + 1) * TOKS]
                v.tensor_tensor(b.tmp[:, 0:512], x[:, 0:512], b.pA[:, :], ALU.mult)
                v.tensor_tensor(b.tmp[:, 512:1024], x[:, 512:1024], b.pB[:, :], ALU.mult)
                v.tensor_tensor(xnv[:, 0:512], b.tmp[:, 0:512], b.pQK[:, :], ALU.add)
                v.tensor_tensor(xnv[:, 512:1024], b.tmp[:, 512:1024], b.pO[:, :], ALU.add)
        self.block(vector=v_apply)
        self.tap(f"xn_{tag}", b.xn, P, NCH * TOKS)

    def emit_ffn(self, tag):
        b = self
        banks4 = [b.pA, b.pB, b.pQK, b.pO]

        def t_lg(t):
            for tt in range(8):
                for c in range(NCH):
                    t.matmul(b.pA[:, tt * 16:tt * 16 + 8],
                             b.xn[:, c * TOKS + tt * P:c * TOKS + (tt + 1) * P],
                             b.wr_bf[:, c * 8:(c + 1) * 8],
                             start=(c == 0), stop=(c == NCH - 1), skip_group_check=True)
        self.block(tensor=t_lg)

        def v_sm1(v):
            for tt in range(8):
                v.tensor_tensor(b.lgs[:, tt * 8:(tt + 1) * 8],
                                b.pA[:, tt * 16:tt * 16 + 8], b.br_b[:, :], ALU.add)
            for tt in range(8):
                v.reduce_max(b.mxb[:, tt:tt + 1], b.lgs[:, tt * 8:(tt + 1) * 8], AX)
            for tt in range(8):
                v.tensor_scalar(b.exb[:, tt * 8:(tt + 1) * 8], b.lgs[:, tt * 8:(tt + 1) * 8],
                                b.mxb[:, tt:tt + 1], None, ALU.subtract)
        self.block(vector=v_sm1)

        def s_exp(s):
            s.activation(b.exb[:, :], b.exb[:, :], AF.Exp)
        self.block(scalar=s_exp)

        def v_sm2a(v):
            for tt in range(8):
                v.reduce_sum(b.seb[:, tt:tt + 1], b.exb[:, tt * 8:(tt + 1) * 8], AX)
        self.block(vector=v_sm2a)

        def s_recip(s):
            s.activation(b.wb[:, :], b.seb[:, :], AF.Ln)
            s.activation(b.wb[:, :], b.wb[:, :], AF.Exp, scale=-1.0)
        self.block(scalar=s_recip)

        def v_sm2(v):
            for tt in range(8):
                v.tensor_scalar(b.ohb[:, tt * 8:(tt + 1) * 8], b.lgs[:, tt * 8:(tt + 1) * 8],
                                b.mxb[:, tt:tt + 1], None, ALU.is_equal)
            for tt in range(8):
                v.tensor_scalar(b.mw_tm[:, tt * 8:(tt + 1) * 8],
                                b.ohb[:, tt * 8:(tt + 1) * 8],
                                b.wb[:, tt:tt + 1], FFN_SCALE, ALU.mult, ALU.mult)
        self.block(vector=v_sm2)
        self.tap(f"seb_{tag}", b.seb, P, 8)
        self.tap(f"lgs_{tag}", b.lgs, P, 64)
        self.tap(f"exb_{tag}", b.exb, P, 64)
        self.tap(f"ohb_{tag}", b.ohb, P, 64)
        self.tap(f"mxb_{tag}", b.mxb, P, 8)
        self.tap(f"wb_{tag}", b.wb, P, 8)
        self.tap(f"mwtm_{tag}", b.mw_tm, P, 64)

        def t_tr(t):
            for tt in range(8):
                dst = b.pO if tt < 4 else b.pU
                t.transpose(dst[0:8, (tt % 4) * P:(tt % 4 + 1) * P],
                            b.mw_tm[:, tt * 8:(tt + 1) * 8], b.idF32[:, :])
        self.block(tensor=t_tr)

        def v_mw(v):
            v.tensor_copy(b.mw_fm[0:8, 0:512], b.pO[0:8, :])
            v.tensor_copy(b.mw_fm[0:8, 512:1024], b.pU[0:8, :])
        self.block(vector=v_mw)

        for wave in range(2):
            def t_ex(t, wave=wave):
                for i in range(2):
                    rc = wave * 2 + i
                    for tt in range(2):
                        bank = banks4[i * 2 + tt]
                        t.matmul(bank[:, :], b.eexp[:, rc * P:(rc + 1) * P],
                                 b.mw_fm[0:8, tt * 512:(tt + 1) * 512],
                                 start=True, stop=True)
            self.block(tensor=t_ex)

            def v_ex(v, wave=wave):
                for i in range(2):
                    rc = wave * 2 + i
                    for tt in range(2):
                        bank = banks4[i * 2 + tt]
                        v.tensor_copy(
                            b.mask_b[:, rc * TOKS + tt * 512:rc * TOKS + (tt + 1) * 512],
                            bank[:, :])
            self.block(vector=v_ex)

        htiles = [(rc, tt) for rc in range(4) for tt in range(2)]
        bsets = [(b.pA, b.pB), (b.pQK, b.pO)]
        t0, v0 = self.c_t, self.c_v

        def t_h(t):
            for w in range(0, len(htiles), 2):
                if w >= 4:
                    t.wait_ge(b.sem_v, v0 + w // 2 - 1)
                for i, (rc, tt) in enumerate(htiles[w:w + 2]):
                    bank = bsets[(w // 2) % 2][i]
                    for c in range(NCH):
                        ins = t.matmul(bank[:, :],
                                       b.a_bf[:, c * 512 + rc * P:c * 512 + (rc + 1) * P],
                                       b.xn[:, c * TOKS + tt * 512:c * TOKS + (tt + 1) * 512],
                                       start=(c == 0), stop=(c == NCH - 1))
                b.c_t += 1
                ins.then_inc(b.sem_t, 1)

        def v_h(v):
            for w in range(0, len(htiles), 2):
                v.wait_ge(b.sem_t, t0 + w // 2 + 1)
                for i, (rc, tt) in enumerate(htiles[w:w + 2]):
                    bank = bsets[(w // 2) % 2][i]
                    sl = slice(rc * TOKS + tt * 512, rc * TOKS + (tt + 1) * 512)
                    ins = v.tensor_tensor(b.hm[:, sl], bank[:, :], b.mask_b[:, sl], ALU.mult)
                b.c_v += 1
                ins.then_inc(b.sem_v, 1)

        self.block(tensor=t_h, vector=v_h)
        self.tap(f"hm_{tag}", b.hm, P, 4 * TOKS)

        store = (tag == "ffn2")
        t0, v0 = self.c_t, self.c_v
        rotplan = []
        sp = b.spare
        sm = list(b.slotmap)
        for oc in range(NCH):
            rotplan.append((sm[oc], sp))
            sm[oc], sp = sp, sm[oc]

        def t_y(t):
            for oc in range(NCH):
                if oc >= 2:
                    t.wait_ge(b.sem_v, v0 + oc - 1)
                for tt in range(2):
                    bank = bsets[oc % 2][tt]
                    for rc in range(4):
                        ins = t.matmul(bank[:, :],
                                       b.b_bf[:, rc * 2048 + oc * P:rc * 2048 + (oc + 1) * P],
                                       b.hm[:, rc * TOKS + tt * 512:rc * TOKS + (tt + 1) * 512],
                                       start=(rc == 0), stop=(rc == 3))
                b.c_t += 1
                ins.then_inc(b.sem_t, 1)

        def v_y(v):
            for oc in range(NCH):
                v.wait_ge(b.sem_t, t0 + oc + 1)
                olds, news = rotplan[oc]
                for tt in range(2):
                    bank = bsets[oc % 2][tt]
                    ins = v.tensor_tensor(b.xslot[news][:, tt * 512:(tt + 1) * 512],
                                          bank[:, :],
                                          b.xslot[olds][:, tt * 512:(tt + 1) * 512], ALU.add)
                b.c_v += 1
                ins.then_inc(b.sem_v, 1)

        bodies = dict(tensor=t_y, vector=v_y)
        if store:
            def y_st(y):
                for oc in range(NCH):
                    y.wait_ge(b.sem_v, v0 + oc + 1)
                    self.ydma(y, b.out_t[oc * P:(oc + 1) * P, :],
                              b.xslot[rotplan[oc][1]][:, :])
                y.wait_ge(b.sem_y, b.exp_y)
            bodies["sync"] = y_st
        self.block(**bodies)
        for oc in range(NCH):
            b.rotate(oc)

    def emit_attn(self):
        b = self
        self.emit_ln("attn")

        spill_map = list(self.slotmap)

        def y_spill(y):
            self.ydma(y, b.xspill.ap(), b.xall[:, :])
        self.block(sync=y_spill)
        spill_cnt = b.exp_y

        self.emit_proj(0, 0, extra_wait=spill_cnt)
        self.emit_amath(0, 0)
        self.emit_cc_issue(0)
        for g in range(NG):
            gb = g % 2
            if g + 1 < NG:
                self.emit_proj(g + 1, (g + 1) % 2)
            self.emit_cc_fixup(g, gb)
            self.emit_bmath(g, gb)
            if g + 1 < NG:
                self.emit_amath(g + 1, (g + 1) % 2)
                self.emit_cc_issue(g + 1)

        self.tap("o", b.o_bf, P, NH * TOKS)

        def s_rv(s):
            s.activation(b.rstd_r[0:1, :], b.pV0[0:1, :], AF.Ln,
                         bias=b.eps_rms[0:1, 0:1], scale=1.0 / 2048.0)
        self.block(scalar=s_rv)

        def s_rv2(s):
            s.activation(b.rstd_r[0:1, :], b.rstd_r[0:1, :], AF.Exp, scale=-0.5)
        self.block(scalar=s_rv2)

        def t_rvb(t):
            t.matmul(b.pQK[:, :], b.ones_row_f[:, :], b.rstd_r[0:1, 0:512],
                     start=True, stop=True)
            t.matmul(b.pU[:, :], b.ones_row_f[:, :], b.rstd_r[0:1, 512:1024],
                     start=True, stop=True)
        self.block(tensor=t_rvb)

        def v_rvb(v):
            v.tensor_copy(b.rstd_sb[:, 0:512], b.pQK[:, :])
            v.tensor_copy(b.rstd_sb[:, 512:1024], b.pU[:, :])
        self.block(vector=v_rvb)

        def y_rest(y):
            self.ydma(y, b.xall[:, :], b.xspill.ap())
        self.block(sync=y_rest)
        restore_cnt = b.exp_y
        self.slotmap = spill_map
        self.spare = [i for i in range(NSLOT) if i not in spill_map][0]

        self.emit_gate()
        self.tap("os", b.o_bf, P, NH * TOKS)

        self.emit_wout(restore_cnt)
        self.tap("x2", self.xs(0), P, TOKS)

    def emit_proj(self, g, gb, extra_wait=0):
        b = self
        kg, vg, qg = b.k_g[gb], b.v_g[gb], b.q_g[gb]
        pieces = []
        for which, colbase in (("k", 2048 + g * 512), ("v", 4096 + g * 512),
                               ("q", 0 + g * 512)):
            for half in range(2):
                pieces.append((which, colbase + half * 256, half))

        waves = []
        for p, (which, cb, half) in enumerate(pieces):
            ws = b.wstr[p % 2]
            if which in ("k", "q"):
                dst = kg if which == "k" else qg
                for hi2 in range(2):
                    hi = half * 2 + hi2

                    def mm(t, banks, ws=ws, hi2=hi2):
                        for tt in range(2):
                            for c in range(NCH):
                                ins = t.matmul(
                                    banks[tt][:, :],
                                    ws[:, c * 256 + hi2 * P:c * 256 + (hi2 + 1) * P],
                                    b.xn[:, c * TOKS + tt * 512:c * TOKS + (tt + 1) * 512],
                                    start=(c == 0), stop=(c == NCH - 1))
                        return ins

                    def ev(s, banks, dst=dst, hi=hi):
                        for tt in range(2):
                            ins = s.activation(
                                dst[:, hi * TOKS + tt * 512:hi * TOKS + (tt + 1) * 512],
                                banks[tt][:, :], AF.Silu)
                        return ins
                    waves.append((p, mm, ev))
            else:
                for wv in range(4):
                    def mm(t, banks, ws=ws, wv=wv):
                        for i in range(2):
                            tt = wv * 2 + i
                            for c in range(NCH):
                                ins = t.matmul(
                                    banks[i][:, 0:256],
                                    b.xn[:, c * TOKS + tt * P:c * TOKS + (tt + 1) * P],
                                    ws[:, c * 256:(c + 1) * 256],
                                    start=(c == 0), stop=(c == NCH - 1))
                        return ins

                    def ev(s, banks, vg=vg, wv=wv, half=half):
                        for i in range(2):
                            tt = wv * 2 + i
                            ins = s.activation(
                                vg[:, tt * 512 + half * 256:tt * 512 + (half + 1) * 256],
                                banks[i][:, 0:256], AF.Silu)
                        return ins
                    waves.append((p, mm, ev))

        nw = len(waves)
        bsets = [(b.pA, b.pB), (b.pQK, b.pO)]
        t0, s0 = self.c_t, self.c_s
        piece_first = {}
        piece_last = {}
        for w, (p, _, _) in enumerate(waves):
            piece_first.setdefault(p, w)
            piece_last[p] = w
        piece_cnt = [self.exp_w[p % 2] + 16 * (p // 2 + 1) for p in range(len(pieces))]

        def g_body(gq):
            gq.wait_ge(b.sem_cast, 48)
            if extra_wait:
                gq.wait_ge(b.sem_y, extra_wait)
            for p, (which, cb, half) in enumerate(pieces):
                if p >= 2:
                    gq.wait_ge(b.sem_t, t0 + piece_last[p - 1] + 1)
                srcap = bass.AP(b.wqkv_bf, cb, [[6144, P], [6144 * P, NCH], [1, 256]])
                b.wdma(gq, p % 2, b.wstr[p % 2].ap().rearrange("p (c n) -> p c n", c=NCH), srcap)

        def t_body(t):
            for w, (p, mm, _) in enumerate(waves):
                if w == piece_first[p]:
                    t.wait_ge(b.sem_w[p % 2], piece_cnt[p])
                if w >= 2:
                    t.wait_ge(b.sem_s, s0 + w - 1)
                ins = mm(t, bsets[w % 2])
                b.c_t += 1
                ins.then_inc(b.sem_t, 1)

        def s_body(s):
            for w, (p, _, ev) in enumerate(waves):
                s.wait_ge(b.sem_t, t0 + w + 1)
                ins = ev(s, bsets[w % 2])
                b.c_s += 1
                ins.then_inc(b.sem_s, 1)

        self.block(sync=g_body, tensor=t_body, scalar=s_body)
        if g == 0:
            self.tap("k_g0", kg, P, HG * TOKS)
            self.tap("v_g0", vg, P, HG * TOKS)
            self.tap("q_g0", qg, P, HG * TOKS)

    def emit_amath(self, g, gb):
        b = self
        kg, vg = b.k_g[gb], b.v_g[gb]
        Sg = b.S_g[gb]
        ubank = [b.pU, b.pA]
        TTR = [0] * HG
        TU = [0] * HG
        VKS = [0] * HG
        VSC = [0] * HG
        ct = self.c_t
        for hi in range(HG):
            ct += 1; TTR[hi] = ct
            if hi >= 1:
                ct += 1; TU[hi - 1] = ct
        ct += 1; TU[HG - 1] = ct
        cv = self.c_v
        for hi in range(HG):
            cv += 1; VKS[hi] = cv
            if hi >= 1:
                cv += 1; VSC[hi - 1] = cv
        cv += 1; VSC[HG - 1] = cv

        def emit_tr(t, hi):
            for tt in range(8):
                ins = t.transpose(b.pT[:, tt * P:(tt + 1) * P],
                                  kg[:, hi * TOKS + tt * P:hi * TOKS + (tt + 1) * P],
                                  b.idBF[:, :])
            return ins

        def emit_u(t, hi):
            for j in range(NBLK):
                for mc in range(2):
                    tt = 2 * j + mc
                    ins = t.matmul(ubank[hi % 2][:, j * DH:(j + 1) * DH],
                                   b.ks_h[hi % 2][:, tt * P:(tt + 1) * P],
                                   vg[:, tt * 512 + hi * DH:tt * 512 + (hi + 1) * DH],
                                   start=(mc == 0), stop=(mc == 1))
            return ins

        def t_body(t):
            for hi in range(HG):
                if hi >= 1:
                    t.wait_ge(b.sem_v, VKS[hi - 1])
                ins = emit_tr(t, hi)
                ins.then_inc(b.sem_t, 1)
                if hi >= 1:
                    ins = emit_u(t, hi - 1)
                    ins.then_inc(b.sem_t, 1)
            t.wait_ge(b.sem_v, VKS[HG - 1])
            ins = emit_u(t, HG - 1)
            ins.then_inc(b.sem_t, 1)
            b.c_t = ct

        def emit_ks(v, hi):
            hh = g * HG + hi
            for tt in range(8):
                col = (tt % 2) * 16 + hh
                ins = v.tensor_scalar(b.ks_h[hi % 2][:, tt * P:(tt + 1) * P],
                                      b.pT[:, tt * P:(tt + 1) * P],
                                      b.kd_colsT[:, col:col + 1], None, ALU.mult)
            return ins

        def emit_scan(v, hi):
            hh = g * HG + hi
            base = hi * 4 * DH
            v.tensor_copy(Sg[:, base:base + DH], ubank[hi % 2][:, 0:DH])
            for j in range(1, NBLK):
                ins = v.scalar_tensor_tensor(Sg[:, base + j * DH:base + (j + 1) * DH],
                                             Sg[:, base + (j - 1) * DH:base + j * DH],
                                             b.bdp_b[:, hh * NBLK + 1:hh * NBLK + 2],
                                             ubank[hi % 2][:, j * DH:(j + 1) * DH],
                                             ALU.mult, ALU.add)
            return ins

        def v_body(v):
            for hi in range(HG):
                v.wait_ge(b.sem_t, TTR[hi])
                ins = emit_ks(v, hi)
                ins.then_inc(b.sem_v, 1)
                if hi >= 1:
                    v.wait_ge(b.sem_t, TU[hi - 1])
                    ins = emit_scan(v, hi - 1)
                    ins.then_inc(b.sem_v, 1)
            v.wait_ge(b.sem_t, TU[HG - 1])
            ins = emit_scan(v, HG - 1)
            ins.then_inc(b.sem_v, 1)
            b.c_v = cv

        def g_body(gq):
            for hi in range(HG):
                gq.wait_ge(b.sem_v, VSC[hi])
                base = hi * 4 * DH
                b.gdma(gq, b.cc_in[g][hi * P:(hi + 1) * P, :],
                       Sg[:, base + 3 * DH:base + 4 * DH])
            gq.wait_ge(b.sem_g, b.exp_g)

        self.block(tensor=t_body, vector=v_body, gpsimd=g_body)
        if g == 0:
            self.tap("S_g0", Sg, P, HG * 4 * DH)

    def emit_cc_issue(self, g):
        b = self

        def g_cc(gq):
            gq.collective_compute(
                "AllGather", ALU.bypass,
                replica_groups=[[0, 1], [2, 3], [4, 5], [6, 7]],
                ins=[b.cc_in[g].ap()], outs=[b.cc_out[g].ap()],
            ).then_inc(b.sem_cc)
            b.exp_cc += 1
        self.block(gpsimd=g_cc)

    def emit_cc_fixup(self, g, gb):
        b = self
        Sg, Tg, kvi = b.S_g[gb], b.T_g[gb], b.kvi_g[gb]

        def g_cc(gq):
            gq.wait_ge(b.sem_cc, g + 1)
            b.gdma(gq, b.gtmp.ap().rearrange("p (h e) -> p h e", h=HG),
                   bass.AP(b.cc_out[g], 0, [[DH, P], [P * DH, HG], [1, DH]]))
            gq.wait_ge(b.sem_g, b.exp_g)
        self.block(gpsimd=g_cc)

        def v_kvi(v):
            v.tensor_scalar(kvi[:, :], b.gtmp[:, :], b.flag_c[:, 0:1], None, ALU.mult)
        self.block(vector=v_kvi)

        def v_fix(v):
            for hi in range(HG):
                hh = g * HG + hi
                base = hi * 4 * DH
                v.tensor_copy(Tg[:, base:base + DH], kvi[:, hi * DH:(hi + 1) * DH])
                for j in range(1, NBLK):
                    v.scalar_tensor_tensor(Tg[:, base + j * DH:base + (j + 1) * DH],
                                           kvi[:, hi * DH:(hi + 1) * DH],
                                           b.bdp_b[:, hh * NBLK + j:hh * NBLK + j + 1],
                                           Sg[:, base + (j - 1) * DH:base + j * DH],
                                           ALU.mult, ALU.add)
        self.block(vector=v_fix)
        if g == 0:
            self.tap("T_g0", Tg, P, HG * 4 * DH)

    def emit_bmath(self, g, gb):
        b = self
        kg, vg, qg = b.k_g[gb], b.v_g[gb], b.q_g[gb]
        Tg = b.T_g[gb]

        def g_qd(gq):
            b.gdma(gq, b.qdB_g.ap().rearrange("p (h l) -> p h l", h=HG),
                   bass.AP(b.qd_dram, g * HG * BLK, [[0, P], [BLK, HG], [1, BLK]]))
            gq.wait_ge(b.sem_g, b.exp_g)
        self.block(gpsimd=g_qd)

        def v_d(v):
            for hi in range(HG):
                hh = g * HG + hi
                v.tensor_scalar(b.D_g[:, hi * 512:(hi + 1) * 512], b.idx_c[:, :],
                                b.s_bc[:, hh:hh + 1], None, ALU.mult)
        self.block(vector=v_d)

        def s_d(s):
            for hi in range(HG):
                s.activation(b.D_g[:, hi * 512:(hi + 1) * 512],
                             b.D_g[:, hi * 512:(hi + 1) * 512], AF.Exp)
        self.block(scalar=s_d)
        if g == 0:
            self.tap("D_g0", b.D_g, P, HG * 512)
            self.tap("qdB_g0", b.qdB_g, P, HG * BLK)

        NI = HG * NBLK
        qkbank = [b.pQK, b.pU]
        obank = [b.pA, b.pO]
        TQK = [0] * NI
        TO = [0] * NI
        TV = [0] * NI
        VOE = [0] * NI
        VQK = [0] * NI
        SQ = [0] * NI
        ct = self.c_t
        for i in range(NI):
            ct += 1; TQK[i] = ct
            if i >= 1:
                ct += 1; TO[i - 1] = ct
            if i >= 4:
                ct += 1; TV[i - 4] = ct
        ct += 1; TO[NI - 1] = ct
        for j in range(NI - 4, NI):
            ct += 1; TV[j] = ct
        cv = self.c_v
        for i in range(NI):
            if i >= 1:
                cv += 1; VOE[i - 1] = cv
            cv += 1; VQK[i] = cv
        cv += 1; VOE[NI - 1] = cv
        cs = self.c_s
        for j in range(NI):
            cs += 1; SQ[j] = cs

        def mm_qk(t, i):
            hi, blk = i // NBLK, i % NBLK
            for mc in range(2):
                tt = 2 * blk + mc
                ins = t.matmul(qkbank[i % 2][:, mc * 256:(mc + 1) * 256],
                               kg[:, hi * TOKS + tt * P:hi * TOKS + (tt + 1) * P],
                               qg[:, hi * TOKS + blk * BLK:hi * TOKS + (blk + 1) * BLK],
                               start=True, stop=True)
            return ins

        def mm_o(t, j):
            hi, blk = j // NBLK, j % NBLK
            for mc in range(2):
                tt = 2 * blk + mc
                t.matmul(obank[j % 2][:, 0:256],
                         vg[:, tt * 512 + hi * DH:tt * 512 + (hi + 1) * DH],
                         b.qkD[j % 2][:, mc * 256:(mc + 1) * 256],
                         start=(mc == 0), stop=False)
            base = hi * 4 * DH
            return t.matmul(obank[j % 2][:, 0:256],
                            Tg[:, base + blk * DH:base + (blk + 1) * DH],
                            b.qs_b[j % 2][:, :], start=False, stop=True)

        def mm_var(t, j):
            hi, blk = j // NBLK, j % NBLK
            hh = g * HG + hi
            return t.matmul(b.pV0[0:1, blk * BLK:(blk + 1) * BLK],
                            b.onesP_bf[:, :], b.osq[j % 2][:, :],
                            start=(hh == 0 and blk in (0, 2)),
                            stop=(hh == NH - 1 and blk in (1, 3)),
                            skip_group_check=True)

        def t_body(t):
            def do_o(j):
                t.wait_ge(b.sem_v, VQK[j])
                if j >= 2:
                    t.wait_ge(b.sem_v, VOE[j - 2])
                mm_o(t, j).then_inc(b.sem_t, 1)

            def do_var(j):
                t.wait_ge(b.sem_s, SQ[j])
                mm_var(t, j).then_inc(b.sem_t, 1)

            for i in range(NI):
                if i >= 2:
                    t.wait_ge(b.sem_v, VQK[i - 2])
                mm_qk(t, i).then_inc(b.sem_t, 1)
                if i >= 1:
                    do_o(i - 1)
                if i >= 4:
                    do_var(i - 4)
            do_o(NI - 1)
            for j in range(NI - 4, NI):
                do_var(j)
            b.c_t = ct

        def v_body(v):
            def do_oe(j):
                hi, blk = j // NBLK, j % NBLK
                hh = g * HG + hi
                v.wait_ge(b.sem_t, TO[j])
                ins = v.tensor_copy(
                    b.o_bf[:, hh * TOKS + blk * BLK:hh * TOKS + (blk + 1) * BLK],
                    obank[j % 2][:, 0:256])
                ins.then_inc(b.sem_v, 1)

            for i in range(NI):
                if i >= 1:
                    do_oe(i - 1)
                hi, blk = i // NBLK, i % NBLK
                v.wait_ge(b.sem_t, TQK[i])
                v.tensor_tensor(b.qkD[i % 2][:, :], qkbank[i % 2][:, :],
                                b.D_g[:, hi * 512:(hi + 1) * 512], ALU.mult)
                ins = v.tensor_tensor(
                    b.qs_b[i % 2][:, :],
                    qg[:, hi * TOKS + blk * BLK:hi * TOKS + (blk + 1) * BLK],
                    b.qdB_g[:, hi * BLK:(hi + 1) * BLK], ALU.mult)
                ins.then_inc(b.sem_v, 1)
            do_oe(NI - 1)
            b.c_v = cv

        def s_body(s):
            for j in range(NI):
                hi, blk = j // NBLK, j % NBLK
                hh = g * HG + hi
                s.wait_ge(b.sem_v, VOE[j])
                if j >= 2:
                    s.wait_ge(b.sem_t, TV[j - 2])
                ins = s.activation(
                    b.osq[j % 2][:, :],
                    b.o_bf[:, hh * TOKS + blk * BLK:hh * TOKS + (blk + 1) * BLK],
                    AF.Square)
                ins.then_inc(b.sem_s, 1)
            b.c_s = cs

        self.block(tensor=t_body, vector=v_body, scalar=s_body)

    def emit_gate(self):
        b = self
        bset = [(b.pA, b.pB), (b.pQK, b.pO)]
        t0, s0, v0 = self.c_t, self.c_s, self.c_v
        piece_cnt = [self.exp_w[p % 2] + 16 * (p // 2 + 1) for p in range(8)]

        def g_body(gq):
            gq.wait_ge(b.sem_cast, 48)
            for p in range(8):
                if p >= 2:
                    gq.wait_ge(b.sem_t, t0 + 2 * (p - 1))
                srcap = bass.AP(b.wg_bf, p * 256, [[2048, P], [2048 * P, NCH], [1, 256]])
                b.wdma(gq, p % 2, b.gst[p % 2].ap().rearrange("p (c n) -> p c n", c=NCH), srcap)

        def t_body(t):
            for oc in range(NCH):
                p, oc2 = oc // 2, oc % 2
                ws = b.gst[p % 2]
                if oc2 == 0:
                    t.wait_ge(b.sem_w[p % 2], piece_cnt[p])
                if oc >= 2:
                    t.wait_ge(b.sem_s, s0 + oc - 1)
                for tt in range(2):
                    bank = bset[oc % 2][tt]
                    for c in range(NCH):
                        ins = t.matmul(bank[:, :],
                                       ws[:, c * 256 + oc2 * P:c * 256 + (oc2 + 1) * P],
                                       b.xn[:, c * TOKS + tt * 512:c * TOKS + (tt + 1) * 512],
                                       start=(c == 0), stop=(c == NCH - 1))
                b.c_t += 1
                ins.then_inc(b.sem_t, 1)

        def s_body(s):
            for oc in range(NCH):
                s.wait_ge(b.sem_t, t0 + oc + 1)
                if oc >= 2:
                    s.wait_ge(b.sem_v, v0 + oc - 1)
                for tt in range(2):
                    ins = s.activation(b.sg_b[oc % 2][:, tt * 512:(tt + 1) * 512],
                                       bset[oc % 2][tt][:, :], AF.Sigmoid)
                b.c_s += 1
                ins.then_inc(b.sem_s, 1)

        def v_body(v):
            for oc in range(NCH):
                v.wait_ge(b.sem_s, s0 + oc + 1)
                for tt in range(2):
                    v.scalar_tensor_tensor(
                        b.t2_b[0][:, tt * 512:(tt + 1) * 512],
                        b.sg_b[oc % 2][:, tt * 512:(tt + 1) * 512], b.rmsw[:, oc:oc + 1],
                        b.o_bf[:, oc * TOKS + tt * 512:oc * TOKS + (tt + 1) * 512],
                        ALU.mult, ALU.mult)
                for tt in range(2):
                    ins = v.tensor_tensor(
                        b.o_bf[:, oc * TOKS + tt * 512:oc * TOKS + (tt + 1) * 512],
                        b.t2_b[0][:, tt * 512:(tt + 1) * 512],
                        b.rstd_sb[:, tt * 512:(tt + 1) * 512], ALU.mult)
                b.c_v += 1
                ins.then_inc(b.sem_v, 1)

        self.block(sync=g_body, tensor=t_body, scalar=s_body, vector=v_body)

    def emit_wout(self, restore_cnt=0):
        b = self
        bset = [(b.pA, b.pB), (b.pQK, b.pO)]
        t0, v0 = self.c_t, self.c_v
        piece_cnt = [self.exp_w[p % 2] + 16 * (p // 2 + 1) for p in range(8)]

        def g_body(gq):
            gq.wait_ge(b.sem_cast, 48)
            for p in range(8):
                if p >= 2:
                    gq.wait_ge(b.sem_t, t0 + 2 * (p - 1))
                srcap = bass.AP(b.wo_bf, p * 256, [[2048, P], [2048 * P, NCH], [1, 256]])
                b.wdma(gq, p % 2, b.gst[p % 2].ap().rearrange("p (c n) -> p c n", c=NCH), srcap)

        def t_body(t):
            for oc in range(NCH):
                p, oc2 = oc // 2, oc % 2
                ws = b.gst[p % 2]
                if oc2 == 0:
                    t.wait_ge(b.sem_w[p % 2], piece_cnt[p])
                if oc >= 2:
                    t.wait_ge(b.sem_v, v0 + oc - 1)
                for tt in range(2):
                    bank = bset[oc % 2][tt]
                    for c in range(NCH):
                        ins = t.matmul(bank[:, :],
                                       ws[:, c * 256 + oc2 * P:c * 256 + (oc2 + 1) * P],
                                       b.o_bf[:, c * TOKS + tt * 512:c * TOKS + (tt + 1) * 512],
                                       start=(c == 0), stop=(c == NCH - 1))
                b.c_t += 1
                ins.then_inc(b.sem_t, 1)

        def v_body(v):
            for oc in range(NCH):
                v.wait_ge(b.sem_t, t0 + oc + 1)
                if oc == 0 and restore_cnt:
                    v.wait_ge(b.sem_y, restore_cnt)
                new = b.xslot[b.spare]
                old = b.xs(oc)
                for tt in range(2):
                    bank = bset[oc % 2][tt]
                    ins = v.tensor_tensor(new[:, tt * 512:(tt + 1) * 512], bank[:, :],
                                          old[:, tt * 512:(tt + 1) * 512], ALU.add)
                b.rotate(oc)
                b.c_v += 1
                ins.then_inc(b.sem_v, 1)

        self.block(sync=g_body, tensor=t_body, vector=v_body)

    def emit_store(self):
        b = self

        def y_out(y):
            for c in range(NCH):
                self.ydma(y, b.out_t[c * P:(c + 1) * P, :], b.xs(c)[:, :])
            y.wait_ge(b.sem_y, b.exp_y)
        self.block(sync=y_out)

    def build(self, upto="all"):
        self.emit_setup()
        self.load_ffn_weights()
        self.emit_ln("ffn1")
        if upto == "ln1":
            self.emit_store(); self.es.close(); return self.nc
        self.emit_ffn("ffn1")
        self.tap("x1", self.xs(0), P, TOKS)
        if upto == "ffn1":
            self.emit_store(); self.es.close(); return self.nc
        self.emit_attn()
        if upto == "attn":
            self.emit_store(); self.es.close(); return self.nc
        self.load_ffn_weights()
        self.emit_ln("ffn2")
        self.emit_ffn("ffn2")
        self.es.close()
        return self.nc



def _host_consts():
    m = np.arange(BLK)
    l = np.arange(BLK)
    idx = (m[:, None] - l[None, :]).astype(np.float32)
    idx[m[:, None] > l[None, :]] = NEG_BIG
    idx_c = np.concatenate([idx[0:128, :], idx[128:256, :]], axis=1).astype(np.float32)
    gq = np.broadcast_to(-(l + 1).astype(np.float32), (NH, BLK)).copy()
    gk = np.broadcast_to(-(255 - m).astype(np.float32), (NH, BLK)).copy()
    gbd = np.broadcast_to((-256.0 * np.arange(NBLK)).astype(np.float32), (NH, NBLK)).copy()
    eexp = np.zeros((8, 512), np.float32)
    for j in range(512):
        eexp[j // 64, j] = 1.0
    return idx_c, gq, gk, gbd, eexp


def build_inputs(inputs):
    x = np.asarray(inputs["L0_attn_hidden_states"], np.float32)
    wqkv = np.ascontiguousarray(np.asarray(inputs["L0_attn_w_qkv"], np.float32))
    wgate = np.ascontiguousarray(np.asarray(inputs["L0_attn_w_gate"], np.float32))
    woutw = np.ascontiguousarray(np.asarray(inputs["L0_attn_w_out"], np.float32))
    rmsw = np.asarray(inputs["L0_attn_rms_weight"], np.float32)
    slope = np.asarray(inputs["L0_attn_slope_rate"], np.float32).reshape(NH)
    A = np.asarray(inputs["L0_ffn_A"], np.float32)
    B = np.asarray(inputs["L0_ffn_B"], np.float32)
    Wr = np.asarray(inputs["L0_ffn_Wr"], np.float32)
    br = np.asarray(inputs["L0_ffn_br"], np.float32)

    a_t = np.ascontiguousarray(A.reshape(512, 2048).T)
    b_t = np.ascontiguousarray(B.transpose(0, 2, 1).reshape(512, 2048))
    wr_t = np.ascontiguousarray(Wr.T)
    rmsw_fm = np.ascontiguousarray(rmsw.reshape(NCH, P).T)
    idx_c, gq, gk, gbd, eexp = _host_consts()

    common = dict(
        wqkv=wqkv, wgate=wgate, woutw=woutw, a_t=a_t, b_t=b_t, wr_t=wr_t,
        br_in=np.ascontiguousarray(br.reshape(1, 8)), rmsw_in=rmsw_fm,
        s_col_in=np.ascontiguousarray(slope.reshape(NH, 1)),
        idx_in=idx_c, gq_in=gq, gk_in=gk, gbd_in=gbd, eexp_in=eexp,
    )
    in_maps = []
    for c in range(8):
        bidx, half = c // 2, c % 2
        xc = np.ascontiguousarray(x[bidx, half * TOKS:(half + 1) * TOKS, :].T)
        flag = np.full((P, 1), 1.0 if half == 1 else 0.0, np.float32)
        m = dict(common)
        m["x_in"] = xc
        m["flag_in"] = flag
        in_maps.append(m)
    return in_maps


def kernel(**inputs):
    in_maps = build_inputs(inputs)
    bld = Builder()
    nc = bld.build()
    res = run_bass_kernel_spmd(nc, in_maps, core_ids=list(range(8)))
    out = np.empty((4, 2048, 2048), np.float32)
    for c in range(8):
        bidx, half = c // 2, c % 2
        out[bidx, half * TOKS:(half + 1) * TOKS, :] = res.results[c]["out_t"].T
    return out


# version 08
# speedup vs baseline: 1.1484x; 1.1435x over previous
import numpy as np
from contextlib import ExitStack

import concourse.bass as bass
import concourse.mybir as mybir
from concourse.bass_utils import run_bass_kernel_spmd

F32 = mybir.dt.float32
BF16 = mybir.dt.bfloat16
AF = mybir.ActivationFunctionType
ALU = mybir.AluOpType
AX = mybir.AxisListType.X

P = 128
TOKS = 1024
NCH = 16
NH = 16
HG = 4
NG = NH // HG
BLK = 256
NBLK = TOKS // BLK
DH = 128
FFN_SCALE = 0.5 / 64.0
NEG_BIG = -1.0e9

SB0 = 16512
X_BASE = SB0
X_SLOT = 4096
NSLOT = 17
XN_BASE = SB0 + 69632
OB_BASE = SB0 + 102400
WK_BASE = SB0 + 135168
CONST_BASE = SB0 + 196608

KVQ = [X_BASE, X_BASE + 24576]
WSTR = [X_BASE + 49152, X_BASE + 57344]

WK_S = [WK_BASE, WK_BASE + 8192]
WK_T = [WK_BASE + 16384, WK_BASE + 20480]
WK_KVI = [WK_BASE + 24576, WK_BASE + 26624]
WK_D = WK_BASE + 28672
WK_QDB = WK_BASE + 36864
WK_KS = [WK_BASE + 40960, WK_BASE + 43008]
WK_QS = [WK_BASE + 45056, WK_BASE + 45568]
WK_QKD = [WK_BASE + 46080, WK_BASE + 47104]
WK_GTMP = WK_BASE + 48128
WK_OSQ = [WK_BASE + 50176, WK_BASE + 50688]

WK_ABF = WK_BASE
WK_BBF = WK_BASE + 16384
WK_WRBF = WK_BASE + 32768
WK_XB = WK_BASE + 33280
WK_SQ = WK_BASE + 41472
WK_TMP = WK_BASE + 49664
WK_ROWA = WK_BASE + 53760
WK_LGS = WK_BASE + 57856
WK_EXB = WK_BASE + 58112
WK_OHB = WK_BASE + 58368
WK_MXB = WK_BASE + 58624
WK_NMX = WK_BASE + 58656
WK_SEB = WK_BASE + 58688
WK_WB = WK_BASE + 58720
WK_GST = [WK_BASE, WK_BASE + 8192]
WK_SG = [WK_BASE + 16384, WK_BASE + 18432]
WK_T2 = [WK_BASE + 20480, WK_BASE + 22528]

OB_MASKB = OB_BASE
OB_HM = OB_BASE + 16384
OB_MWTM = OB_BASE + 24576
OB_MWFM = OB_BASE + 24832

_CO = [CONST_BASE]


def _co(nbytes, align=64):
    _CO[0] = (_CO[0] + align - 1) // align * align
    off = _CO[0]
    _CO[0] += nbytes
    return off


class Builder:
    def __init__(self, taps=(), precast=False, cc_mode="block"):
        self.precast = precast
        self.cc_mode = cc_mode
        self.taps = set(taps)
        _CO[0] = CONST_BASE
        nc = bass.Bass("TRN2", target_bir_lowering=False)
        self.nc = nc
        self.es = ExitStack()

        dt = nc.dram_tensor
        self.x_in = dt("x_in", [2048, TOKS], F32, kind="ExternalInput")
        self.wqkv = dt("wqkv", [2048, 6144], F32, kind="ExternalInput")
        self.wgate = dt("wgate", [2048, 2048], F32, kind="ExternalInput")
        self.woutw = dt("woutw", [2048, 2048], F32, kind="ExternalInput")
        self.a_t = dt("a_t", [2048, 512], F32, kind="ExternalInput")
        self.b_t = dt("b_t", [512, 2048], F32, kind="ExternalInput")
        self.wr_t = dt("wr_t", [2048, 8], F32, kind="ExternalInput")
        self.br_in = dt("br_in", [1, 8], F32, kind="ExternalInput")
        self.rmsw_in = dt("rmsw_in", [P, NCH], F32, kind="ExternalInput")
        self.s_col_in = dt("s_col_in", [NH, 1], F32, kind="ExternalInput")
        self.flag_in = dt("flag_in", [P, 1], F32, kind="ExternalInput")
        self.idx_in = dt("idx_in", [P, 512], F32, kind="ExternalInput")
        self.gq_in = dt("gq_in", [NH, BLK], F32, kind="ExternalInput")
        self.gk_in = dt("gk_in", [NH, BLK], F32, kind="ExternalInput")
        self.gbd_in = dt("gbd_in", [NH, NBLK], F32, kind="ExternalInput")
        self.eexp_in = dt("eexp_in", [8, 512], F32, kind="ExternalInput")
        self.out_t = dt("out_t", [2048, TOKS], F32, kind="ExternalOutput")
        self.xspill = dt("xspill", [P, NSLOT * TOKS], F32)
        self.wqkv_bf = dt("wqkv_bf", [2048, 6144], BF16)
        self.wg_bf = dt("wg_bf", [2048, 2048], BF16)
        self.wo_bf = dt("wo_bf", [2048, 2048], BF16)
        self.qd_dram = dt("qd_dram", [NH, BLK], F32)
        self.bdp_dram = dt("bdp_dram", [NH, NBLK], F32)
        self.cc_in = [dt(f"cc_in{g}", [HG * P, DH], F32) for g in range(NG)]
        self.cc_out = [dt(f"cc_out{g}", [2 * HG * P, DH], F32) for g in range(NG)]

        at = nc.alloc_sbuf_tensor_at
        self.xslot = [at(f"xslot{i}", [P, TOKS], F32, offset=X_BASE + i * X_SLOT)
                      for i in range(NSLOT)]
        self.xall = at("xall", [P, NSLOT * TOKS], F32, offset=X_BASE)
        self.slotmap = list(range(NCH))
        self.spare = NCH
        self.xn = at("xn", [P, NCH * TOKS], BF16, offset=XN_BASE)
        self.o_bf = at("o_bf", [P, NH * TOKS], BF16, offset=OB_BASE)
        self.k_g = [at(f"k_g{i}", [P, HG * TOKS], BF16, offset=KVQ[i]) for i in range(2)]
        self.v_g = [at(f"v_g{i}", [P, HG * TOKS], BF16, offset=KVQ[i] + 8192)
                    for i in range(2)]
        self.q_g = [at(f"q_g{i}", [P, HG * TOKS], BF16, offset=KVQ[i] + 16384)
                    for i in range(2)]
        self.wstr = [at(f"wstr{i}", [P, NCH * 256], BF16, offset=WSTR[i]) for i in range(2)]
        self.S_g = [at(f"S_g{i}", [P, HG * 4 * DH], F32, offset=WK_S[i]) for i in range(2)]
        self.T_g = [at(f"T_g{i}", [P, HG * 4 * DH], BF16, offset=WK_T[i]) for i in range(2)]
        self.kvi_g = [at(f"kvi_g{i}", [P, HG * DH], F32, offset=WK_KVI[i]) for i in range(2)]
        self.D_g = at("D_g", [P, HG * 512], F32, offset=WK_D)
        self.qdB_g = at("qdB_g", [P, HG * BLK], F32, offset=WK_QDB)
        self.ks_h = [at(f"ks_h{i}", [P, TOKS], BF16, offset=WK_KS[i]) for i in range(2)]
        self.qs_b = [at(f"qs_b{i}", [P, BLK], BF16, offset=WK_QS[i]) for i in range(2)]
        self.qkD = [at(f"qkD{i}", [P, 512], BF16, offset=WK_QKD[i]) for i in range(2)]
        self.gtmp = at("gtmp", [P, 512], F32, offset=WK_GTMP)
        self.osq = [at(f"osq{i}", [P, BLK], BF16, offset=WK_OSQ[i]) for i in range(2)]
        self.a_bf = at("a_bf", [P, NCH * 512], BF16, offset=WK_ABF)
        self.b_bf = at("b_bf", [P, 4 * 2048], BF16, offset=WK_BBF)
        self.wr_bf = at("wr_bf", [P, NCH * 8], BF16, offset=WK_WRBF)
        self.xb = at("xb", [P, 4 * TOKS], BF16, offset=WK_XB)
        self.sq = at("sq", [P, 4 * TOKS], BF16, offset=WK_SQ)
        self.tmp = at("tmp", [P, TOKS], F32, offset=WK_TMP)
        self.rows_a = at("rows_a", [P, TOKS], F32, offset=WK_ROWA)
        self.rstd_sb = at("rstd_sb", [P, TOKS], F32, offset=WK_XB)
        self.nm_r = at("nm_r", [1, TOKS], F32, offset=WK_XB + 4096)
        self.m2_r = at("m2_r", [1, TOKS], F32, offset=WK_TMP)
        self.rstd_r = at("rstd_r", [1, TOKS], F32, offset=WK_SQ)
        self.nmr_r = at("nmr_r", [1, TOKS], F32, offset=WK_SQ + 4096)
        self.lgs = at("lgs", [P, 64], F32, offset=WK_LGS)
        self.exb = at("exb", [P, 64], F32, offset=WK_EXB)
        self.ohb = at("ohb", [P, 64], F32, offset=WK_OHB)
        self.mxb = at("mxb", [P, 8], F32, offset=WK_MXB)
        self.nmxb = at("nmxb", [P, 8], F32, offset=WK_NMX)
        self.seb = at("seb", [P, 8], F32, offset=WK_SEB)
        self.wb = at("wb", [P, 8], F32, offset=WK_WB)
        self.mask_b = at("mask_b", [P, 4 * TOKS], F32, offset=OB_MASKB)
        self.hm = at("hm", [P, 4 * TOKS], BF16, offset=OB_HM)
        self.mw_tm = at("mw_tm", [P, 64], F32, offset=OB_MWTM)
        self.mw_fm = at("mw_fm", [8, TOKS], F32, offset=OB_MWFM)
        self.gst = [at(f"gst{i}", [P, NCH * 256], BF16, offset=WK_GST[i]) for i in range(2)]
        self.sg_b = [at(f"sg_b{i}", [P, TOKS], F32, offset=WK_BASE + 16384 + i * 4096)
                     for i in range(2)]
        self.t2_b = [at(f"t2_b{i}", [P, TOKS], F32, offset=WK_BASE + 24576 + i * 4096)
                     for i in range(2)]
        self.idBF = at("idBF", [P, P], BF16, offset=_co(256))
        self.idF32 = at("idF32", [P, P], F32, offset=_co(512))
        self.ones_row_bf = at("ones_row_bf", [1, P], BF16, offset=_co(256))
        self.ones_row_f = at("ones_row_f", [1, P], F32, offset=_co(512))
        self.onesP_bf = at("onesP_bf", [P, 1], BF16, offset=_co(32))
        self.eps_ln = at("eps_ln", [1, 1], F32, offset=_co(32))
        self.eps_rms = at("eps_rms", [1, 1], F32, offset=_co(32))
        self.s_col = at("s_col", [NH, 1], F32, offset=_co(32))
        self.gq = at("gq", [NH, BLK], F32, offset=_co(1024))
        self.gk = at("gk", [NH, BLK], F32, offset=_co(1024))
        self.gbd = at("gbd", [NH, NBLK], F32, offset=_co(32))
        self.qd16 = at("qd16", [NH, BLK], F32, offset=_co(1024))
        self.kd16 = at("kd16", [NH, BLK], F32, offset=_co(1024))
        self.bdp16 = at("bdp16", [NH, NBLK], F32, offset=_co(32))
        self.idx_c = at("idx_c", [P, 512], F32, offset=_co(2048))
        self.eexp = at("eexp", [8, 512], F32, offset=_co(2048))
        self.br_b = at("br_b", [P, 8], F32, offset=_co(32))
        self.rmsw = at("rmsw", [P, NCH], F32, offset=_co(64))
        self.s_bc = at("s_bc", [P, NH], F32, offset=_co(64))
        self.bdp_b = at("bdp_b", [P, NH * NBLK], F32, offset=_co(256))
        self.kd_colsT = at("kd_colsT", [P, 32], F32, offset=_co(128))
        self.flag_c = at("flag_c", [P, 1], F32, offset=_co(32))
        assert _CO[0] <= 229000, _CO[0]

        ec = self.es.enter_context
        self.pA = ec(nc.psum_tensor("pA", [P, 512], F32))
        self.pB = ec(nc.psum_tensor("pB", [P, 512], F32))
        self.pQK = ec(nc.psum_tensor("pQK", [P, 512], F32))
        self.pU = ec(nc.psum_tensor("pU", [P, 512], F32))
        self.pO = ec(nc.psum_tensor("pO", [P, 512], F32))
        self.pT = ec(nc.psum_tensor("pT", [P, 1024], BF16))
        self.pV0 = ec(nc.psum_tensor("pV0", [1, 1024], F32))

        self.sem_g = ec(nc.semaphore("sem_g"))
        self.sem_y = ec(nc.semaphore("sem_y"))
        self.sem_cc = ec(nc.semaphore("sem_cc"))
        self.sem_cast = ec(nc.semaphore("sem_cast"))
        self.sem_w = [ec(nc.semaphore("sem_w0")), ec(nc.semaphore("sem_w1"))]
        self.exp_w = [0, 0]
        self.sem_t = ec(nc.semaphore("sem_t"))
        self.sem_v = ec(nc.semaphore("sem_v"))
        self.sem_s = ec(nc.semaphore("sem_s"))
        self.exp_g = 0
        self.exp_y = 0
        self.exp_cc = 0
        self.c_t = 0
        self.c_v = 0
        self.c_s = 0

    def block(self, **bodies):
        with self.nc.Block() as b:
            for eng in ("sync", "gpsimd", "tensor", "vector", "scalar"):
                if eng in bodies:
                    getattr(b, eng)(bodies[eng])

    def gdma(self, g, out, in_):
        g.dma_start(out=out, in_=in_).then_inc(self.sem_g, 16)
        self.exp_g += 16

    def wdma(self, g, slot, out, in_):
        g.dma_start(out=out, in_=in_).then_inc(self.sem_w[slot], 16)
        self.exp_w[slot] += 16
        return self.exp_w[slot]

    def xall16(self):
        return self.xall[:, 0:NCH * TOKS].rearrange("p (c t) -> p c t", c=NCH)

    def ydma(self, y, out, in_):
        y.dma_start(out=out, in_=in_).then_inc(self.sem_y, 16)
        self.exp_y += 16

    def tap(self, name, tensor, rows, cols):
        if name not in self.taps:
            return
        t = self.nc.dram_tensor(f"tap_{name}", [rows, cols], tensor.dtype,
                                kind="ExternalOutput")

        def _s(y):
            self.ydma(y, t[:, :], tensor[0:rows, 0:cols])
            y.wait_ge(self.sem_y, self.exp_y)
        self.block(sync=_s)

    def mark_t(self, ins):
        ins.then_inc(self.sem_t, 1)
        self.c_t += 1
        return self.c_t

    def mark_v(self, ins):
        ins.then_inc(self.sem_v, 1)
        self.c_v += 1
        return self.c_v

    def mark_s(self, ins):
        ins.then_inc(self.sem_s, 1)
        self.c_s += 1
        return self.c_s

    def xs(self, c):
        return self.xslot[self.slotmap[c]]

    def rotate(self, c):
        old = self.slotmap[c]
        self.slotmap[c] = self.spare
        self.spare = old

    def emit_setup(self):
        b = self

        def g_pre(g):
            g.memset(b.idBF[:, :], 1.0)
            g.memset(b.idF32[:, :], 1.0)
        self.block(gpsimd=g_pre)

        def g_body(g):
            g.affine_select(b.idBF[:, :], b.idBF[:, :], [[-1, P]], ALU.is_equal, 0.0,
                            base=0, channel_multiplier=1)
            g.affine_select(b.idF32[:, :], b.idF32[:, :], [[-1, P]], ALU.is_equal, 0.0,
                            base=0, channel_multiplier=1)
            g.memset(b.ones_row_bf[:, :], 1.0)
            g.memset(b.ones_row_f[:, :], 1.0)
            g.memset(b.onesP_bf[:, :], 1.0)
            g.memset(b.eps_ln[:, :], 1e-5)
            g.memset(b.eps_rms[:, :], 1e-6)
            b.gdma(g, b.s_col[:, :], b.s_col_in[:, :])
            b.gdma(g, b.gq[:, :], b.gq_in[:, :])
            b.gdma(g, b.gk[:, :], b.gk_in[:, :])
            b.gdma(g, b.gbd[:, :], b.gbd_in[:, :])
            b.gdma(g, b.idx_c[:, :], b.idx_in[:, :])
            b.gdma(g, b.eexp[:, :], b.eexp_in[:, :])
            b.gdma(g, b.rmsw[:, :], b.rmsw_in[:, :])
            b.gdma(g, b.flag_c[:, :], b.flag_in[:, :])
            b.gdma(g, b.br_b[:, :], bass.AP(b.br_in, 0, [[0, P], [1, 8]]))
            b.gdma(g, b.s_bc[:, :], bass.AP(b.s_col_in, 0, [[0, P], [1, NH]]))
            g.wait_ge(b.sem_g, b.exp_g)
            if b.precast:
                g.dma_start(out=b.wqkv_bf.ap(), in_=b.wqkv.ap()).then_inc(b.sem_cast, 16)
                g.dma_start(out=b.wg_bf.ap(), in_=b.wgate.ap()).then_inc(b.sem_cast, 16)
                g.dma_start(out=b.wo_bf.ap(), in_=b.woutw.ap()).then_inc(b.sem_cast, 16)

        def y_body(y):
            b.ydma(y, b.xall16(),
                   b.x_in.ap().rearrange("(c p) t -> p c t", p=P))
            y.wait_ge(b.sem_y, b.exp_y)

        self.block(gpsimd=g_body, sync=y_body)
        self.cast_cnt = self.exp_g

        def v_body(v):
            v.tensor_scalar(b.qd16[:, :], b.gq[:, :], b.s_col[:, 0:1], None, ALU.mult)
            v.tensor_scalar(b.kd16[:, :], b.gk[:, :], b.s_col[:, 0:1], None, ALU.mult)
            v.tensor_scalar(b.bdp16[:, :], b.gbd[:, :], b.s_col[:, 0:1], None, ALU.mult)
        self.block(vector=v_body)

        def s_body(s):
            s.activation(b.qd16[:, :], b.qd16[:, :], AF.Exp)
            s.activation(b.kd16[:, :], b.kd16[:, :], AF.Exp)
            s.activation(b.bdp16[:, :], b.bdp16[:, :], AF.Exp)
        self.block(scalar=s_body)

        def g2(g):
            b.gdma(g, b.qd_dram[:, :], b.qd16[:, :])
            b.gdma(g, b.bdp_dram[:, :], b.bdp16[:, :])
            g.wait_ge(b.sem_g, b.exp_g)
            b.gdma(g, b.bdp_b[:, :],
                   bass.AP(b.bdp_dram, 0, [[0, P], [NBLK, NH], [1, NBLK]]))
            g.wait_ge(b.sem_g, b.exp_g)
        self.block(gpsimd=g2)

        def t_body(t):
            t.transpose(b.pQK[:, 0:16], b.kd16[:, 0:128], b.idF32[0:16, 0:16])
            t.transpose(b.pQK[:, 16:32], b.kd16[:, 128:256], b.idF32[0:16, 0:16])
        self.block(tensor=t_body)

        def v2(v):
            v.tensor_copy(b.kd_colsT[:, 0:32], b.pQK[:, 0:32])
        self.block(vector=v2)

    def load_ffn_weights(self):
        b = self

        def g_body(g):
            b.gdma(g, b.a_bf.ap().rearrange("p (c r) -> p c r", c=NCH),
                   b.a_t.ap().rearrange("(c p) r -> p c r", p=P))
            b.gdma(g, b.b_bf.ap().rearrange("p (c o) -> p c o", c=4),
                   b.b_t.ap().rearrange("(c p) o -> p c o", p=P))
            b.gdma(g, b.wr_bf.ap().rearrange("p (c e) -> p c e", c=NCH),
                   b.wr_t.ap().rearrange("(c p) e -> p c e", p=P))
            g.wait_ge(b.sem_g, b.exp_g)
        self.block(gpsimd=g_body)

    def emit_ln(self, tag):
        b = self
        nb = 8
        t0, s0, v0 = self.c_t, self.c_s, self.c_v
        xsrc = [b.xs(c) for c in range(NCH)]

        def s_body(s):
            for k in range(nb):
                off = (k % 2) * 2 * TOKS
                if k >= 2:
                    s.wait_ge(b.sem_t, t0 + k - 1)
                for i, c in enumerate([2 * k, 2 * k + 1]):
                    ins = s.activation(b.sq[:, off + i * TOKS:off + (i + 1) * TOKS],
                                       xsrc[c][:, :], AF.Square)
                b.c_s += 1
                ins.then_inc(b.sem_s, 1)

        def v_body(v):
            for k in range(nb):
                off = (k % 2) * 2 * TOKS
                if k >= 2:
                    v.wait_ge(b.sem_t, t0 + k - 1)
                for i, c in enumerate([2 * k, 2 * k + 1]):
                    ins = v.tensor_copy(b.xb[:, off + i * TOKS:off + (i + 1) * TOKS],
                                        xsrc[c][:, :])
                b.c_v += 1
                ins.then_inc(b.sem_v, 1)

        def t_body(t):
            for k in range(nb):
                off = (k % 2) * 2 * TOKS
                t.wait_ge(b.sem_s, s0 + k + 1)
                t.wait_ge(b.sem_v, v0 + k + 1)
                for i, c in enumerate([2 * k, 2 * k + 1]):
                    st = (c == 0)
                    sp = (c == NCH - 1)
                    xbv = b.xb[:, off + i * TOKS:off + (i + 1) * TOKS]
                    sqv = b.sq[:, off + i * TOKS:off + (i + 1) * TOKS]
                    t.matmul(b.pV0[0:1, 0:512], b.onesP_bf[:, :], xbv[:, 0:512],
                             start=st, stop=sp, skip_group_check=True)
                    t.matmul(b.pV0[0:1, 512:1024], b.onesP_bf[:, :], xbv[:, 512:1024],
                             start=st, stop=sp, skip_group_check=True)
                    t.matmul(b.pU[0:1, 0:512], b.onesP_bf[:, :], sqv[:, 0:512],
                             start=st, stop=sp, skip_group_check=True)
                    ins = t.matmul(b.pQK[0:1, 0:512], b.onesP_bf[:, :], sqv[:, 512:1024],
                                   start=st, stop=sp, skip_group_check=True)
                b.c_t += 1
                ins.then_inc(b.sem_t, 1)

        self.block(scalar=s_body, vector=v_body, tensor=t_body)

        def s_nm(s):
            s.activation(b.nm_r[0:1, :], b.pV0[0:1, :], AF.Copy, scale=-1.0 / 2048.0)
        self.block(scalar=s_nm)

        def s_m2(s):
            s.activation(b.m2_r[0:1, :], b.nm_r[0:1, :], AF.Square)
        self.block(scalar=s_m2)

        def v_var(v):
            v.scalar_tensor_tensor(b.rstd_r[0:1, 0:512], b.pU[0:1, 0:512],
                                   1.0 / 2048.0, b.m2_r[0:1, 0:512],
                                   ALU.mult, ALU.subtract)
            v.scalar_tensor_tensor(b.rstd_r[0:1, 512:1024], b.pQK[0:1, 0:512],
                                   1.0 / 2048.0, b.m2_r[0:1, 512:1024],
                                   ALU.mult, ALU.subtract)
        self.block(vector=v_var)

        def s_rstd(s):
            s.activation(b.rstd_r[0:1, :], b.rstd_r[0:1, :], AF.Ln,
                         bias=b.eps_ln[0:1, 0:1])
            s.activation(b.rstd_r[0:1, :], b.rstd_r[0:1, :], AF.Exp, scale=-0.5)
        self.block(scalar=s_rstd)

        def v_nmr(v):
            v.tensor_tensor(b.nmr_r[0:1, :], b.nm_r[0:1, :], b.rstd_r[0:1, :],
                            ALU.mult)
        self.block(vector=v_nmr)

        def t_bcast(t):
            t.matmul(b.pA[:, :], b.ones_row_f[:, :], b.rstd_r[0:1, 0:512],
                     start=True, stop=True)
            t.matmul(b.pB[:, :], b.ones_row_f[:, :], b.rstd_r[0:1, 512:1024],
                     start=True, stop=True)
            t.matmul(b.pQK[:, :], b.ones_row_f[:, :], b.nmr_r[0:1, 0:512],
                     start=True, stop=True)
            t.matmul(b.pO[:, :], b.ones_row_f[:, :], b.nmr_r[0:1, 512:1024],
                     start=True, stop=True)
        self.block(tensor=t_bcast)

        def v_apply(v):
            for c in range(NCH):
                x = b.xs(c)
                xnv = b.xn[:, c * TOKS:(c + 1) * TOKS]
                v.tensor_tensor(b.tmp[:, 0:512], x[:, 0:512], b.pA[:, :], ALU.mult)
                v.tensor_tensor(b.tmp[:, 512:1024], x[:, 512:1024], b.pB[:, :], ALU.mult)
                v.tensor_tensor(xnv[:, 0:512], b.tmp[:, 0:512], b.pQK[:, :], ALU.add)
                v.tensor_tensor(xnv[:, 512:1024], b.tmp[:, 512:1024], b.pO[:, :], ALU.add)
        self.block(vector=v_apply)
        self.tap(f"xn_{tag}", b.xn, P, NCH * TOKS)

    def emit_ffn(self, tag):
        b = self
        banks4 = [b.pA, b.pB, b.pQK, b.pO]

        def t_lg(t):
            for tt in range(8):
                for c in range(NCH):
                    t.matmul(b.pA[:, tt * 16:tt * 16 + 8],
                             b.xn[:, c * TOKS + tt * P:c * TOKS + (tt + 1) * P],
                             b.wr_bf[:, c * 8:(c + 1) * 8],
                             start=(c == 0), stop=(c == NCH - 1), skip_group_check=True)
        self.block(tensor=t_lg)

        def v_sm1(v):
            for tt in range(8):
                v.tensor_tensor(b.lgs[:, tt * 8:(tt + 1) * 8],
                                b.pA[:, tt * 16:tt * 16 + 8], b.br_b[:, :], ALU.add)
            for tt in range(8):
                v.reduce_max(b.mxb[:, tt:tt + 1], b.lgs[:, tt * 8:(tt + 1) * 8], AX)
            for tt in range(8):
                v.tensor_scalar(b.exb[:, tt * 8:(tt + 1) * 8], b.lgs[:, tt * 8:(tt + 1) * 8],
                                b.mxb[:, tt:tt + 1], None, ALU.subtract)
        self.block(vector=v_sm1)

        def s_exp(s):
            s.activation(b.exb[:, :], b.exb[:, :], AF.Exp)
        self.block(scalar=s_exp)

        def v_sm2a(v):
            for tt in range(8):
                v.reduce_sum(b.seb[:, tt:tt + 1], b.exb[:, tt * 8:(tt + 1) * 8], AX)
        self.block(vector=v_sm2a)

        def s_recip(s):
            s.activation(b.wb[:, :], b.seb[:, :], AF.Ln)
            s.activation(b.wb[:, :], b.wb[:, :], AF.Exp, scale=-1.0)
        self.block(scalar=s_recip)

        def v_sm2(v):
            for tt in range(8):
                v.tensor_scalar(b.ohb[:, tt * 8:(tt + 1) * 8], b.lgs[:, tt * 8:(tt + 1) * 8],
                                b.mxb[:, tt:tt + 1], None, ALU.is_equal)
            for tt in range(8):
                v.tensor_scalar(b.mw_tm[:, tt * 8:(tt + 1) * 8],
                                b.ohb[:, tt * 8:(tt + 1) * 8],
                                b.wb[:, tt:tt + 1], FFN_SCALE, ALU.mult, ALU.mult)
        self.block(vector=v_sm2)
        self.tap(f"seb_{tag}", b.seb, P, 8)
        self.tap(f"lgs_{tag}", b.lgs, P, 64)
        self.tap(f"exb_{tag}", b.exb, P, 64)
        self.tap(f"ohb_{tag}", b.ohb, P, 64)
        self.tap(f"mxb_{tag}", b.mxb, P, 8)
        self.tap(f"wb_{tag}", b.wb, P, 8)
        self.tap(f"mwtm_{tag}", b.mw_tm, P, 64)

        def t_tr(t):
            for tt in range(8):
                dst = b.pO if tt < 4 else b.pU
                t.transpose(dst[0:8, (tt % 4) * P:(tt % 4 + 1) * P],
                            b.mw_tm[:, tt * 8:(tt + 1) * 8], b.idF32[:, :])
        self.block(tensor=t_tr)

        def v_mw(v):
            v.tensor_copy(b.mw_fm[0:8, 0:512], b.pO[0:8, :])
            v.tensor_copy(b.mw_fm[0:8, 512:1024], b.pU[0:8, :])
        self.block(vector=v_mw)

        for wave in range(2):
            def t_ex(t, wave=wave):
                for i in range(2):
                    rc = wave * 2 + i
                    for tt in range(2):
                        bank = banks4[i * 2 + tt]
                        t.matmul(bank[:, :], b.eexp[:, rc * P:(rc + 1) * P],
                                 b.mw_fm[0:8, tt * 512:(tt + 1) * 512],
                                 start=True, stop=True)
            self.block(tensor=t_ex)

            def v_ex(v, wave=wave):
                for i in range(2):
                    rc = wave * 2 + i
                    for tt in range(2):
                        bank = banks4[i * 2 + tt]
                        v.tensor_copy(
                            b.mask_b[:, rc * TOKS + tt * 512:rc * TOKS + (tt + 1) * 512],
                            bank[:, :])
            self.block(vector=v_ex)

        htiles = [(rc, tt) for rc in range(4) for tt in range(2)]
        bsets = [(b.pA, b.pB), (b.pQK, b.pO)]
        t0, v0 = self.c_t, self.c_v

        def t_h(t):
            for w in range(0, len(htiles), 2):
                if w >= 4:
                    t.wait_ge(b.sem_v, v0 + w // 2 - 1)
                for i, (rc, tt) in enumerate(htiles[w:w + 2]):
                    bank = bsets[(w // 2) % 2][i]
                    for c in range(NCH):
                        ins = t.matmul(bank[:, :],
                                       b.a_bf[:, c * 512 + rc * P:c * 512 + (rc + 1) * P],
                                       b.xn[:, c * TOKS + tt * 512:c * TOKS + (tt + 1) * 512],
                                       start=(c == 0), stop=(c == NCH - 1))
                b.c_t += 1
                ins.then_inc(b.sem_t, 1)

        def v_h(v):
            for w in range(0, len(htiles), 2):
                v.wait_ge(b.sem_t, t0 + w // 2 + 1)
                for i, (rc, tt) in enumerate(htiles[w:w + 2]):
                    bank = bsets[(w // 2) % 2][i]
                    sl = slice(rc * TOKS + tt * 512, rc * TOKS + (tt + 1) * 512)
                    ins = v.tensor_tensor(b.hm[:, sl], bank[:, :], b.mask_b[:, sl], ALU.mult)
                b.c_v += 1
                ins.then_inc(b.sem_v, 1)

        self.block(tensor=t_h, vector=v_h)
        self.tap(f"hm_{tag}", b.hm, P, 4 * TOKS)

        store = (tag == "ffn2")
        t0, v0 = self.c_t, self.c_v
        rotplan = []
        sp = b.spare
        sm = list(b.slotmap)
        for oc in range(NCH):
            rotplan.append((sm[oc], sp))
            sm[oc], sp = sp, sm[oc]

        def t_y(t):
            for oc in range(NCH):
                if oc >= 2:
                    t.wait_ge(b.sem_v, v0 + oc - 1)
                for tt in range(2):
                    bank = bsets[oc % 2][tt]
                    for rc in range(4):
                        ins = t.matmul(bank[:, :],
                                       b.b_bf[:, rc * 2048 + oc * P:rc * 2048 + (oc + 1) * P],
                                       b.hm[:, rc * TOKS + tt * 512:rc * TOKS + (tt + 1) * 512],
                                       start=(rc == 0), stop=(rc == 3))
                b.c_t += 1
                ins.then_inc(b.sem_t, 1)

        def v_y(v):
            for oc in range(NCH):
                v.wait_ge(b.sem_t, t0 + oc + 1)
                olds, news = rotplan[oc]
                for tt in range(2):
                    bank = bsets[oc % 2][tt]
                    ins = v.tensor_tensor(b.xslot[news][:, tt * 512:(tt + 1) * 512],
                                          bank[:, :],
                                          b.xslot[olds][:, tt * 512:(tt + 1) * 512], ALU.add)
                b.c_v += 1
                ins.then_inc(b.sem_v, 1)

        bodies = dict(tensor=t_y, vector=v_y)
        if store:
            def y_st(y):
                for oc in range(NCH):
                    y.wait_ge(b.sem_v, v0 + oc + 1)
                    self.ydma(y, b.out_t[oc * P:(oc + 1) * P, :],
                              b.xslot[rotplan[oc][1]][:, :])
                y.wait_ge(b.sem_y, b.exp_y)
            bodies["sync"] = y_st
        self.block(**bodies)
        for oc in range(NCH):
            b.rotate(oc)

    def emit_attn(self):
        b = self
        spill_map = list(self.slotmap)

        def s_spill(s):
            self.ydma(s, b.xspill.ap(), b.xall[:, :])
        self.block(scalar=s_spill)
        spill_cnt = b.exp_y

        self.emit_ln("attn")

        self.mark_phase("ln2_done")
        inproj = (self.cc_mode == "inproj")
        self.emit_proj(0, 0, extra_wait=spill_cnt)
        self.mark_phase("proj0_done")
        self.emit_amath(0, 0, pre_issue=1)
        if not inproj:
            self.emit_cc_issue(0)
        for g in range(NG):
            gb = g % 2
            if g + 1 < NG:
                self.emit_proj(g + 1, (g + 1) % 2, cc_pre=(g if inproj else None),
                               preissued=True)
                self.mark_phase(f"proj{g+1}_done")
            elif inproj:
                self.emit_cc_issue(g)
            self.emit_cc_fixup(g, gb)
            self.mark_phase(f"fix{g}_done")
            self.emit_bmath(g, gb)
            self.mark_phase(f"bmath{g}_done")
            if g + 1 < NG:
                self.emit_amath(g + 1, (g + 1) % 2,
                                pre_issue=(g + 2 if g + 2 < NG else None))
                if not inproj:
                    self.emit_cc_issue(g + 1)
                self.mark_phase(f"amath{g+1}_done")

        self.tap("o", b.o_bf, P, NH * TOKS)

        def s_rv(s):
            s.activation(b.rstd_r[0:1, :], b.pV0[0:1, :], AF.Ln,
                         bias=b.eps_rms[0:1, 0:1], scale=1.0 / 2048.0)
        self.block(scalar=s_rv)

        def s_rv2(s):
            s.activation(b.rstd_r[0:1, :], b.rstd_r[0:1, :], AF.Exp, scale=-0.5)
        self.block(scalar=s_rv2)

        def t_rvb(t):
            t.matmul(b.pQK[:, :], b.ones_row_f[:, :], b.rstd_r[0:1, 0:512],
                     start=True, stop=True)
            t.matmul(b.pU[:, :], b.ones_row_f[:, :], b.rstd_r[0:1, 512:1024],
                     start=True, stop=True)
        self.block(tensor=t_rvb)

        def v_rvb(v):
            v.tensor_copy(b.rstd_sb[:, 0:512], b.pQK[:, :])
            v.tensor_copy(b.rstd_sb[:, 512:1024], b.pU[:, :])
        self.block(vector=v_rvb)

        def y_rest(y):
            self.ydma(y, b.xall[:, :], b.xspill.ap())
        self.block(sync=y_rest)
        restore_cnt = b.exp_y
        self.slotmap = spill_map
        self.spare = [i for i in range(NSLOT) if i not in spill_map][0]

        self.emit_gate()
        self.mark_phase("gate_done")
        self.tap("os", b.o_bf, P, NH * TOKS)

        self.emit_wout(restore_cnt)
        self.mark_phase("wout_done")
        self.tap("x2", self.xs(0), P, TOKS)

    def emit_proj(self, g, gb, extra_wait=0, cc_pre=None, preissued=False):
        b = self
        kg, vg, qg = b.k_g[gb], b.v_g[gb], b.q_g[gb]
        pieces = []
        for which, colbase in (("k", 2048 + g * 512), ("v", 4096 + g * 512),
                               ("q", 0 + g * 512)):
            for half in range(2):
                pieces.append((which, colbase + half * 256, half))

        waves = []
        for p, (which, cb, half) in enumerate(pieces):
            ws = b.wstr[p % 2]
            if which in ("k", "q"):
                dst = kg if which == "k" else qg
                for hi2 in range(2):
                    hi = half * 2 + hi2

                    def mm(t, banks, ws=ws, hi2=hi2):
                        for tt in range(2):
                            for c in range(NCH):
                                ins = t.matmul(
                                    banks[tt][:, :],
                                    ws[:, c * 256 + hi2 * P:c * 256 + (hi2 + 1) * P],
                                    b.xn[:, c * TOKS + tt * 512:c * TOKS + (tt + 1) * 512],
                                    start=(c == 0), stop=(c == NCH - 1))
                        return ins

                    def ev(s, banks, dst=dst, hi=hi):
                        for tt in range(2):
                            ins = s.activation(
                                dst[:, hi * TOKS + tt * 512:hi * TOKS + (tt + 1) * 512],
                                banks[tt][:, :], AF.Silu)
                        return ins
                    waves.append((p, mm, ev))
            else:
                for wv in range(4):
                    def mm(t, banks, ws=ws, wv=wv):
                        for i in range(2):
                            tt = wv * 2 + i
                            for c in range(NCH):
                                ins = t.matmul(
                                    banks[i][:, 0:256],
                                    b.xn[:, c * TOKS + tt * P:c * TOKS + (tt + 1) * P],
                                    ws[:, c * 256:(c + 1) * 256],
                                    start=(c == 0), stop=(c == NCH - 1))
                        return ins

                    def ev(s, banks, vg=vg, wv=wv, half=half):
                        for i in range(2):
                            tt = wv * 2 + i
                            ins = s.activation(
                                vg[:, tt * 512 + half * 256:tt * 512 + (half + 1) * 256],
                                banks[i][:, 0:256], AF.Silu)
                        return ins
                    waves.append((p, mm, ev))

        nw = len(waves)
        bsets = [(b.pA, b.pB), (b.pQK, b.pO)]
        t0, s0 = self.c_t, self.c_s
        piece_first = {}
        piece_last = {}
        for w, (p, _, _) in enumerate(waves):
            piece_first.setdefault(p, w)
            piece_last[p] = w
        if preissued:
            piece_cnt = [self.exp_w[p % 2] + 16 * (p // 2) for p in range(len(pieces))]
        else:
            piece_cnt = [self.exp_w[p % 2] + 16 * (p // 2 + 1) for p in range(len(pieces))]

        wsrc = b.wqkv_bf if b.precast else b.wqkv
        skip = 2 if preissued else 0

        def g_body(gq):
            if b.precast:
                gq.wait_ge(b.sem_cast, 48)
            if extra_wait:
                gq.wait_ge(b.sem_y, extra_wait)
            for p, (which, cb, half) in enumerate(pieces):
                if p < skip:
                    continue
                if p >= 2:
                    gq.wait_ge(b.sem_t, t0 + piece_last[p - 1] + 1)
                srcap = bass.AP(wsrc, cb, [[6144, P], [6144 * P, NCH], [1, 256]])
                b.wdma(gq, p % 2, b.wstr[p % 2].ap().rearrange("p (c n) -> p c n", c=NCH), srcap)

        def t_body(t):
            for w, (p, mm, _) in enumerate(waves):
                if w == piece_first[p]:
                    t.wait_ge(b.sem_w[p % 2], piece_cnt[p])
                if w >= 2:
                    t.wait_ge(b.sem_s, s0 + w - 1)
                ins = mm(t, bsets[w % 2])
                b.c_t += 1
                ins.then_inc(b.sem_t, 1)

        def s_body(s):
            for w, (p, _, ev) in enumerate(waves):
                s.wait_ge(b.sem_t, t0 + w + 1)
                ins = ev(s, bsets[w % 2])
                b.c_s += 1
                ins.then_inc(b.sem_s, 1)

        def g_cc(gq):
            if cc_pre is not None:
                gq.collective_compute(
                    "AllGather", ALU.bypass,
                    replica_groups=[[0, 1], [2, 3], [4, 5], [6, 7]],
                    ins=[b.cc_in[cc_pre].ap()], outs=[b.cc_out[cc_pre].ap()],
                ).then_inc(b.sem_cc)
                b.exp_cc += 1

        if b.precast:
            self.block(sync=g_body, tensor=t_body, scalar=s_body, gpsimd=g_cc)
        else:
            def g_all(gq):
                g_cc(gq)
                g_body(gq)
            self.block(gpsimd=g_all, tensor=t_body, scalar=s_body)
        if g == 0:
            self.tap("k_g0", kg, P, HG * TOKS)
            self.tap("v_g0", vg, P, HG * TOKS)
            self.tap("q_g0", qg, P, HG * TOKS)

    def emit_amath(self, g, gb, pre_issue=None):
        b = self
        kg, vg = b.k_g[gb], b.v_g[gb]
        Sg = b.S_g[gb]
        ubank = [b.pU, b.pA]
        TTR = [0] * HG
        TU = [0] * HG
        VKS = [0] * HG
        VSC = [0] * HG
        ct = self.c_t
        for hi in range(HG):
            ct += 1; TTR[hi] = ct
            if hi >= 1:
                ct += 1; TU[hi - 1] = ct
        ct += 1; TU[HG - 1] = ct
        cv = self.c_v
        for hi in range(HG):
            cv += 1; VKS[hi] = cv
            if hi >= 1:
                cv += 1; VSC[hi - 1] = cv
        cv += 1; VSC[HG - 1] = cv

        def emit_tr(t, hi):
            for tt in range(8):
                ins = t.transpose(b.pT[:, tt * P:(tt + 1) * P],
                                  kg[:, hi * TOKS + tt * P:hi * TOKS + (tt + 1) * P],
                                  b.idBF[:, :])
            return ins

        def emit_u(t, hi):
            for j in range(NBLK):
                for mc in range(2):
                    tt = 2 * j + mc
                    ins = t.matmul(ubank[hi % 2][:, j * DH:(j + 1) * DH],
                                   b.ks_h[hi % 2][:, tt * P:(tt + 1) * P],
                                   vg[:, tt * 512 + hi * DH:tt * 512 + (hi + 1) * DH],
                                   start=(mc == 0), stop=(mc == 1))
            return ins

        def t_body(t):
            for hi in range(HG):
                if hi >= 1:
                    t.wait_ge(b.sem_v, VKS[hi - 1])
                ins = emit_tr(t, hi)
                ins.then_inc(b.sem_t, 1)
                if hi >= 1:
                    ins = emit_u(t, hi - 1)
                    ins.then_inc(b.sem_t, 1)
            t.wait_ge(b.sem_v, VKS[HG - 1])
            ins = emit_u(t, HG - 1)
            ins.then_inc(b.sem_t, 1)
            b.c_t = ct

        def emit_ks(v, hi):
            hh = g * HG + hi
            for tt in range(8):
                col = (tt % 2) * 16 + hh
                ins = v.tensor_scalar(b.ks_h[hi % 2][:, tt * P:(tt + 1) * P],
                                      b.pT[:, tt * P:(tt + 1) * P],
                                      b.kd_colsT[:, col:col + 1], None, ALU.mult)
            return ins

        def emit_scan(v, hi):
            hh = g * HG + hi
            base = hi * 4 * DH
            v.tensor_copy(Sg[:, base:base + DH], ubank[hi % 2][:, 0:DH])
            for j in range(1, NBLK):
                ins = v.scalar_tensor_tensor(Sg[:, base + j * DH:base + (j + 1) * DH],
                                             Sg[:, base + (j - 1) * DH:base + j * DH],
                                             b.bdp_b[:, hh * NBLK + 1:hh * NBLK + 2],
                                             ubank[hi % 2][:, j * DH:(j + 1) * DH],
                                             ALU.mult, ALU.add)
            return ins

        def v_body(v):
            for hi in range(HG):
                v.wait_ge(b.sem_t, TTR[hi])
                ins = emit_ks(v, hi)
                ins.then_inc(b.sem_v, 1)
                if hi >= 1:
                    v.wait_ge(b.sem_t, TU[hi - 1])
                    ins = emit_scan(v, hi - 1)
                    ins.then_inc(b.sem_v, 1)
            v.wait_ge(b.sem_t, TU[HG - 1])
            ins = emit_scan(v, HG - 1)
            ins.then_inc(b.sem_v, 1)
            b.c_v = cv

        def g_body(gq):
            if pre_issue is not None:
                ng = pre_issue
                for p in range(2):
                    cb = (2048 if p == 0 else 2048 + 256) + ng * 512
                    srcap = bass.AP(b.wqkv, cb, [[6144, P], [6144 * P, NCH], [1, 256]])
                    b.wdma(gq, p, b.wstr[p].ap().rearrange("p (c n) -> p c n", c=NCH),
                           srcap)
            for hi in range(HG):
                gq.wait_ge(b.sem_v, VSC[hi])
                base = hi * 4 * DH
                b.gdma(gq, b.cc_in[g][hi * P:(hi + 1) * P, :],
                       Sg[:, base + 3 * DH:base + 4 * DH])
            gq.wait_ge(b.sem_g, b.exp_g)

        self.block(tensor=t_body, vector=v_body, gpsimd=g_body)
        if g == 0:
            self.tap("S_g0", Sg, P, HG * 4 * DH)

    def emit_cc_issue(self, g):
        b = self

        def g_cc(gq):
            gq.collective_compute(
                "AllGather", ALU.bypass,
                replica_groups=[[0, 1], [2, 3], [4, 5], [6, 7]],
                ins=[b.cc_in[g].ap()], outs=[b.cc_out[g].ap()],
            ).then_inc(b.sem_cc)
            b.exp_cc += 1
        self.block(gpsimd=g_cc)

    def emit_cc_fixup(self, g, gb):
        b = self
        Sg, Tg, kvi = b.S_g[gb], b.T_g[gb], b.kvi_g[gb]

        def g_cc(gq):
            gq.wait_ge(b.sem_cc, g + 1)
            b.gdma(gq, b.gtmp.ap().rearrange("p (h e) -> p h e", h=HG),
                   bass.AP(b.cc_out[g], 0, [[DH, P], [P * DH, HG], [1, DH]]))
            gq.wait_ge(b.sem_g, b.exp_g)
        self.block(gpsimd=g_cc)

        def v_kvi(v):
            v.tensor_scalar(kvi[:, :], b.gtmp[:, :], b.flag_c[:, 0:1], None, ALU.mult)
        self.block(vector=v_kvi)

        def v_fix(v):
            for hi in range(HG):
                hh = g * HG + hi
                base = hi * 4 * DH
                v.tensor_copy(Tg[:, base:base + DH], kvi[:, hi * DH:(hi + 1) * DH])
                for j in range(1, NBLK):
                    v.scalar_tensor_tensor(Tg[:, base + j * DH:base + (j + 1) * DH],
                                           kvi[:, hi * DH:(hi + 1) * DH],
                                           b.bdp_b[:, hh * NBLK + j:hh * NBLK + j + 1],
                                           Sg[:, base + (j - 1) * DH:base + j * DH],
                                           ALU.mult, ALU.add)
        self.block(vector=v_fix)
        if g == 0:
            self.tap("T_g0", Tg, P, HG * 4 * DH)

    def emit_bmath(self, g, gb):
        b = self
        kg, vg, qg = b.k_g[gb], b.v_g[gb], b.q_g[gb]
        Tg = b.T_g[gb]

        def g_qd(gq):
            b.gdma(gq, b.qdB_g.ap().rearrange("p (h l) -> p h l", h=HG),
                   bass.AP(b.qd_dram, g * HG * BLK, [[0, P], [BLK, HG], [1, BLK]]))
            gq.wait_ge(b.sem_g, b.exp_g)
        self.block(gpsimd=g_qd)

        def v_d(v):
            for hi in range(HG):
                hh = g * HG + hi
                v.tensor_scalar(b.D_g[:, hi * 512:(hi + 1) * 512], b.idx_c[:, :],
                                b.s_bc[:, hh:hh + 1], None, ALU.mult)
        self.block(vector=v_d)

        def s_d(s):
            for hi in range(HG):
                s.activation(b.D_g[:, hi * 512:(hi + 1) * 512],
                             b.D_g[:, hi * 512:(hi + 1) * 512], AF.Exp)
        self.block(scalar=s_d)
        if g == 0:
            self.tap("D_g0", b.D_g, P, HG * 512)
            self.tap("qdB_g0", b.qdB_g, P, HG * BLK)

        NI = HG * NBLK
        qkbank = [b.pQK, b.pU]
        obank = [b.pA, b.pO]
        TQK = [0] * NI
        TO = [0] * NI
        TV = [0] * NI
        VOE = [0] * NI
        VQK = [0] * NI
        SQ = [0] * NI
        ct = self.c_t
        for i in range(NI):
            ct += 1; TQK[i] = ct
            if i >= 1:
                ct += 1; TO[i - 1] = ct
            if i >= 4:
                ct += 1; TV[i - 4] = ct
        ct += 1; TO[NI - 1] = ct
        for j in range(NI - 4, NI):
            ct += 1; TV[j] = ct
        cv = self.c_v
        for i in range(NI):
            if i >= 1:
                cv += 1; VOE[i - 1] = cv
            cv += 1; VQK[i] = cv
        cv += 1; VOE[NI - 1] = cv
        cs = self.c_s
        for j in range(NI):
            cs += 1; SQ[j] = cs

        def mm_qk(t, i):
            hi, blk = i // NBLK, i % NBLK
            for mc in range(2):
                tt = 2 * blk + mc
                ins = t.matmul(qkbank[i % 2][:, mc * 256:(mc + 1) * 256],
                               kg[:, hi * TOKS + tt * P:hi * TOKS + (tt + 1) * P],
                               qg[:, hi * TOKS + blk * BLK:hi * TOKS + (blk + 1) * BLK],
                               start=True, stop=True)
            return ins

        def mm_o(t, j):
            hi, blk = j // NBLK, j % NBLK
            for mc in range(2):
                tt = 2 * blk + mc
                t.matmul(obank[j % 2][:, 0:256],
                         vg[:, tt * 512 + hi * DH:tt * 512 + (hi + 1) * DH],
                         b.qkD[j % 2][:, mc * 256:(mc + 1) * 256],
                         start=(mc == 0), stop=False)
            base = hi * 4 * DH
            return t.matmul(obank[j % 2][:, 0:256],
                            Tg[:, base + blk * DH:base + (blk + 1) * DH],
                            b.qs_b[j % 2][:, :], start=False, stop=True)

        def mm_var(t, j):
            hi, blk = j // NBLK, j % NBLK
            hh = g * HG + hi
            return t.matmul(b.pV0[0:1, blk * BLK:(blk + 1) * BLK],
                            b.onesP_bf[:, :], b.osq[j % 2][:, :],
                            start=(hh == 0 and blk in (0, 2)),
                            stop=(hh == NH - 1 and blk in (1, 3)),
                            skip_group_check=True)

        def t_body(t):
            def do_o(j):
                t.wait_ge(b.sem_v, VQK[j])
                if j >= 2:
                    t.wait_ge(b.sem_v, VOE[j - 2])
                mm_o(t, j).then_inc(b.sem_t, 1)

            def do_var(j):
                t.wait_ge(b.sem_s, SQ[j])
                mm_var(t, j).then_inc(b.sem_t, 1)

            for i in range(NI):
                if i >= 2:
                    t.wait_ge(b.sem_v, VQK[i - 2])
                mm_qk(t, i).then_inc(b.sem_t, 1)
                if i >= 1:
                    do_o(i - 1)
                if i >= 4:
                    do_var(i - 4)
            do_o(NI - 1)
            for j in range(NI - 4, NI):
                do_var(j)
            b.c_t = ct

        def v_body(v):
            def do_oe(j):
                hi, blk = j // NBLK, j % NBLK
                hh = g * HG + hi
                v.wait_ge(b.sem_t, TO[j])
                ins = v.tensor_copy(
                    b.o_bf[:, hh * TOKS + blk * BLK:hh * TOKS + (blk + 1) * BLK],
                    obank[j % 2][:, 0:256])
                ins.then_inc(b.sem_v, 1)

            for i in range(NI):
                if i >= 1:
                    do_oe(i - 1)
                hi, blk = i // NBLK, i % NBLK
                v.wait_ge(b.sem_t, TQK[i])
                v.tensor_tensor(b.qkD[i % 2][:, :], qkbank[i % 2][:, :],
                                b.D_g[:, hi * 512:(hi + 1) * 512], ALU.mult)
                ins = v.tensor_tensor(
                    b.qs_b[i % 2][:, :],
                    qg[:, hi * TOKS + blk * BLK:hi * TOKS + (blk + 1) * BLK],
                    b.qdB_g[:, hi * BLK:(hi + 1) * BLK], ALU.mult)
                ins.then_inc(b.sem_v, 1)
            do_oe(NI - 1)
            b.c_v = cv

        def s_body(s):
            for j in range(NI):
                hi, blk = j // NBLK, j % NBLK
                hh = g * HG + hi
                s.wait_ge(b.sem_v, VOE[j])
                if j >= 2:
                    s.wait_ge(b.sem_t, TV[j - 2])
                ins = s.activation(
                    b.osq[j % 2][:, :],
                    b.o_bf[:, hh * TOKS + blk * BLK:hh * TOKS + (blk + 1) * BLK],
                    AF.Square)
                ins.then_inc(b.sem_s, 1)
            b.c_s = cs

        self.block(tensor=t_body, vector=v_body, scalar=s_body)

    def emit_gate(self):
        b = self
        bset = [(b.pA, b.pB), (b.pQK, b.pO)]
        t0, s0, v0 = self.c_t, self.c_s, self.c_v
        piece_cnt = [self.exp_w[p % 2] + 16 * (p // 2 + 1) for p in range(8)]

        wsrc = b.wg_bf if b.precast else b.wgate

        def g_body(gq):
            if b.precast:
                gq.wait_ge(b.sem_cast, 48)
            for p in range(8):
                if p >= 2:
                    gq.wait_ge(b.sem_t, t0 + 2 * (p - 1))
                srcap = bass.AP(wsrc, p * 256, [[2048, P], [2048 * P, NCH], [1, 256]])
                b.wdma(gq, p % 2, b.gst[p % 2].ap().rearrange("p (c n) -> p c n", c=NCH), srcap)

        def t_body(t):
            for oc in range(NCH):
                p, oc2 = oc // 2, oc % 2
                ws = b.gst[p % 2]
                if oc2 == 0:
                    t.wait_ge(b.sem_w[p % 2], piece_cnt[p])
                if oc >= 2:
                    t.wait_ge(b.sem_s, s0 + oc - 1)
                for tt in range(2):
                    bank = bset[oc % 2][tt]
                    for c in range(NCH):
                        ins = t.matmul(bank[:, :],
                                       ws[:, c * 256 + oc2 * P:c * 256 + (oc2 + 1) * P],
                                       b.xn[:, c * TOKS + tt * 512:c * TOKS + (tt + 1) * 512],
                                       start=(c == 0), stop=(c == NCH - 1))
                b.c_t += 1
                ins.then_inc(b.sem_t, 1)

        def s_body(s):
            for oc in range(NCH):
                s.wait_ge(b.sem_t, t0 + oc + 1)
                if oc >= 2:
                    s.wait_ge(b.sem_v, v0 + oc - 1)
                for tt in range(2):
                    ins = s.activation(b.sg_b[oc % 2][:, tt * 512:(tt + 1) * 512],
                                       bset[oc % 2][tt][:, :], AF.Sigmoid)
                b.c_s += 1
                ins.then_inc(b.sem_s, 1)

        def v_body(v):
            for oc in range(NCH):
                v.wait_ge(b.sem_s, s0 + oc + 1)
                for tt in range(2):
                    v.scalar_tensor_tensor(
                        b.t2_b[0][:, tt * 512:(tt + 1) * 512],
                        b.sg_b[oc % 2][:, tt * 512:(tt + 1) * 512], b.rmsw[:, oc:oc + 1],
                        b.o_bf[:, oc * TOKS + tt * 512:oc * TOKS + (tt + 1) * 512],
                        ALU.mult, ALU.mult)
                for tt in range(2):
                    ins = v.tensor_tensor(
                        b.o_bf[:, oc * TOKS + tt * 512:oc * TOKS + (tt + 1) * 512],
                        b.t2_b[0][:, tt * 512:(tt + 1) * 512],
                        b.rstd_sb[:, tt * 512:(tt + 1) * 512], ALU.mult)
                b.c_v += 1
                ins.then_inc(b.sem_v, 1)

        if b.precast:
            self.block(sync=g_body, tensor=t_body, scalar=s_body, vector=v_body)
        else:
            self.block(gpsimd=g_body, tensor=t_body, scalar=s_body, vector=v_body)

    def emit_wout(self, restore_cnt=0):
        b = self
        bset = [(b.pA, b.pB), (b.pQK, b.pO)]
        t0, v0 = self.c_t, self.c_v
        piece_cnt = [self.exp_w[p % 2] + 16 * (p // 2 + 1) for p in range(8)]

        wsrc = b.wo_bf if b.precast else b.woutw

        def g_body(gq):
            if b.precast:
                gq.wait_ge(b.sem_cast, 48)
            for p in range(8):
                if p >= 2:
                    gq.wait_ge(b.sem_t, t0 + 2 * (p - 1))
                srcap = bass.AP(wsrc, p * 256, [[2048, P], [2048 * P, NCH], [1, 256]])
                b.wdma(gq, p % 2, b.gst[p % 2].ap().rearrange("p (c n) -> p c n", c=NCH), srcap)

        def t_body(t):
            for oc in range(NCH):
                p, oc2 = oc // 2, oc % 2
                ws = b.gst[p % 2]
                if oc2 == 0:
                    t.wait_ge(b.sem_w[p % 2], piece_cnt[p])
                if oc >= 2:
                    t.wait_ge(b.sem_v, v0 + oc - 1)
                for tt in range(2):
                    bank = bset[oc % 2][tt]
                    for c in range(NCH):
                        ins = t.matmul(bank[:, :],
                                       ws[:, c * 256 + oc2 * P:c * 256 + (oc2 + 1) * P],
                                       b.o_bf[:, c * TOKS + tt * 512:c * TOKS + (tt + 1) * 512],
                                       start=(c == 0), stop=(c == NCH - 1))
                b.c_t += 1
                ins.then_inc(b.sem_t, 1)

        def v_body(v):
            for oc in range(NCH):
                v.wait_ge(b.sem_t, t0 + oc + 1)
                if oc == 0 and restore_cnt:
                    v.wait_ge(b.sem_y, restore_cnt)
                new = b.xslot[b.spare]
                old = b.xs(oc)
                for tt in range(2):
                    bank = bset[oc % 2][tt]
                    ins = v.tensor_tensor(new[:, tt * 512:(tt + 1) * 512], bank[:, :],
                                          old[:, tt * 512:(tt + 1) * 512], ALU.add)
                b.rotate(oc)
                b.c_v += 1
                ins.then_inc(b.sem_v, 1)

        if b.precast:
            self.block(sync=g_body, tensor=t_body, vector=v_body)
        else:
            self.block(gpsimd=g_body, tensor=t_body, vector=v_body)

    def emit_store(self):
        b = self

        def y_out(y):
            for c in range(NCH):
                self.ydma(y, b.out_t[c * P:(c + 1) * P, :], b.xs(c)[:, :])
            y.wait_ge(b.sem_y, b.exp_y)
        self.block(sync=y_out)

    def mark_phase(self, name):
        self.phase_marks.append((name, self.nc.n_instructions()))

    def build(self, upto="all"):
        self.phase_marks = []
        self.mark_phase("start")
        self.emit_setup()
        self.mark_phase("setup_done")
        self.load_ffn_weights()
        self.emit_ln("ffn1")
        self.mark_phase("ln1_done")
        if upto == "ln1":
            self.emit_store(); self.es.close(); return self.nc
        self.emit_ffn("ffn1")
        self.mark_phase("ffn1_done")
        self.tap("x1", self.xs(0), P, TOKS)
        if upto == "ffn1":
            self.emit_store(); self.es.close(); return self.nc
        self.emit_attn()
        self.mark_phase("attn_done")
        if upto == "attn":
            self.emit_store(); self.es.close(); return self.nc
        self.load_ffn_weights()
        self.emit_ln("ffn2")
        self.mark_phase("ln3_done")
        self.emit_ffn("ffn2")
        self.mark_phase("ffn2_done")
        self.es.close()
        return self.nc



def _host_consts():
    m = np.arange(BLK)
    l = np.arange(BLK)
    idx = (m[:, None] - l[None, :]).astype(np.float32)
    idx[m[:, None] > l[None, :]] = NEG_BIG
    idx_c = np.concatenate([idx[0:128, :], idx[128:256, :]], axis=1).astype(np.float32)
    gq = np.broadcast_to(-(l + 1).astype(np.float32), (NH, BLK)).copy()
    gk = np.broadcast_to(-(255 - m).astype(np.float32), (NH, BLK)).copy()
    gbd = np.broadcast_to((-256.0 * np.arange(NBLK)).astype(np.float32), (NH, NBLK)).copy()
    eexp = np.zeros((8, 512), np.float32)
    for j in range(512):
        eexp[j // 64, j] = 1.0
    return idx_c, gq, gk, gbd, eexp


def build_inputs(inputs):
    x = np.asarray(inputs["L0_attn_hidden_states"], np.float32)
    wqkv = np.ascontiguousarray(np.asarray(inputs["L0_attn_w_qkv"], np.float32))
    wgate = np.ascontiguousarray(np.asarray(inputs["L0_attn_w_gate"], np.float32))
    woutw = np.ascontiguousarray(np.asarray(inputs["L0_attn_w_out"], np.float32))
    rmsw = np.asarray(inputs["L0_attn_rms_weight"], np.float32)
    slope = np.asarray(inputs["L0_attn_slope_rate"], np.float32).reshape(NH)
    A = np.asarray(inputs["L0_ffn_A"], np.float32)
    B = np.asarray(inputs["L0_ffn_B"], np.float32)
    Wr = np.asarray(inputs["L0_ffn_Wr"], np.float32)
    br = np.asarray(inputs["L0_ffn_br"], np.float32)

    a_t = np.ascontiguousarray(A.reshape(512, 2048).T)
    b_t = np.ascontiguousarray(B.transpose(0, 2, 1).reshape(512, 2048))
    wr_t = np.ascontiguousarray(Wr.T)
    rmsw_fm = np.ascontiguousarray(rmsw.reshape(NCH, P).T)
    idx_c, gq, gk, gbd, eexp = _host_consts()

    common = dict(
        wqkv=wqkv, wgate=wgate, woutw=woutw, a_t=a_t, b_t=b_t, wr_t=wr_t,
        br_in=np.ascontiguousarray(br.reshape(1, 8)), rmsw_in=rmsw_fm,
        s_col_in=np.ascontiguousarray(slope.reshape(NH, 1)),
        idx_in=idx_c, gq_in=gq, gk_in=gk, gbd_in=gbd, eexp_in=eexp,
    )
    in_maps = []
    for c in range(8):
        bidx, half = c // 2, c % 2
        xc = np.ascontiguousarray(x[bidx, half * TOKS:(half + 1) * TOKS, :].T)
        flag = np.full((P, 1), 1.0 if half == 1 else 0.0, np.float32)
        m = dict(common)
        m["x_in"] = xc
        m["flag_in"] = flag
        in_maps.append(m)
    return in_maps


def kernel(**inputs):
    in_maps = build_inputs(inputs)
    bld = Builder()
    nc = bld.build()
    res = run_bass_kernel_spmd(nc, in_maps, core_ids=list(range(8)))
    out = np.empty((4, 2048, 2048), np.float32)
    for c in range(8):
        bidx, half = c // 2, c % 2
        out[bidx, half * TOKS:(half + 1) * TOKS, :] = res.results[c]["out_t"].T
    return out


# version 09
# speedup vs baseline: 1.2297x; 1.0708x over previous
import numpy as np
from contextlib import ExitStack

import concourse.bass as bass
import concourse.mybir as mybir
from concourse.bass_utils import run_bass_kernel_spmd

F32 = mybir.dt.float32
BF16 = mybir.dt.bfloat16
AF = mybir.ActivationFunctionType
ALU = mybir.AluOpType
AX = mybir.AxisListType.X

P = 128
TOKS = 1024
NCH = 16
NH = 16
HG = 4
NG = NH // HG
BLK = 256
NBLK = TOKS // BLK
DH = 128
FFN_SCALE = 0.5 / 64.0
NEG_BIG = -1.0e9

SB0 = 16512
X_BASE = SB0
X_SLOT = 4096
NSLOT = 17
XN_BASE = SB0 + 69632
OB_BASE = SB0 + 102400
WK_BASE = SB0 + 135168
CONST_BASE = SB0 + 196608

KVQ = [X_BASE, X_BASE + 24576]
WSTR = [X_BASE + 49152, X_BASE + 57344]

WK_S = [WK_BASE, WK_BASE + 8192]
WK_T = [WK_BASE + 16384, WK_BASE + 20480]
WK_KVI = [WK_BASE + 24576, WK_BASE + 26624]
WK_D = WK_BASE + 28672
WK_QDB = WK_BASE + 36864
WK_KS = [WK_BASE + 40960, WK_BASE + 43008]
WK_QS = [WK_BASE + 45056, WK_BASE + 45568]
WK_QKD = [WK_BASE + 46080, WK_BASE + 47104]
WK_GTMP = WK_BASE + 48128
WK_OSQ = [WK_BASE + 50176, WK_BASE + 50688]

WK_ABF = WK_BASE
WK_BBF = WK_BASE + 16384
WK_WRBF = WK_BASE + 32768
WK_XB = WK_BASE + 33280
WK_SQ = WK_BASE + 41472
WK_TMP = WK_BASE + 49664
WK_ROWA = WK_BASE + 53760
WK_LGS = WK_BASE + 57856
WK_EXB = WK_BASE + 58112
WK_OHB = WK_BASE + 58368
WK_MXB = WK_BASE + 58624
WK_NMX = WK_BASE + 58656
WK_SEB = WK_BASE + 58688
WK_WB = WK_BASE + 58720
WK_GST = [WK_BASE, WK_BASE + 8192]
WK_SG = [WK_BASE + 16384, WK_BASE + 18432]
WK_T2 = [WK_BASE + 20480, WK_BASE + 22528]

OB_MASKB = OB_BASE
OB_HM = OB_BASE + 16384
OB_MWTM = OB_BASE + 24576
OB_MWFM = OB_BASE + 24832

_CO = [CONST_BASE]


def _co(nbytes, align=64):
    _CO[0] = (_CO[0] + align - 1) // align * align
    off = _CO[0]
    _CO[0] += nbytes
    return off


class Builder:
    def __init__(self, taps=(), precast=False, cc_mode="block"):
        self.precast = precast
        self.cc_mode = cc_mode
        self.taps = set(taps)
        _CO[0] = CONST_BASE
        nc = bass.Bass("TRN2", target_bir_lowering=False)
        self.nc = nc
        self.es = ExitStack()

        dt = nc.dram_tensor
        self.x_in = dt("x_in", [2048, TOKS], F32, kind="ExternalInput")
        self.wqkv = dt("wqkv", [P, 24 * 4096], F32, kind="ExternalInput")
        self.wgate = dt("wgate", [P, 8 * 4096], F32, kind="ExternalInput")
        self.woutw = dt("woutw", [P, 8 * 4096], F32, kind="ExternalInput")
        self.a_t = dt("a_t", [2048, 512], F32, kind="ExternalInput")
        self.b_t = dt("b_t", [512, 2048], F32, kind="ExternalInput")
        self.wr_t = dt("wr_t", [2048, 8], F32, kind="ExternalInput")
        self.br_in = dt("br_in", [1, 8], F32, kind="ExternalInput")
        self.rmsw_in = dt("rmsw_in", [P, NCH], F32, kind="ExternalInput")
        self.s_col_in = dt("s_col_in", [NH, 1], F32, kind="ExternalInput")
        self.flag_in = dt("flag_in", [P, 1], F32, kind="ExternalInput")
        self.idx_in = dt("idx_in", [P, 512], F32, kind="ExternalInput")
        self.gq_in = dt("gq_in", [NH, BLK], F32, kind="ExternalInput")
        self.gk_in = dt("gk_in", [NH, BLK], F32, kind="ExternalInput")
        self.gbd_in = dt("gbd_in", [NH, NBLK], F32, kind="ExternalInput")
        self.eexp_in = dt("eexp_in", [8, 512], F32, kind="ExternalInput")
        self.out_t = dt("out_t", [2048, TOKS], F32, kind="ExternalOutput")
        self.xspill = dt("xspill", [P, NSLOT * TOKS], F32)
        self.wqkv_bf = dt("wqkv_bf", [2048, 6144], BF16)
        self.wg_bf = dt("wg_bf", [2048, 2048], BF16)
        self.wo_bf = dt("wo_bf", [2048, 2048], BF16)
        self.qd_dram = dt("qd_dram", [NH, BLK], F32)
        self.bdp_dram = dt("bdp_dram", [NH, NBLK], F32)
        self.cc_in = [dt(f"cc_in{g}", [HG * P, DH], F32) for g in range(NG)]
        self.cc_out = [dt(f"cc_out{g}", [2 * HG * P, DH], F32) for g in range(NG)]

        at = nc.alloc_sbuf_tensor_at
        self.xslot = [at(f"xslot{i}", [P, TOKS], F32, offset=X_BASE + i * X_SLOT)
                      for i in range(NSLOT)]
        self.xall = at("xall", [P, NSLOT * TOKS], F32, offset=X_BASE)
        self.slotmap = list(range(NCH))
        self.spare = NCH
        self.xn = at("xn", [P, NCH * TOKS], BF16, offset=XN_BASE)
        self.o_bf = at("o_bf", [P, NH * TOKS], BF16, offset=OB_BASE)
        self.k_g = [at(f"k_g{i}", [P, HG * TOKS], BF16, offset=KVQ[i]) for i in range(2)]
        self.v_g = [at(f"v_g{i}", [P, HG * TOKS], BF16, offset=KVQ[i] + 8192)
                    for i in range(2)]
        self.q_g = [at(f"q_g{i}", [P, HG * TOKS], BF16, offset=KVQ[i] + 16384)
                    for i in range(2)]
        self.wstr = [at(f"wstr{i}", [P, NCH * 256], BF16, offset=WSTR[i]) for i in range(2)]
        self.S_g = [at(f"S_g{i}", [P, HG * 4 * DH], F32, offset=WK_S[i]) for i in range(2)]
        self.T_g = [at(f"T_g{i}", [P, HG * 4 * DH], BF16, offset=WK_T[i]) for i in range(2)]
        self.kvi_g = [at(f"kvi_g{i}", [P, HG * DH], F32, offset=WK_KVI[i]) for i in range(2)]
        self.D_g = at("D_g", [P, HG * 512], F32, offset=WK_D)
        self.qdB_g = at("qdB_g", [P, HG * BLK], F32, offset=WK_QDB)
        self.ks_h = [at(f"ks_h{i}", [P, TOKS], BF16, offset=WK_KS[i]) for i in range(2)]
        self.qs_b = [at(f"qs_b{i}", [P, BLK], BF16, offset=WK_QS[i]) for i in range(2)]
        self.qkD = [at(f"qkD{i}", [P, 512], BF16, offset=WK_QKD[i]) for i in range(2)]
        self.gtmp = at("gtmp", [P, 512], F32, offset=WK_GTMP)
        self.osq = [at(f"osq{i}", [P, BLK], BF16, offset=WK_OSQ[i]) for i in range(2)]
        self.a_bf = at("a_bf", [P, NCH * 512], BF16, offset=WK_ABF)
        self.b_bf = at("b_bf", [P, 4 * 2048], BF16, offset=WK_BBF)
        self.wr_bf = at("wr_bf", [P, NCH * 8], BF16, offset=WK_WRBF)
        self.xb = at("xb", [P, 4 * TOKS], BF16, offset=WK_XB)
        self.sq = at("sq", [P, 4 * TOKS], BF16, offset=WK_SQ)
        self.tmp = at("tmp", [P, TOKS], F32, offset=WK_TMP)
        self.rows_a = at("rows_a", [P, TOKS], F32, offset=WK_ROWA)
        self.rstd_sb = at("rstd_sb", [P, TOKS], F32, offset=WK_XB)
        self.nm_r = at("nm_r", [1, TOKS], F32, offset=WK_XB + 4096)
        self.m2_r = at("m2_r", [1, TOKS], F32, offset=WK_TMP)
        self.rstd_r = at("rstd_r", [1, TOKS], F32, offset=WK_SQ)
        self.nmr_r = at("nmr_r", [1, TOKS], F32, offset=WK_SQ + 4096)
        self.lgs = at("lgs", [P, 64], F32, offset=WK_LGS)
        self.exb = at("exb", [P, 64], F32, offset=WK_EXB)
        self.ohb = at("ohb", [P, 64], F32, offset=WK_OHB)
        self.mxb = at("mxb", [P, 8], F32, offset=WK_MXB)
        self.nmxb = at("nmxb", [P, 8], F32, offset=WK_NMX)
        self.seb = at("seb", [P, 8], F32, offset=WK_SEB)
        self.wb = at("wb", [P, 8], F32, offset=WK_WB)
        self.mask_b = at("mask_b", [P, 4 * TOKS], F32, offset=OB_MASKB)
        self.hm = at("hm", [P, 4 * TOKS], BF16, offset=OB_HM)
        self.mw_tm = at("mw_tm", [P, 64], F32, offset=OB_MWTM)
        self.mw_fm = at("mw_fm", [8, TOKS], F32, offset=OB_MWFM)
        self.gst = [at(f"gst{i}", [P, NCH * 256], BF16, offset=WK_GST[i]) for i in range(2)]
        self.sg_b = [at(f"sg_b{i}", [P, TOKS], F32, offset=WK_BASE + 16384 + i * 4096)
                     for i in range(2)]
        self.t2_b = [at(f"t2_b{i}", [P, TOKS], F32, offset=WK_BASE + 24576 + i * 4096)
                     for i in range(2)]
        self.idBF = at("idBF", [P, P], BF16, offset=_co(256))
        self.idF32 = at("idF32", [P, P], F32, offset=_co(512))
        self.ones_row_bf = at("ones_row_bf", [1, P], BF16, offset=_co(256))
        self.ones_row_f = at("ones_row_f", [1, P], F32, offset=_co(512))
        self.onesP_bf = at("onesP_bf", [P, 1], BF16, offset=_co(32))
        self.eps_ln = at("eps_ln", [1, 1], F32, offset=_co(32))
        self.eps_rms = at("eps_rms", [1, 1], F32, offset=_co(32))
        self.s_col = at("s_col", [NH, 1], F32, offset=_co(32))
        self.gq = at("gq", [NH, BLK], F32, offset=_co(1024))
        self.gk = at("gk", [NH, BLK], F32, offset=_co(1024))
        self.gbd = at("gbd", [NH, NBLK], F32, offset=_co(32))
        self.qd16 = at("qd16", [NH, BLK], F32, offset=_co(1024))
        self.kd16 = at("kd16", [NH, BLK], F32, offset=_co(1024))
        self.bdp16 = at("bdp16", [NH, NBLK], F32, offset=_co(32))
        self.idx_c = at("idx_c", [P, 512], F32, offset=_co(2048))
        self.eexp = at("eexp", [8, 512], F32, offset=_co(2048))
        self.br_b = at("br_b", [P, 8], F32, offset=_co(32))
        self.rmsw = at("rmsw", [P, NCH], F32, offset=_co(64))
        self.s_bc = at("s_bc", [P, NH], F32, offset=_co(64))
        self.bdp_b = at("bdp_b", [P, NH * NBLK], F32, offset=_co(256))
        self.kd_colsT = at("kd_colsT", [P, 32], F32, offset=_co(128))
        self.flag_c = at("flag_c", [P, 1], F32, offset=_co(32))
        assert _CO[0] <= 229000, _CO[0]

        ec = self.es.enter_context
        self.pA = ec(nc.psum_tensor("pA", [P, 512], F32))
        self.pB = ec(nc.psum_tensor("pB", [P, 512], F32))
        self.pQK = ec(nc.psum_tensor("pQK", [P, 512], F32))
        self.pU = ec(nc.psum_tensor("pU", [P, 512], F32))
        self.pO = ec(nc.psum_tensor("pO", [P, 512], F32))
        self.pT = ec(nc.psum_tensor("pT", [P, 1024], BF16))
        self.pV0 = ec(nc.psum_tensor("pV0", [1, 1024], F32))

        self.sem_g = ec(nc.semaphore("sem_g"))
        self.sem_y = ec(nc.semaphore("sem_y"))
        self.sem_cc = ec(nc.semaphore("sem_cc"))
        self.sem_cast = ec(nc.semaphore("sem_cast"))
        self.sem_w = [ec(nc.semaphore("sem_w0")), ec(nc.semaphore("sem_w1"))]
        self.exp_w = [0, 0]
        self.sem_t = ec(nc.semaphore("sem_t"))
        self.sem_v = ec(nc.semaphore("sem_v"))
        self.sem_s = ec(nc.semaphore("sem_s"))
        self.exp_g = 0
        self.exp_y = 0
        self.exp_cc = 0
        self.c_t = 0
        self.c_v = 0
        self.c_s = 0

    def block(self, **bodies):
        with self.nc.Block() as b:
            for eng in ("sync", "gpsimd", "tensor", "vector", "scalar"):
                if eng in bodies:
                    getattr(b, eng)(bodies[eng])

    def gdma(self, g, out, in_):
        g.dma_start(out=out, in_=in_).then_inc(self.sem_g, 16)
        self.exp_g += 16

    def wdma(self, g, slot, out, in_):
        g.dma_start(out=out, in_=in_).then_inc(self.sem_w[slot], 16)
        self.exp_w[slot] += 16
        return self.exp_w[slot]

    def xall16(self):
        return self.xall[:, 0:NCH * TOKS].rearrange("p (c t) -> p c t", c=NCH)

    def ydma(self, y, out, in_):
        y.dma_start(out=out, in_=in_).then_inc(self.sem_y, 16)
        self.exp_y += 16

    def tap(self, name, tensor, rows, cols):
        if name not in self.taps:
            return
        t = self.nc.dram_tensor(f"tap_{name}", [rows, cols], tensor.dtype,
                                kind="ExternalOutput")

        def _s(y):
            self.ydma(y, t[:, :], tensor[0:rows, 0:cols])
            y.wait_ge(self.sem_y, self.exp_y)
        self.block(sync=_s)

    def mark_t(self, ins):
        ins.then_inc(self.sem_t, 1)
        self.c_t += 1
        return self.c_t

    def mark_v(self, ins):
        ins.then_inc(self.sem_v, 1)
        self.c_v += 1
        return self.c_v

    def mark_s(self, ins):
        ins.then_inc(self.sem_s, 1)
        self.c_s += 1
        return self.c_s

    def xs(self, c):
        return self.xslot[self.slotmap[c]]

    def rotate(self, c):
        old = self.slotmap[c]
        self.slotmap[c] = self.spare
        self.spare = old

    def emit_setup(self):
        b = self

        def g_pre(g):
            g.memset(b.idBF[:, :], 1.0)
            g.memset(b.idF32[:, :], 1.0)
        self.block(gpsimd=g_pre)

        def g_body(g):
            g.affine_select(b.idBF[:, :], b.idBF[:, :], [[-1, P]], ALU.is_equal, 0.0,
                            base=0, channel_multiplier=1)
            g.affine_select(b.idF32[:, :], b.idF32[:, :], [[-1, P]], ALU.is_equal, 0.0,
                            base=0, channel_multiplier=1)
            g.memset(b.ones_row_bf[:, :], 1.0)
            g.memset(b.ones_row_f[:, :], 1.0)
            g.memset(b.onesP_bf[:, :], 1.0)
            g.memset(b.eps_ln[:, :], 1e-5)
            g.memset(b.eps_rms[:, :], 1e-6)
            b.gdma(g, b.s_col[:, :], b.s_col_in[:, :])
            b.gdma(g, b.gq[:, :], b.gq_in[:, :])
            b.gdma(g, b.gk[:, :], b.gk_in[:, :])
            b.gdma(g, b.gbd[:, :], b.gbd_in[:, :])
            b.gdma(g, b.idx_c[:, :], b.idx_in[:, :])
            b.gdma(g, b.eexp[:, :], b.eexp_in[:, :])
            b.gdma(g, b.rmsw[:, :], b.rmsw_in[:, :])
            b.gdma(g, b.flag_c[:, :], b.flag_in[:, :])
            b.gdma(g, b.br_b[:, :], bass.AP(b.br_in, 0, [[0, P], [1, 8]]))
            b.gdma(g, b.s_bc[:, :], bass.AP(b.s_col_in, 0, [[0, P], [1, NH]]))
            g.wait_ge(b.sem_g, b.exp_g)


        def y_body(y):
            b.ydma(y, b.xall16(),
                   b.x_in.ap().rearrange("(c p) t -> p c t", p=P))
            y.wait_ge(b.sem_y, b.exp_y)

        self.block(gpsimd=g_body, sync=y_body)
        self.cast_cnt = self.exp_g

        def v_body(v):
            v.tensor_scalar(b.qd16[:, :], b.gq[:, :], b.s_col[:, 0:1], None, ALU.mult)
            v.tensor_scalar(b.kd16[:, :], b.gk[:, :], b.s_col[:, 0:1], None, ALU.mult)
            v.tensor_scalar(b.bdp16[:, :], b.gbd[:, :], b.s_col[:, 0:1], None, ALU.mult)
        self.block(vector=v_body)

        def s_body(s):
            s.activation(b.qd16[:, :], b.qd16[:, :], AF.Exp)
            s.activation(b.kd16[:, :], b.kd16[:, :], AF.Exp)
            s.activation(b.bdp16[:, :], b.bdp16[:, :], AF.Exp)
        self.block(scalar=s_body)

        def g2(g):
            b.gdma(g, b.qd_dram[:, :], b.qd16[:, :])
            b.gdma(g, b.bdp_dram[:, :], b.bdp16[:, :])
            g.wait_ge(b.sem_g, b.exp_g)
            b.gdma(g, b.bdp_b[:, :],
                   bass.AP(b.bdp_dram, 0, [[0, P], [NBLK, NH], [1, NBLK]]))
            g.wait_ge(b.sem_g, b.exp_g)
        self.block(gpsimd=g2)

        def t_body(t):
            t.transpose(b.pQK[:, 0:16], b.kd16[:, 0:128], b.idF32[0:16, 0:16])
            t.transpose(b.pQK[:, 16:32], b.kd16[:, 128:256], b.idF32[0:16, 0:16])
        self.block(tensor=t_body)

        def v2(v):
            v.tensor_copy(b.kd_colsT[:, 0:32], b.pQK[:, 0:32])
        self.block(vector=v2)

    def load_ffn_weights(self):
        b = self

        def g_body(g):
            b.gdma(g, b.a_bf.ap().rearrange("p (c r) -> p c r", c=NCH),
                   b.a_t.ap().rearrange("(c p) r -> p c r", p=P))
            b.gdma(g, b.b_bf.ap().rearrange("p (c o) -> p c o", c=4),
                   b.b_t.ap().rearrange("(c p) o -> p c o", p=P))
            b.gdma(g, b.wr_bf.ap().rearrange("p (c e) -> p c e", c=NCH),
                   b.wr_t.ap().rearrange("(c p) e -> p c e", p=P))
            g.wait_ge(b.sem_g, b.exp_g)
        self.block(gpsimd=g_body)

    def emit_ln(self, tag):
        b = self
        nb = 8
        t0, s0, v0 = self.c_t, self.c_s, self.c_v
        xsrc = [b.xs(c) for c in range(NCH)]

        def s_body(s):
            for k in range(nb):
                off = (k % 2) * 2 * TOKS
                if k >= 2:
                    s.wait_ge(b.sem_t, t0 + k - 1)
                for i, c in enumerate([2 * k, 2 * k + 1]):
                    ins = s.activation(b.sq[:, off + i * TOKS:off + (i + 1) * TOKS],
                                       xsrc[c][:, :], AF.Square)
                b.c_s += 1
                ins.then_inc(b.sem_s, 1)

        def v_body(v):
            for k in range(nb):
                off = (k % 2) * 2 * TOKS
                if k >= 2:
                    v.wait_ge(b.sem_t, t0 + k - 1)
                for i, c in enumerate([2 * k, 2 * k + 1]):
                    ins = v.tensor_copy(b.xb[:, off + i * TOKS:off + (i + 1) * TOKS],
                                        xsrc[c][:, :])
                b.c_v += 1
                ins.then_inc(b.sem_v, 1)

        def t_body(t):
            for k in range(nb):
                off = (k % 2) * 2 * TOKS
                t.wait_ge(b.sem_s, s0 + k + 1)
                t.wait_ge(b.sem_v, v0 + k + 1)
                for i, c in enumerate([2 * k, 2 * k + 1]):
                    st = (c == 0)
                    sp = (c == NCH - 1)
                    xbv = b.xb[:, off + i * TOKS:off + (i + 1) * TOKS]
                    sqv = b.sq[:, off + i * TOKS:off + (i + 1) * TOKS]
                    t.matmul(b.pV0[0:1, 0:512], b.onesP_bf[:, :], xbv[:, 0:512],
                             start=st, stop=sp, skip_group_check=True)
                    t.matmul(b.pV0[0:1, 512:1024], b.onesP_bf[:, :], xbv[:, 512:1024],
                             start=st, stop=sp, skip_group_check=True)
                    t.matmul(b.pU[0:1, 0:512], b.onesP_bf[:, :], sqv[:, 0:512],
                             start=st, stop=sp, skip_group_check=True)
                    ins = t.matmul(b.pQK[0:1, 0:512], b.onesP_bf[:, :], sqv[:, 512:1024],
                                   start=st, stop=sp, skip_group_check=True)
                b.c_t += 1
                ins.then_inc(b.sem_t, 1)

        self.block(scalar=s_body, vector=v_body, tensor=t_body)

        def s_nm(s):
            s.activation(b.nm_r[0:1, :], b.pV0[0:1, :], AF.Copy, scale=-1.0 / 2048.0)
        self.block(scalar=s_nm)

        def s_m2(s):
            s.activation(b.m2_r[0:1, :], b.nm_r[0:1, :], AF.Square)
        self.block(scalar=s_m2)

        def v_var(v):
            v.scalar_tensor_tensor(b.rstd_r[0:1, 0:512], b.pU[0:1, 0:512],
                                   1.0 / 2048.0, b.m2_r[0:1, 0:512],
                                   ALU.mult, ALU.subtract)
            v.scalar_tensor_tensor(b.rstd_r[0:1, 512:1024], b.pQK[0:1, 0:512],
                                   1.0 / 2048.0, b.m2_r[0:1, 512:1024],
                                   ALU.mult, ALU.subtract)
        self.block(vector=v_var)

        def s_rstd(s):
            s.activation(b.rstd_r[0:1, :], b.rstd_r[0:1, :], AF.Ln,
                         bias=b.eps_ln[0:1, 0:1])
            s.activation(b.rstd_r[0:1, :], b.rstd_r[0:1, :], AF.Exp, scale=-0.5)
        self.block(scalar=s_rstd)

        def v_nmr(v):
            v.tensor_tensor(b.nmr_r[0:1, :], b.nm_r[0:1, :], b.rstd_r[0:1, :],
                            ALU.mult)
        self.block(vector=v_nmr)

        def t_bcast(t):
            t.matmul(b.pA[:, :], b.ones_row_f[:, :], b.rstd_r[0:1, 0:512],
                     start=True, stop=True)
            t.matmul(b.pB[:, :], b.ones_row_f[:, :], b.rstd_r[0:1, 512:1024],
                     start=True, stop=True)
            t.matmul(b.pQK[:, :], b.ones_row_f[:, :], b.nmr_r[0:1, 0:512],
                     start=True, stop=True)
            t.matmul(b.pO[:, :], b.ones_row_f[:, :], b.nmr_r[0:1, 512:1024],
                     start=True, stop=True)
        self.block(tensor=t_bcast)

        def v_apply(v):
            for c in range(NCH):
                x = b.xs(c)
                xnv = b.xn[:, c * TOKS:(c + 1) * TOKS]
                v.tensor_tensor(b.tmp[:, 0:512], x[:, 0:512], b.pA[:, :], ALU.mult)
                v.tensor_tensor(b.tmp[:, 512:1024], x[:, 512:1024], b.pB[:, :], ALU.mult)
                v.tensor_tensor(xnv[:, 0:512], b.tmp[:, 0:512], b.pQK[:, :], ALU.add)
                v.tensor_tensor(xnv[:, 512:1024], b.tmp[:, 512:1024], b.pO[:, :], ALU.add)
        self.block(vector=v_apply)
        self.tap(f"xn_{tag}", b.xn, P, NCH * TOKS)

    def emit_ffn(self, tag):
        b = self
        banks4 = [b.pA, b.pB, b.pQK, b.pO]

        def t_lg(t):
            for tt in range(8):
                for c in range(NCH):
                    t.matmul(b.pA[:, tt * 16:tt * 16 + 8],
                             b.xn[:, c * TOKS + tt * P:c * TOKS + (tt + 1) * P],
                             b.wr_bf[:, c * 8:(c + 1) * 8],
                             start=(c == 0), stop=(c == NCH - 1), skip_group_check=True)
        self.block(tensor=t_lg)

        def v_sm1(v):
            for tt in range(8):
                v.tensor_tensor(b.lgs[:, tt * 8:(tt + 1) * 8],
                                b.pA[:, tt * 16:tt * 16 + 8], b.br_b[:, :], ALU.add)
            for tt in range(8):
                v.reduce_max(b.mxb[:, tt:tt + 1], b.lgs[:, tt * 8:(tt + 1) * 8], AX)
            for tt in range(8):
                v.tensor_scalar(b.exb[:, tt * 8:(tt + 1) * 8], b.lgs[:, tt * 8:(tt + 1) * 8],
                                b.mxb[:, tt:tt + 1], None, ALU.subtract)
        self.block(vector=v_sm1)

        def s_exp(s):
            s.activation(b.exb[:, :], b.exb[:, :], AF.Exp)
        self.block(scalar=s_exp)

        def v_sm2a(v):
            for tt in range(8):
                v.reduce_sum(b.seb[:, tt:tt + 1], b.exb[:, tt * 8:(tt + 1) * 8], AX)
        self.block(vector=v_sm2a)

        def s_recip(s):
            s.activation(b.wb[:, :], b.seb[:, :], AF.Ln)
            s.activation(b.wb[:, :], b.wb[:, :], AF.Exp, scale=-1.0)
        self.block(scalar=s_recip)

        def v_sm2(v):
            for tt in range(8):
                v.tensor_scalar(b.ohb[:, tt * 8:(tt + 1) * 8], b.lgs[:, tt * 8:(tt + 1) * 8],
                                b.mxb[:, tt:tt + 1], None, ALU.is_equal)
            for tt in range(8):
                v.tensor_scalar(b.mw_tm[:, tt * 8:(tt + 1) * 8],
                                b.ohb[:, tt * 8:(tt + 1) * 8],
                                b.wb[:, tt:tt + 1], FFN_SCALE, ALU.mult, ALU.mult)
        self.block(vector=v_sm2)
        self.tap(f"seb_{tag}", b.seb, P, 8)
        self.tap(f"lgs_{tag}", b.lgs, P, 64)
        self.tap(f"exb_{tag}", b.exb, P, 64)
        self.tap(f"ohb_{tag}", b.ohb, P, 64)
        self.tap(f"mxb_{tag}", b.mxb, P, 8)
        self.tap(f"wb_{tag}", b.wb, P, 8)
        self.tap(f"mwtm_{tag}", b.mw_tm, P, 64)

        def t_tr(t):
            for tt in range(8):
                dst = b.pO if tt < 4 else b.pU
                t.transpose(dst[0:8, (tt % 4) * P:(tt % 4 + 1) * P],
                            b.mw_tm[:, tt * 8:(tt + 1) * 8], b.idF32[:, :])
        self.block(tensor=t_tr)

        def v_mw(v):
            v.tensor_copy(b.mw_fm[0:8, 0:512], b.pO[0:8, :])
            v.tensor_copy(b.mw_fm[0:8, 512:1024], b.pU[0:8, :])
        self.block(vector=v_mw)

        for wave in range(2):
            def t_ex(t, wave=wave):
                for i in range(2):
                    rc = wave * 2 + i
                    for tt in range(2):
                        bank = banks4[i * 2 + tt]
                        t.matmul(bank[:, :], b.eexp[:, rc * P:(rc + 1) * P],
                                 b.mw_fm[0:8, tt * 512:(tt + 1) * 512],
                                 start=True, stop=True)
            self.block(tensor=t_ex)

            def v_ex(v, wave=wave):
                for i in range(2):
                    rc = wave * 2 + i
                    for tt in range(2):
                        bank = banks4[i * 2 + tt]
                        v.tensor_copy(
                            b.mask_b[:, rc * TOKS + tt * 512:rc * TOKS + (tt + 1) * 512],
                            bank[:, :])
            self.block(vector=v_ex)

        htiles = [(rc, tt) for rc in range(4) for tt in range(2)]
        bsets = [(b.pA, b.pB), (b.pQK, b.pO)]
        t0, v0 = self.c_t, self.c_v

        def t_h(t):
            for w in range(0, len(htiles), 2):
                if w >= 4:
                    t.wait_ge(b.sem_v, v0 + w // 2 - 1)
                for i, (rc, tt) in enumerate(htiles[w:w + 2]):
                    bank = bsets[(w // 2) % 2][i]
                    for c in range(NCH):
                        ins = t.matmul(bank[:, :],
                                       b.a_bf[:, c * 512 + rc * P:c * 512 + (rc + 1) * P],
                                       b.xn[:, c * TOKS + tt * 512:c * TOKS + (tt + 1) * 512],
                                       start=(c == 0), stop=(c == NCH - 1))
                b.c_t += 1
                ins.then_inc(b.sem_t, 1)

        def v_h(v):
            for w in range(0, len(htiles), 2):
                v.wait_ge(b.sem_t, t0 + w // 2 + 1)
                for i, (rc, tt) in enumerate(htiles[w:w + 2]):
                    bank = bsets[(w // 2) % 2][i]
                    sl = slice(rc * TOKS + tt * 512, rc * TOKS + (tt + 1) * 512)
                    ins = v.tensor_tensor(b.hm[:, sl], bank[:, :], b.mask_b[:, sl], ALU.mult)
                b.c_v += 1
                ins.then_inc(b.sem_v, 1)

        self.block(tensor=t_h, vector=v_h)
        self.tap(f"hm_{tag}", b.hm, P, 4 * TOKS)

        store = (tag == "ffn2")
        t0, v0 = self.c_t, self.c_v
        rotplan = []
        sp = b.spare
        sm = list(b.slotmap)
        for oc in range(NCH):
            rotplan.append((sm[oc], sp))
            sm[oc], sp = sp, sm[oc]

        def t_y(t):
            for oc in range(NCH):
                if oc >= 2:
                    t.wait_ge(b.sem_v, v0 + oc - 1)
                for tt in range(2):
                    bank = bsets[oc % 2][tt]
                    for rc in range(4):
                        ins = t.matmul(bank[:, :],
                                       b.b_bf[:, rc * 2048 + oc * P:rc * 2048 + (oc + 1) * P],
                                       b.hm[:, rc * TOKS + tt * 512:rc * TOKS + (tt + 1) * 512],
                                       start=(rc == 0), stop=(rc == 3))
                b.c_t += 1
                ins.then_inc(b.sem_t, 1)

        def v_y(v):
            for oc in range(NCH):
                v.wait_ge(b.sem_t, t0 + oc + 1)
                olds, news = rotplan[oc]
                for tt in range(2):
                    bank = bsets[oc % 2][tt]
                    ins = v.tensor_tensor(b.xslot[news][:, tt * 512:(tt + 1) * 512],
                                          bank[:, :],
                                          b.xslot[olds][:, tt * 512:(tt + 1) * 512], ALU.add)
                b.c_v += 1
                ins.then_inc(b.sem_v, 1)

        bodies = dict(tensor=t_y, vector=v_y)
        if store:
            def y_st(y):
                for oc in range(NCH):
                    y.wait_ge(b.sem_v, v0 + oc + 1)
                    self.ydma(y, b.out_t[oc * P:(oc + 1) * P, :],
                              b.xslot[rotplan[oc][1]][:, :])
                y.wait_ge(b.sem_y, b.exp_y)
            bodies["sync"] = y_st
        self.block(**bodies)
        for oc in range(NCH):
            b.rotate(oc)

    def emit_attn(self):
        b = self
        spill_map = list(self.slotmap)

        def s_spill(s):
            self.ydma(s, b.xspill.ap(), b.xall[:, :])
        self.block(scalar=s_spill)
        spill_cnt = b.exp_y

        self.emit_ln("attn")

        self.mark_phase("ln2_done")
        inproj = (self.cc_mode == "inproj")
        self.emit_proj(0, 0, extra_wait=spill_cnt)
        self.mark_phase("proj0_done")
        self.emit_amath(0, 0, pre_issue=1)
        if not inproj:
            self.emit_cc_issue(0)
        for g in range(NG):
            gb = g % 2
            if g + 1 < NG:
                self.emit_proj(g + 1, (g + 1) % 2, cc_pre=(g if inproj else None),
                               preissued=True)
                self.mark_phase(f"proj{g+1}_done")
            elif inproj:
                self.emit_cc_issue(g)
            self.emit_cc_fixup(g, gb)
            self.mark_phase(f"fix{g}_done")
            self.emit_bmath(g, gb)
            self.mark_phase(f"bmath{g}_done")
            if g + 1 < NG:
                self.emit_amath(g + 1, (g + 1) % 2,
                                pre_issue=(g + 2 if g + 2 < NG else None))
                if not inproj:
                    self.emit_cc_issue(g + 1)
                self.mark_phase(f"amath{g+1}_done")

        self.tap("o", b.o_bf, P, NH * TOKS)

        def s_rv(s):
            s.activation(b.rstd_r[0:1, :], b.pV0[0:1, :], AF.Ln,
                         bias=b.eps_rms[0:1, 0:1], scale=1.0 / 2048.0)
        self.block(scalar=s_rv)

        def s_rv2(s):
            s.activation(b.rstd_r[0:1, :], b.rstd_r[0:1, :], AF.Exp, scale=-0.5)
        self.block(scalar=s_rv2)

        def t_rvb(t):
            t.matmul(b.pQK[:, :], b.ones_row_f[:, :], b.rstd_r[0:1, 0:512],
                     start=True, stop=True)
            t.matmul(b.pU[:, :], b.ones_row_f[:, :], b.rstd_r[0:1, 512:1024],
                     start=True, stop=True)
        self.block(tensor=t_rvb)

        def v_rvb(v):
            v.tensor_copy(b.rstd_sb[:, 0:512], b.pQK[:, :])
            v.tensor_copy(b.rstd_sb[:, 512:1024], b.pU[:, :])
        self.block(vector=v_rvb)

        def y_rest(y):
            self.ydma(y, b.xall[:, :], b.xspill.ap())
        self.block(sync=y_rest)
        restore_cnt = b.exp_y
        self.slotmap = spill_map
        self.spare = [i for i in range(NSLOT) if i not in spill_map][0]

        self.emit_gate()
        self.mark_phase("gate_done")
        self.tap("os", b.o_bf, P, NH * TOKS)

        self.emit_wout(restore_cnt)
        self.mark_phase("wout_done")
        self.tap("x2", self.xs(0), P, TOKS)

    def emit_proj(self, g, gb, extra_wait=0, cc_pre=None, preissued=False):
        b = self
        kg, vg, qg = b.k_g[gb], b.v_g[gb], b.q_g[gb]
        pieces = []
        for which, colbase in (("k", 2048 + g * 512), ("v", 4096 + g * 512),
                               ("q", 0 + g * 512)):
            for half in range(2):
                pieces.append((which, colbase + half * 256, half))

        waves = []
        for p, (which, cb, half) in enumerate(pieces):
            ws = b.wstr[p % 2]
            if which in ("k", "q"):
                dst = kg if which == "k" else qg
                for hi2 in range(2):
                    hi = half * 2 + hi2

                    def mm(t, banks, ws=ws, hi2=hi2):
                        for tt in range(2):
                            for c in range(NCH):
                                ins = t.matmul(
                                    banks[tt][:, :],
                                    ws[:, c * 256 + hi2 * P:c * 256 + (hi2 + 1) * P],
                                    b.xn[:, c * TOKS + tt * 512:c * TOKS + (tt + 1) * 512],
                                    start=(c == 0), stop=(c == NCH - 1))
                        return ins

                    def ev(s, banks, dst=dst, hi=hi):
                        for tt in range(2):
                            ins = s.activation(
                                dst[:, hi * TOKS + tt * 512:hi * TOKS + (tt + 1) * 512],
                                banks[tt][:, :], AF.Silu)
                        return ins
                    waves.append((p, mm, ev))
            else:
                for wv in range(4):
                    def mm(t, banks, ws=ws, wv=wv):
                        for i in range(2):
                            tt = wv * 2 + i
                            for c in range(NCH):
                                ins = t.matmul(
                                    banks[i][:, 0:256],
                                    b.xn[:, c * TOKS + tt * P:c * TOKS + (tt + 1) * P],
                                    ws[:, c * 256:(c + 1) * 256],
                                    start=(c == 0), stop=(c == NCH - 1))
                        return ins

                    def ev(s, banks, vg=vg, wv=wv, half=half):
                        for i in range(2):
                            tt = wv * 2 + i
                            ins = s.activation(
                                vg[:, tt * 512 + half * 256:tt * 512 + (half + 1) * 256],
                                banks[i][:, 0:256], AF.Silu)
                        return ins
                    waves.append((p, mm, ev))

        nw = len(waves)
        bsets = [(b.pA, b.pB), (b.pQK, b.pO)]
        t0, s0 = self.c_t, self.c_s
        piece_first = {}
        piece_last = {}
        for w, (p, _, _) in enumerate(waves):
            piece_first.setdefault(p, w)
            piece_last[p] = w
        if preissued:
            piece_cnt = [self.exp_w[p % 2] + 16 * (p // 2) for p in range(len(pieces))]
        else:
            piece_cnt = [self.exp_w[p % 2] + 16 * (p // 2 + 1) for p in range(len(pieces))]

        skip = 2 if preissued else 0

        def g_body(gq):
            if b.precast:
                gq.wait_ge(b.sem_cast, 48)
            if extra_wait:
                gq.wait_ge(b.sem_y, extra_wait)
            for p, (which, cb, half) in enumerate(pieces):
                if p < skip:
                    continue
                if p >= 2:
                    gq.wait_ge(b.sem_t, t0 + piece_last[p - 1] + 1)
                pid = cb // 256
                srcap = bass.AP(b.wqkv, pid * 4096, [[24 * 4096, P], [1, 4096]])
                b.wdma(gq, p % 2, b.wstr[p % 2][:, :], srcap)

        def t_body(t):
            for w, (p, mm, _) in enumerate(waves):
                if w == piece_first[p]:
                    t.wait_ge(b.sem_w[p % 2], piece_cnt[p])
                if w >= 2:
                    t.wait_ge(b.sem_s, s0 + w - 1)
                ins = mm(t, bsets[w % 2])
                b.c_t += 1
                ins.then_inc(b.sem_t, 1)

        def s_body(s):
            for w, (p, _, ev) in enumerate(waves):
                s.wait_ge(b.sem_t, t0 + w + 1)
                ins = ev(s, bsets[w % 2])
                b.c_s += 1
                ins.then_inc(b.sem_s, 1)

        def g_cc(gq):
            if cc_pre is not None:
                gq.collective_compute(
                    "AllGather", ALU.bypass,
                    replica_groups=[[0, 1], [2, 3], [4, 5], [6, 7]],
                    ins=[b.cc_in[cc_pre].ap()], outs=[b.cc_out[cc_pre].ap()],
                ).then_inc(b.sem_cc)
                b.exp_cc += 1

        def g_all(gq):
            g_cc(gq)
            g_body(gq)
        self.block(gpsimd=g_all, tensor=t_body, scalar=s_body)
        if g == 0:
            self.tap("k_g0", kg, P, HG * TOKS)
            self.tap("v_g0", vg, P, HG * TOKS)
            self.tap("q_g0", qg, P, HG * TOKS)

    def emit_amath(self, g, gb, pre_issue=None):
        b = self
        kg, vg = b.k_g[gb], b.v_g[gb]
        Sg = b.S_g[gb]
        ubank = [b.pU, b.pA]
        TTR = [0] * HG
        TU = [0] * HG
        VKS = [0] * HG
        VSC = [0] * HG
        ct = self.c_t
        for hi in range(HG):
            ct += 1; TTR[hi] = ct
            if hi >= 1:
                ct += 1; TU[hi - 1] = ct
        ct += 1; TU[HG - 1] = ct
        cv = self.c_v
        for hi in range(HG):
            cv += 1; VKS[hi] = cv
            if hi >= 1:
                cv += 1; VSC[hi - 1] = cv
        cv += 1; VSC[HG - 1] = cv

        def emit_tr(t, hi):
            for tt in range(8):
                ins = t.transpose(b.pT[:, tt * P:(tt + 1) * P],
                                  kg[:, hi * TOKS + tt * P:hi * TOKS + (tt + 1) * P],
                                  b.idBF[:, :])
            return ins

        def emit_u(t, hi):
            for j in range(NBLK):
                for mc in range(2):
                    tt = 2 * j + mc
                    ins = t.matmul(ubank[hi % 2][:, j * DH:(j + 1) * DH],
                                   b.ks_h[hi % 2][:, tt * P:(tt + 1) * P],
                                   vg[:, tt * 512 + hi * DH:tt * 512 + (hi + 1) * DH],
                                   start=(mc == 0), stop=(mc == 1))
            return ins

        def t_body(t):
            for hi in range(HG):
                if hi >= 1:
                    t.wait_ge(b.sem_v, VKS[hi - 1])
                ins = emit_tr(t, hi)
                ins.then_inc(b.sem_t, 1)
                if hi >= 1:
                    ins = emit_u(t, hi - 1)
                    ins.then_inc(b.sem_t, 1)
            t.wait_ge(b.sem_v, VKS[HG - 1])
            ins = emit_u(t, HG - 1)
            ins.then_inc(b.sem_t, 1)
            b.c_t = ct

        def emit_ks(v, hi):
            hh = g * HG + hi
            for tt in range(8):
                col = (tt % 2) * 16 + hh
                ins = v.tensor_scalar(b.ks_h[hi % 2][:, tt * P:(tt + 1) * P],
                                      b.pT[:, tt * P:(tt + 1) * P],
                                      b.kd_colsT[:, col:col + 1], None, ALU.mult)
            return ins

        def emit_scan(v, hi):
            hh = g * HG + hi
            base = hi * 4 * DH
            v.tensor_copy(Sg[:, base:base + DH], ubank[hi % 2][:, 0:DH])
            for j in range(1, NBLK):
                ins = v.scalar_tensor_tensor(Sg[:, base + j * DH:base + (j + 1) * DH],
                                             Sg[:, base + (j - 1) * DH:base + j * DH],
                                             b.bdp_b[:, hh * NBLK + 1:hh * NBLK + 2],
                                             ubank[hi % 2][:, j * DH:(j + 1) * DH],
                                             ALU.mult, ALU.add)
            return ins

        def v_body(v):
            for hi in range(HG):
                v.wait_ge(b.sem_t, TTR[hi])
                ins = emit_ks(v, hi)
                ins.then_inc(b.sem_v, 1)
                if hi >= 1:
                    v.wait_ge(b.sem_t, TU[hi - 1])
                    ins = emit_scan(v, hi - 1)
                    ins.then_inc(b.sem_v, 1)
            v.wait_ge(b.sem_t, TU[HG - 1])
            ins = emit_scan(v, HG - 1)
            ins.then_inc(b.sem_v, 1)
            b.c_v = cv

        def g_body(gq):
            if pre_issue is not None:
                ng = pre_issue
                for p in range(2):
                    pid = (2048 + p * 256 + ng * 512) // 256
                    srcap = bass.AP(b.wqkv, pid * 4096, [[24 * 4096, P], [1, 4096]])
                    b.wdma(gq, p, b.wstr[p][:, :], srcap)
            for hi in range(HG):
                gq.wait_ge(b.sem_v, VSC[hi])
                base = hi * 4 * DH
                b.gdma(gq, b.cc_in[g][hi * P:(hi + 1) * P, :],
                       Sg[:, base + 3 * DH:base + 4 * DH])
            gq.wait_ge(b.sem_g, b.exp_g)

        self.block(tensor=t_body, vector=v_body, gpsimd=g_body)
        if g == 0:
            self.tap("S_g0", Sg, P, HG * 4 * DH)

    def emit_cc_issue(self, g):
        b = self

        def g_cc(gq):
            gq.collective_compute(
                "AllGather", ALU.bypass,
                replica_groups=[[0, 1], [2, 3], [4, 5], [6, 7]],
                ins=[b.cc_in[g].ap()], outs=[b.cc_out[g].ap()],
            ).then_inc(b.sem_cc)
            b.exp_cc += 1
        self.block(gpsimd=g_cc)

    def emit_cc_fixup(self, g, gb):
        b = self
        Sg, Tg, kvi = b.S_g[gb], b.T_g[gb], b.kvi_g[gb]

        def g_cc(gq):
            gq.wait_ge(b.sem_cc, g + 1)
            b.gdma(gq, b.gtmp.ap().rearrange("p (h e) -> p h e", h=HG),
                   bass.AP(b.cc_out[g], 0, [[DH, P], [P * DH, HG], [1, DH]]))
            gq.wait_ge(b.sem_g, b.exp_g)
        self.block(gpsimd=g_cc)

        def v_kvi(v):
            v.tensor_scalar(kvi[:, :], b.gtmp[:, :], b.flag_c[:, 0:1], None, ALU.mult)
        self.block(vector=v_kvi)

        def v_fix(v):
            for hi in range(HG):
                hh = g * HG + hi
                base = hi * 4 * DH
                v.tensor_copy(Tg[:, base:base + DH], kvi[:, hi * DH:(hi + 1) * DH])
                for j in range(1, NBLK):
                    v.scalar_tensor_tensor(Tg[:, base + j * DH:base + (j + 1) * DH],
                                           kvi[:, hi * DH:(hi + 1) * DH],
                                           b.bdp_b[:, hh * NBLK + j:hh * NBLK + j + 1],
                                           Sg[:, base + (j - 1) * DH:base + j * DH],
                                           ALU.mult, ALU.add)
        self.block(vector=v_fix)
        if g == 0:
            self.tap("T_g0", Tg, P, HG * 4 * DH)

    def emit_bmath(self, g, gb):
        b = self
        kg, vg, qg = b.k_g[gb], b.v_g[gb], b.q_g[gb]
        Tg = b.T_g[gb]

        def g_qd(gq):
            b.gdma(gq, b.qdB_g.ap().rearrange("p (h l) -> p h l", h=HG),
                   bass.AP(b.qd_dram, g * HG * BLK, [[0, P], [BLK, HG], [1, BLK]]))
            gq.wait_ge(b.sem_g, b.exp_g)
        self.block(gpsimd=g_qd)

        def v_d(v):
            for hi in range(HG):
                hh = g * HG + hi
                v.tensor_scalar(b.D_g[:, hi * 512:(hi + 1) * 512], b.idx_c[:, :],
                                b.s_bc[:, hh:hh + 1], None, ALU.mult)
        self.block(vector=v_d)

        def s_d(s):
            for hi in range(HG):
                s.activation(b.D_g[:, hi * 512:(hi + 1) * 512],
                             b.D_g[:, hi * 512:(hi + 1) * 512], AF.Exp)
        self.block(scalar=s_d)
        if g == 0:
            self.tap("D_g0", b.D_g, P, HG * 512)
            self.tap("qdB_g0", b.qdB_g, P, HG * BLK)

        NI = HG * NBLK
        qkbank = [b.pQK, b.pU]
        obank = [b.pA, b.pO]
        TQK = [0] * NI
        TO = [0] * NI
        TV = [0] * NI
        VOE = [0] * NI
        VQK = [0] * NI
        SQ = [0] * NI
        ct = self.c_t
        for i in range(NI):
            ct += 1; TQK[i] = ct
            if i >= 1:
                ct += 1; TO[i - 1] = ct
            if i >= 4:
                ct += 1; TV[i - 4] = ct
        ct += 1; TO[NI - 1] = ct
        for j in range(NI - 4, NI):
            ct += 1; TV[j] = ct
        cv = self.c_v
        for i in range(NI):
            if i >= 1:
                cv += 1; VOE[i - 1] = cv
            cv += 1; VQK[i] = cv
        cv += 1; VOE[NI - 1] = cv
        cs = self.c_s
        for j in range(NI):
            cs += 1; SQ[j] = cs

        def mm_qk(t, i):
            hi, blk = i // NBLK, i % NBLK
            for mc in range(2):
                tt = 2 * blk + mc
                ins = t.matmul(qkbank[i % 2][:, mc * 256:(mc + 1) * 256],
                               kg[:, hi * TOKS + tt * P:hi * TOKS + (tt + 1) * P],
                               qg[:, hi * TOKS + blk * BLK:hi * TOKS + (blk + 1) * BLK],
                               start=True, stop=True)
            return ins

        def mm_o(t, j):
            hi, blk = j // NBLK, j % NBLK
            for mc in range(2):
                tt = 2 * blk + mc
                t.matmul(obank[j % 2][:, 0:256],
                         vg[:, tt * 512 + hi * DH:tt * 512 + (hi + 1) * DH],
                         b.qkD[j % 2][:, mc * 256:(mc + 1) * 256],
                         start=(mc == 0), stop=False)
            base = hi * 4 * DH
            return t.matmul(obank[j % 2][:, 0:256],
                            Tg[:, base + blk * DH:base + (blk + 1) * DH],
                            b.qs_b[j % 2][:, :], start=False, stop=True)

        def mm_var(t, j):
            hi, blk = j // NBLK, j % NBLK
            hh = g * HG + hi
            return t.matmul(b.pV0[0:1, blk * BLK:(blk + 1) * BLK],
                            b.onesP_bf[:, :], b.osq[j % 2][:, :],
                            start=(hh == 0 and blk in (0, 2)),
                            stop=(hh == NH - 1 and blk in (1, 3)),
                            skip_group_check=True)

        def t_body(t):
            def do_o(j):
                t.wait_ge(b.sem_v, VQK[j])
                if j >= 2:
                    t.wait_ge(b.sem_v, VOE[j - 2])
                mm_o(t, j).then_inc(b.sem_t, 1)

            def do_var(j):
                t.wait_ge(b.sem_s, SQ[j])
                mm_var(t, j).then_inc(b.sem_t, 1)

            for i in range(NI):
                if i >= 2:
                    t.wait_ge(b.sem_v, VQK[i - 2])
                mm_qk(t, i).then_inc(b.sem_t, 1)
                if i >= 1:
                    do_o(i - 1)
                if i >= 4:
                    do_var(i - 4)
            do_o(NI - 1)
            for j in range(NI - 4, NI):
                do_var(j)
            b.c_t = ct

        def v_body(v):
            def do_oe(j):
                hi, blk = j // NBLK, j % NBLK
                hh = g * HG + hi
                v.wait_ge(b.sem_t, TO[j])
                ins = v.tensor_copy(
                    b.o_bf[:, hh * TOKS + blk * BLK:hh * TOKS + (blk + 1) * BLK],
                    obank[j % 2][:, 0:256])
                ins.then_inc(b.sem_v, 1)

            for i in range(NI):
                if i >= 1:
                    do_oe(i - 1)
                hi, blk = i // NBLK, i % NBLK
                v.wait_ge(b.sem_t, TQK[i])
                v.tensor_tensor(b.qkD[i % 2][:, :], qkbank[i % 2][:, :],
                                b.D_g[:, hi * 512:(hi + 1) * 512], ALU.mult)
                ins = v.tensor_tensor(
                    b.qs_b[i % 2][:, :],
                    qg[:, hi * TOKS + blk * BLK:hi * TOKS + (blk + 1) * BLK],
                    b.qdB_g[:, hi * BLK:(hi + 1) * BLK], ALU.mult)
                ins.then_inc(b.sem_v, 1)
            do_oe(NI - 1)
            b.c_v = cv

        def s_body(s):
            for j in range(NI):
                hi, blk = j // NBLK, j % NBLK
                hh = g * HG + hi
                s.wait_ge(b.sem_v, VOE[j])
                if j >= 2:
                    s.wait_ge(b.sem_t, TV[j - 2])
                ins = s.activation(
                    b.osq[j % 2][:, :],
                    b.o_bf[:, hh * TOKS + blk * BLK:hh * TOKS + (blk + 1) * BLK],
                    AF.Square)
                ins.then_inc(b.sem_s, 1)
            b.c_s = cs

        self.block(tensor=t_body, vector=v_body, scalar=s_body)

    def emit_gate(self):
        b = self
        bset = [(b.pA, b.pB), (b.pQK, b.pO)]
        t0, s0, v0 = self.c_t, self.c_s, self.c_v
        piece_cnt = [self.exp_w[p % 2] + 16 * (p // 2 + 1) for p in range(8)]

        def g_body(gq):
            for p in range(8):
                if p >= 2:
                    gq.wait_ge(b.sem_t, t0 + 2 * (p - 1))
                srcap = bass.AP(b.wgate, p * 4096, [[8 * 4096, P], [1, 4096]])
                b.wdma(gq, p % 2, b.gst[p % 2][:, :], srcap)

        def t_body(t):
            for oc in range(NCH):
                p, oc2 = oc // 2, oc % 2
                ws = b.gst[p % 2]
                if oc2 == 0:
                    t.wait_ge(b.sem_w[p % 2], piece_cnt[p])
                if oc >= 2:
                    t.wait_ge(b.sem_s, s0 + oc - 1)
                for tt in range(2):
                    bank = bset[oc % 2][tt]
                    for c in range(NCH):
                        ins = t.matmul(bank[:, :],
                                       ws[:, c * 256 + oc2 * P:c * 256 + (oc2 + 1) * P],
                                       b.xn[:, c * TOKS + tt * 512:c * TOKS + (tt + 1) * 512],
                                       start=(c == 0), stop=(c == NCH - 1))
                b.c_t += 1
                ins.then_inc(b.sem_t, 1)

        def s_body(s):
            for oc in range(NCH):
                s.wait_ge(b.sem_t, t0 + oc + 1)
                if oc >= 2:
                    s.wait_ge(b.sem_v, v0 + oc - 1)
                for tt in range(2):
                    ins = s.activation(b.sg_b[oc % 2][:, tt * 512:(tt + 1) * 512],
                                       bset[oc % 2][tt][:, :], AF.Sigmoid)
                b.c_s += 1
                ins.then_inc(b.sem_s, 1)

        def v_body(v):
            for oc in range(NCH):
                v.wait_ge(b.sem_s, s0 + oc + 1)
                for tt in range(2):
                    v.scalar_tensor_tensor(
                        b.t2_b[0][:, tt * 512:(tt + 1) * 512],
                        b.sg_b[oc % 2][:, tt * 512:(tt + 1) * 512], b.rmsw[:, oc:oc + 1],
                        b.o_bf[:, oc * TOKS + tt * 512:oc * TOKS + (tt + 1) * 512],
                        ALU.mult, ALU.mult)
                for tt in range(2):
                    ins = v.tensor_tensor(
                        b.o_bf[:, oc * TOKS + tt * 512:oc * TOKS + (tt + 1) * 512],
                        b.t2_b[0][:, tt * 512:(tt + 1) * 512],
                        b.rstd_sb[:, tt * 512:(tt + 1) * 512], ALU.mult)
                b.c_v += 1
                ins.then_inc(b.sem_v, 1)

        self.block(gpsimd=g_body, tensor=t_body, scalar=s_body, vector=v_body)

    def emit_wout(self, restore_cnt=0):
        b = self
        bset = [(b.pA, b.pB), (b.pQK, b.pO)]
        t0, v0 = self.c_t, self.c_v
        piece_cnt = [self.exp_w[p % 2] + 16 * (p // 2 + 1) for p in range(8)]

        def g_body(gq):
            for p in range(8):
                if p >= 2:
                    gq.wait_ge(b.sem_t, t0 + 2 * (p - 1))
                srcap = bass.AP(b.woutw, p * 4096, [[8 * 4096, P], [1, 4096]])
                b.wdma(gq, p % 2, b.gst[p % 2][:, :], srcap)

        def t_body(t):
            for oc in range(NCH):
                p, oc2 = oc // 2, oc % 2
                ws = b.gst[p % 2]
                if oc2 == 0:
                    t.wait_ge(b.sem_w[p % 2], piece_cnt[p])
                if oc >= 2:
                    t.wait_ge(b.sem_v, v0 + oc - 1)
                for tt in range(2):
                    bank = bset[oc % 2][tt]
                    for c in range(NCH):
                        ins = t.matmul(bank[:, :],
                                       ws[:, c * 256 + oc2 * P:c * 256 + (oc2 + 1) * P],
                                       b.o_bf[:, c * TOKS + tt * 512:c * TOKS + (tt + 1) * 512],
                                       start=(c == 0), stop=(c == NCH - 1))
                b.c_t += 1
                ins.then_inc(b.sem_t, 1)

        def v_body(v):
            for oc in range(NCH):
                v.wait_ge(b.sem_t, t0 + oc + 1)
                if oc == 0 and restore_cnt:
                    v.wait_ge(b.sem_y, restore_cnt)
                new = b.xslot[b.spare]
                old = b.xs(oc)
                for tt in range(2):
                    bank = bset[oc % 2][tt]
                    ins = v.tensor_tensor(new[:, tt * 512:(tt + 1) * 512], bank[:, :],
                                          old[:, tt * 512:(tt + 1) * 512], ALU.add)
                b.rotate(oc)
                b.c_v += 1
                ins.then_inc(b.sem_v, 1)

        self.block(gpsimd=g_body, tensor=t_body, vector=v_body)

    def emit_store(self):
        b = self

        def y_out(y):
            for c in range(NCH):
                self.ydma(y, b.out_t[c * P:(c + 1) * P, :], b.xs(c)[:, :])
            y.wait_ge(b.sem_y, b.exp_y)
        self.block(sync=y_out)

    def mark_phase(self, name):
        self.phase_marks.append((name, self.nc.n_instructions()))

    def build(self, upto="all"):
        self.phase_marks = []
        self.mark_phase("start")
        self.emit_setup()
        self.mark_phase("setup_done")
        self.load_ffn_weights()
        self.emit_ln("ffn1")
        self.mark_phase("ln1_done")
        if upto == "ln1":
            self.emit_store(); self.es.close(); return self.nc
        self.emit_ffn("ffn1")
        self.mark_phase("ffn1_done")
        self.tap("x1", self.xs(0), P, TOKS)
        if upto == "ffn1":
            self.emit_store(); self.es.close(); return self.nc
        self.emit_attn()
        self.mark_phase("attn_done")
        if upto == "attn":
            self.emit_store(); self.es.close(); return self.nc
        self.load_ffn_weights()
        self.emit_ln("ffn2")
        self.mark_phase("ln3_done")
        self.emit_ffn("ffn2")
        self.mark_phase("ffn2_done")
        self.es.close()
        return self.nc



def _host_consts():
    m = np.arange(BLK)
    l = np.arange(BLK)
    idx = (m[:, None] - l[None, :]).astype(np.float32)
    idx[m[:, None] > l[None, :]] = NEG_BIG
    idx_c = np.concatenate([idx[0:128, :], idx[128:256, :]], axis=1).astype(np.float32)
    gq = np.broadcast_to(-(l + 1).astype(np.float32), (NH, BLK)).copy()
    gk = np.broadcast_to(-(255 - m).astype(np.float32), (NH, BLK)).copy()
    gbd = np.broadcast_to((-256.0 * np.arange(NBLK)).astype(np.float32), (NH, NBLK)).copy()
    eexp = np.zeros((8, 512), np.float32)
    for j in range(512):
        eexp[j // 64, j] = 1.0
    return idx_c, gq, gk, gbd, eexp


def build_inputs(inputs):
    x = np.asarray(inputs["L0_attn_hidden_states"], np.float32)
    def stream_layout(w):
        o = w.shape[1]
        t = w.reshape(NCH, P, o // 256, 256).transpose(1, 2, 0, 3)
        return np.ascontiguousarray(t.reshape(P, -1))

    wqkv = stream_layout(np.asarray(inputs["L0_attn_w_qkv"], np.float32))
    wgate = stream_layout(np.asarray(inputs["L0_attn_w_gate"], np.float32))
    woutw = stream_layout(np.asarray(inputs["L0_attn_w_out"], np.float32))
    rmsw = np.asarray(inputs["L0_attn_rms_weight"], np.float32)
    slope = np.asarray(inputs["L0_attn_slope_rate"], np.float32).reshape(NH)
    A = np.asarray(inputs["L0_ffn_A"], np.float32)
    B = np.asarray(inputs["L0_ffn_B"], np.float32)
    Wr = np.asarray(inputs["L0_ffn_Wr"], np.float32)
    br = np.asarray(inputs["L0_ffn_br"], np.float32)

    a_t = np.ascontiguousarray(A.reshape(512, 2048).T)
    b_t = np.ascontiguousarray(B.transpose(0, 2, 1).reshape(512, 2048))
    wr_t = np.ascontiguousarray(Wr.T)
    rmsw_fm = np.ascontiguousarray(rmsw.reshape(NCH, P).T)
    idx_c, gq, gk, gbd, eexp = _host_consts()

    common = dict(
        wqkv=wqkv, wgate=wgate, woutw=woutw, a_t=a_t, b_t=b_t, wr_t=wr_t,
        br_in=np.ascontiguousarray(br.reshape(1, 8)), rmsw_in=rmsw_fm,
        s_col_in=np.ascontiguousarray(slope.reshape(NH, 1)),
        idx_in=idx_c, gq_in=gq, gk_in=gk, gbd_in=gbd, eexp_in=eexp,
    )
    in_maps = []
    for c in range(8):
        bidx, half = c // 2, c % 2
        xc = np.ascontiguousarray(x[bidx, half * TOKS:(half + 1) * TOKS, :].T)
        flag = np.full((P, 1), 1.0 if half == 1 else 0.0, np.float32)
        m = dict(common)
        m["x_in"] = xc
        m["flag_in"] = flag
        in_maps.append(m)
    return in_maps


def kernel(**inputs):
    in_maps = build_inputs(inputs)
    bld = Builder()
    nc = bld.build()
    res = run_bass_kernel_spmd(nc, in_maps, core_ids=list(range(8)))
    out = np.empty((4, 2048, 2048), np.float32)
    for c in range(8):
        bidx, half = c // 2, c % 2
        out[bidx, half * TOKS:(half + 1) * TOKS, :] = res.results[c]["out_t"].T
    return out


# version 10
# speedup vs baseline: 1.2362x; 1.0053x over previous
import numpy as np
from contextlib import ExitStack

import concourse.bass as bass
import concourse.mybir as mybir
from concourse.bass_utils import run_bass_kernel_spmd

F32 = mybir.dt.float32
BF16 = mybir.dt.bfloat16
AF = mybir.ActivationFunctionType
ALU = mybir.AluOpType
AX = mybir.AxisListType.X

P = 128
TOKS = 1024
NCH = 16
NH = 16
HG = 4
NG = NH // HG
BLK = 256
NBLK = TOKS // BLK
DH = 128
FFN_SCALE = 0.5 / 64.0
NEG_BIG = -1.0e9

SB0 = 16512
X_BASE = SB0
X_SLOT = 4096
NSLOT = 17
XN_BASE = SB0 + 69632
OB_BASE = SB0 + 102400
WK_BASE = SB0 + 135168
CONST_BASE = SB0 + 196608

KVQ = [X_BASE, X_BASE + 24576]
WSTR = [X_BASE + 49152, X_BASE + 57344]

WK_S = [WK_BASE, WK_BASE + 8192]
WK_T = [WK_BASE + 16384, WK_BASE + 20480]
WK_KVI = [WK_BASE + 24576, WK_BASE + 26624]
WK_D = WK_BASE + 28672
WK_QDB = WK_BASE + 36864
WK_KS = [WK_BASE + 40960, WK_BASE + 43008]
WK_QS = [WK_BASE + 45056, WK_BASE + 45568]
WK_QKD = [WK_BASE + 46080, WK_BASE + 47104]
WK_GTMP = WK_BASE + 48128
WK_OSQ = [WK_BASE + 50176, WK_BASE + 50688]

WK_ABF = WK_BASE
WK_BBF = WK_BASE + 16384
WK_WRBF = WK_BASE + 32768
WK_XB = WK_BASE + 33280
WK_SQ = WK_BASE + 41472
WK_TMP = WK_BASE + 49664
WK_ROWA = WK_BASE + 53760
WK_LGS = WK_BASE + 57856
WK_EXB = WK_BASE + 58112
WK_OHB = WK_BASE + 58368
WK_MXB = WK_BASE + 58624
WK_NMX = WK_BASE + 58656
WK_SEB = WK_BASE + 58688
WK_WB = WK_BASE + 58720
WK_GST = [WK_BASE, WK_BASE + 8192]
WK_SG = [WK_BASE + 16384, WK_BASE + 18432]
WK_T2 = [WK_BASE + 20480, WK_BASE + 22528]

OB_MASKB = OB_BASE
OB_HM = OB_BASE + 16384
OB_MWTM = OB_BASE + 24576
OB_MWFM = OB_BASE + 24832

_CO = [CONST_BASE]


def _co(nbytes, align=64):
    _CO[0] = (_CO[0] + align - 1) // align * align
    off = _CO[0]
    _CO[0] += nbytes
    return off


class Builder:
    def __init__(self, taps=(), precast=False, cc_mode="block"):
        self.precast = precast
        self.cc_mode = cc_mode
        self.taps = set(taps)
        _CO[0] = CONST_BASE
        nc = bass.Bass("TRN2", target_bir_lowering=False)
        self.nc = nc
        self.es = ExitStack()

        dt = nc.dram_tensor
        self.x_in = dt("x_in", [2048, TOKS], F32, kind="ExternalInput")
        self.wqkv = dt("wqkv", [P, 24 * 4096], F32, kind="ExternalInput")
        self.wgate = dt("wgate", [P, 8 * 4096], F32, kind="ExternalInput")
        self.woutw = dt("woutw", [P, 8 * 4096], F32, kind="ExternalInput")
        self.a_t = dt("a_t", [2048, 512], F32, kind="ExternalInput")
        self.b_t = dt("b_t", [512, 2048], F32, kind="ExternalInput")
        self.wr_t = dt("wr_t", [2048, 8], F32, kind="ExternalInput")
        self.br_in = dt("br_in", [1, 8], F32, kind="ExternalInput")
        self.rmsw_in = dt("rmsw_in", [P, NCH], F32, kind="ExternalInput")
        self.s_col_in = dt("s_col_in", [NH, 1], F32, kind="ExternalInput")
        self.flag_in = dt("flag_in", [P, 1], F32, kind="ExternalInput")
        self.idx_in = dt("idx_in", [P, 512], F32, kind="ExternalInput")
        self.gq_in = dt("gq_in", [NH, BLK], F32, kind="ExternalInput")
        self.gk_in = dt("gk_in", [NH, BLK], F32, kind="ExternalInput")
        self.gbd_in = dt("gbd_in", [NH, NBLK], F32, kind="ExternalInput")
        self.eexp_in = dt("eexp_in", [8, 512], F32, kind="ExternalInput")
        self.out_t = dt("out_t", [2048, TOKS], F32, kind="ExternalOutput")
        self.xspill = dt("xspill", [P, NSLOT * TOKS], F32)
        self.wqkv_bf = dt("wqkv_bf", [2048, 6144], BF16)
        self.wg_bf = dt("wg_bf", [2048, 2048], BF16)
        self.wo_bf = dt("wo_bf", [2048, 2048], BF16)
        self.qd_dram = dt("qd_dram", [NH, BLK], F32)
        self.bdp_dram = dt("bdp_dram", [NH, NBLK], F32)
        self.cc_in = [dt(f"cc_in{g}", [HG * P, DH], F32) for g in range(NG)]
        self.cc_out = [dt(f"cc_out{g}", [2 * HG * P, DH], F32) for g in range(NG)]

        at = nc.alloc_sbuf_tensor_at
        self.xslot = [at(f"xslot{i}", [P, TOKS], F32, offset=X_BASE + i * X_SLOT)
                      for i in range(NSLOT)]
        self.xall = at("xall", [P, NSLOT * TOKS], F32, offset=X_BASE)
        self.slotmap = list(range(NCH))
        self.spare = NCH
        self.xn = at("xn", [P, NCH * TOKS], BF16, offset=XN_BASE)
        self.o_bf = at("o_bf", [P, NH * TOKS], BF16, offset=OB_BASE)
        self.k_g = [at(f"k_g{i}", [P, HG * TOKS], BF16, offset=KVQ[i]) for i in range(2)]
        self.v_g = [at(f"v_g{i}", [P, HG * TOKS], BF16, offset=KVQ[i] + 8192)
                    for i in range(2)]
        self.q_g = [at(f"q_g{i}", [P, HG * TOKS], BF16, offset=KVQ[i] + 16384)
                    for i in range(2)]
        self.wstr = [at(f"wstr{i}", [P, NCH * 256], BF16, offset=WSTR[i]) for i in range(2)]
        self.S_g = [at(f"S_g{i}", [P, HG * 4 * DH], F32, offset=WK_S[i]) for i in range(2)]
        self.T_g = [at(f"T_g{i}", [P, HG * 4 * DH], BF16, offset=WK_T[i]) for i in range(2)]
        self.kvi_g = [at(f"kvi_g{i}", [P, HG * DH], F32, offset=WK_KVI[i]) for i in range(2)]
        self.D_g = at("D_g", [P, HG * 512], F32, offset=WK_D)
        self.qdB_g = at("qdB_g", [P, HG * BLK], F32, offset=WK_QDB)
        self.ks_h = [at(f"ks_h{i}", [P, TOKS], BF16, offset=WK_KS[i]) for i in range(2)]
        self.qs_b = [at(f"qs_b{i}", [P, BLK], BF16, offset=WK_QS[i]) for i in range(2)]
        self.qkD = [at(f"qkD{i}", [P, 512], BF16, offset=WK_QKD[i]) for i in range(2)]
        self.gtmp = at("gtmp", [P, 512], F32, offset=WK_GTMP)
        self.osq = [at(f"osq{i}", [P, BLK], BF16, offset=WK_OSQ[i]) for i in range(2)]
        self.a_bf = at("a_bf", [P, NCH * 512], BF16, offset=WK_ABF)
        self.b_bf = at("b_bf", [P, 4 * 2048], BF16, offset=WK_BBF)
        self.wr_bf = at("wr_bf", [P, NCH * 8], BF16, offset=WK_WRBF)
        self.xb = at("xb", [P, 4 * TOKS], BF16, offset=WK_XB)
        self.sq = at("sq", [P, 4 * TOKS], BF16, offset=WK_SQ)
        self.tmp = at("tmp", [P, TOKS], F32, offset=WK_TMP)
        self.rows_a = at("rows_a", [P, TOKS], F32, offset=WK_ROWA)
        self.rstd_sb = at("rstd_sb", [P, TOKS], F32, offset=WK_XB)
        self.nm_r = at("nm_r", [1, TOKS], F32, offset=WK_XB + 4096)
        self.m2_r = at("m2_r", [1, TOKS], F32, offset=WK_TMP)
        self.rstd_r = at("rstd_r", [1, TOKS], F32, offset=WK_SQ)
        self.nmr_r = at("nmr_r", [1, TOKS], F32, offset=WK_SQ + 4096)
        self.lgs = at("lgs", [P, 64], F32, offset=WK_LGS)
        self.exb = at("exb", [P, 64], F32, offset=WK_EXB)
        self.ohb = at("ohb", [P, 64], F32, offset=WK_OHB)
        self.mxb = at("mxb", [P, 8], F32, offset=WK_MXB)
        self.nmxb = at("nmxb", [P, 8], F32, offset=WK_NMX)
        self.seb = at("seb", [P, 8], F32, offset=WK_SEB)
        self.wb = at("wb", [P, 8], F32, offset=WK_WB)
        self.mask_b = at("mask_b", [P, 4 * TOKS], F32, offset=OB_MASKB)
        self.hm = at("hm", [P, 4 * TOKS], BF16, offset=OB_HM)
        self.mw_tm = at("mw_tm", [P, 64], F32, offset=OB_MWTM)
        self.mw_fm = at("mw_fm", [8, TOKS], F32, offset=OB_MWFM)
        self.gst = [at(f"gst{i}", [P, NCH * 256], BF16, offset=WK_GST[i]) for i in range(2)]
        self.sg_b = [at(f"sg_b{i}", [P, TOKS], F32, offset=WK_BASE + 16384 + i * 4096)
                     for i in range(2)]
        self.t2_b = [at(f"t2_b{i}", [P, TOKS], F32, offset=WK_BASE + 24576 + i * 4096)
                     for i in range(2)]
        self.idBF = at("idBF", [P, P], BF16, offset=_co(256))
        self.idF32 = at("idF32", [P, P], F32, offset=_co(512))
        self.ones_row_bf = at("ones_row_bf", [1, P], BF16, offset=_co(256))
        self.ones_row_f = at("ones_row_f", [1, P], F32, offset=_co(512))
        self.onesP_bf = at("onesP_bf", [P, 1], BF16, offset=_co(32))
        self.eps_ln = at("eps_ln", [1, 1], F32, offset=_co(32))
        self.eps_rms = at("eps_rms", [1, 1], F32, offset=_co(32))
        self.s_col = at("s_col", [NH, 1], F32, offset=_co(32))
        self.gq = at("gq", [NH, BLK], F32, offset=_co(1024))
        self.gk = at("gk", [NH, BLK], F32, offset=_co(1024))
        self.gbd = at("gbd", [NH, NBLK], F32, offset=_co(32))
        self.qd16 = at("qd16", [NH, BLK], F32, offset=_co(1024))
        self.kd16 = at("kd16", [NH, BLK], F32, offset=_co(1024))
        self.bdp16 = at("bdp16", [NH, NBLK], F32, offset=_co(32))
        self.idx_c = at("idx_c", [P, 512], F32, offset=_co(2048))
        self.eexp = at("eexp", [8, 512], F32, offset=_co(2048))
        self.br_b = at("br_b", [P, 8], F32, offset=_co(32))
        self.rmsw = at("rmsw", [P, NCH], F32, offset=_co(64))
        self.s_bc = at("s_bc", [P, NH], F32, offset=_co(64))
        self.bdp_b = at("bdp_b", [P, NH * NBLK], F32, offset=_co(256))
        self.kd_colsT = at("kd_colsT", [P, 32], F32, offset=_co(128))
        self.flag_c = at("flag_c", [P, 1], F32, offset=_co(32))
        assert _CO[0] <= 229000, _CO[0]

        ec = self.es.enter_context
        self.pA = ec(nc.psum_tensor("pA", [P, 512], F32))
        self.pB = ec(nc.psum_tensor("pB", [P, 512], F32))
        self.pQK = ec(nc.psum_tensor("pQK", [P, 512], F32))
        self.pU = ec(nc.psum_tensor("pU", [P, 512], F32))
        self.pO = ec(nc.psum_tensor("pO", [P, 512], F32))
        self.pT = ec(nc.psum_tensor("pT", [P, 1024], BF16))
        self.pV0 = ec(nc.psum_tensor("pV0", [1, 1024], F32))

        self.sem_g = ec(nc.semaphore("sem_g"))
        self.sem_y = ec(nc.semaphore("sem_y"))
        self.sem_cc = ec(nc.semaphore("sem_cc"))
        self.sem_cast = ec(nc.semaphore("sem_cast"))
        self.sem_w = [ec(nc.semaphore("sem_w0")), ec(nc.semaphore("sem_w1"))]
        self.exp_w = [0, 0]
        self.sem_t = ec(nc.semaphore("sem_t"))
        self.sem_v = ec(nc.semaphore("sem_v"))
        self.sem_s = ec(nc.semaphore("sem_s"))
        self.exp_g = 0
        self.exp_y = 0
        self.exp_cc = 0
        self.c_t = 0
        self.c_v = 0
        self.c_s = 0

    def block(self, **bodies):
        with self.nc.Block() as b:
            for eng in ("sync", "gpsimd", "tensor", "vector", "scalar"):
                if eng in bodies:
                    getattr(b, eng)(bodies[eng])

    def gdma(self, g, out, in_):
        g.dma_start(out=out, in_=in_).then_inc(self.sem_g, 16)
        self.exp_g += 16

    def wdma(self, g, slot, out, in_):
        g.dma_start(out=out, in_=in_).then_inc(self.sem_w[slot], 16)
        self.exp_w[slot] += 16
        return self.exp_w[slot]

    def xall16(self):
        return self.xall[:, 0:NCH * TOKS].rearrange("p (c t) -> p c t", c=NCH)

    def ydma(self, y, out, in_):
        y.dma_start(out=out, in_=in_).then_inc(self.sem_y, 16)
        self.exp_y += 16

    def tap(self, name, tensor, rows, cols):
        if name not in self.taps:
            return
        t = self.nc.dram_tensor(f"tap_{name}", [rows, cols], tensor.dtype,
                                kind="ExternalOutput")

        def _s(y):
            self.ydma(y, t[:, :], tensor[0:rows, 0:cols])
            y.wait_ge(self.sem_y, self.exp_y)
        self.block(sync=_s)

    def mark_t(self, ins):
        ins.then_inc(self.sem_t, 1)
        self.c_t += 1
        return self.c_t

    def mark_v(self, ins):
        ins.then_inc(self.sem_v, 1)
        self.c_v += 1
        return self.c_v

    def mark_s(self, ins):
        ins.then_inc(self.sem_s, 1)
        self.c_s += 1
        return self.c_s

    def xs(self, c):
        return self.xslot[self.slotmap[c]]

    def rotate(self, c):
        old = self.slotmap[c]
        self.slotmap[c] = self.spare
        self.spare = old

    def emit_setup(self):
        b = self

        def g_pre(g):
            g.memset(b.idBF[:, :], 1.0)
            g.memset(b.idF32[:, :], 1.0)
        self.block(gpsimd=g_pre)

        def g_body(g):
            g.affine_select(b.idBF[:, :], b.idBF[:, :], [[-1, P]], ALU.is_equal, 0.0,
                            base=0, channel_multiplier=1)
            g.affine_select(b.idF32[:, :], b.idF32[:, :], [[-1, P]], ALU.is_equal, 0.0,
                            base=0, channel_multiplier=1)
            g.memset(b.ones_row_bf[:, :], 1.0)
            g.memset(b.ones_row_f[:, :], 1.0)
            g.memset(b.onesP_bf[:, :], 1.0)
            g.memset(b.eps_ln[:, :], 1e-5)
            g.memset(b.eps_rms[:, :], 1e-6)
            b.gdma(g, b.s_col[:, :], b.s_col_in[:, :])
            b.gdma(g, b.gq[:, :], b.gq_in[:, :])
            b.gdma(g, b.gk[:, :], b.gk_in[:, :])
            b.gdma(g, b.gbd[:, :], b.gbd_in[:, :])
            b.gdma(g, b.idx_c[:, :], b.idx_in[:, :])
            b.gdma(g, b.eexp[:, :], b.eexp_in[:, :])
            b.gdma(g, b.rmsw[:, :], b.rmsw_in[:, :])
            b.gdma(g, b.flag_c[:, :], b.flag_in[:, :])
            b.gdma(g, b.br_b[:, :], bass.AP(b.br_in, 0, [[0, P], [1, 8]]))
            b.gdma(g, b.s_bc[:, :], bass.AP(b.s_col_in, 0, [[0, P], [1, NH]]))
            g.wait_ge(b.sem_g, b.exp_g)


        def y_body(y):
            b.ydma(y, b.xall16(),
                   b.x_in.ap().rearrange("(c p) t -> p c t", p=P))
            y.wait_ge(b.sem_y, b.exp_y)

        self.block(gpsimd=g_body, sync=y_body)
        self.cast_cnt = self.exp_g

        def v_body(v):
            v.tensor_scalar(b.qd16[:, :], b.gq[:, :], b.s_col[:, 0:1], None, ALU.mult)
            v.tensor_scalar(b.kd16[:, :], b.gk[:, :], b.s_col[:, 0:1], None, ALU.mult)
            v.tensor_scalar(b.bdp16[:, :], b.gbd[:, :], b.s_col[:, 0:1], None, ALU.mult)
        self.block(vector=v_body)

        def s_body(s):
            s.activation(b.qd16[:, :], b.qd16[:, :], AF.Exp)
            s.activation(b.kd16[:, :], b.kd16[:, :], AF.Exp)
            s.activation(b.bdp16[:, :], b.bdp16[:, :], AF.Exp)
        self.block(scalar=s_body)

        def g2(g):
            b.gdma(g, b.qd_dram[:, :], b.qd16[:, :])
            b.gdma(g, b.bdp_dram[:, :], b.bdp16[:, :])
            g.wait_ge(b.sem_g, b.exp_g)
            b.gdma(g, b.bdp_b[:, :],
                   bass.AP(b.bdp_dram, 0, [[0, P], [NBLK, NH], [1, NBLK]]))
            g.wait_ge(b.sem_g, b.exp_g)
        self.block(gpsimd=g2)

        def t_body(t):
            t.transpose(b.pQK[:, 0:16], b.kd16[:, 0:128], b.idF32[0:16, 0:16])
            t.transpose(b.pQK[:, 16:32], b.kd16[:, 128:256], b.idF32[0:16, 0:16])
        self.block(tensor=t_body)

        def v2(v):
            v.tensor_copy(b.kd_colsT[:, 0:32], b.pQK[:, 0:32])
        self.block(vector=v2)

    def load_ffn_weights(self):
        b = self

        def g_body(g):
            b.gdma(g, b.a_bf.ap().rearrange("p (c r) -> p c r", c=NCH),
                   b.a_t.ap().rearrange("(c p) r -> p c r", p=P))
            b.gdma(g, b.b_bf.ap().rearrange("p (c o) -> p c o", c=4),
                   b.b_t.ap().rearrange("(c p) o -> p c o", p=P))
            b.gdma(g, b.wr_bf.ap().rearrange("p (c e) -> p c e", c=NCH),
                   b.wr_t.ap().rearrange("(c p) e -> p c e", p=P))
            g.wait_ge(b.sem_g, b.exp_g)
        self.block(gpsimd=g_body)

    def emit_ln(self, tag):
        b = self
        nb = 8
        t0, s0, v0 = self.c_t, self.c_s, self.c_v
        xsrc = [b.xs(c) for c in range(NCH)]

        def s_body(s):
            for k in range(nb):
                off = (k % 2) * 2 * TOKS
                if k >= 2:
                    s.wait_ge(b.sem_t, t0 + k - 1)
                for i, c in enumerate([2 * k, 2 * k + 1]):
                    ins = s.activation(b.sq[:, off + i * TOKS:off + (i + 1) * TOKS],
                                       xsrc[c][:, :], AF.Square)
                b.c_s += 1
                ins.then_inc(b.sem_s, 1)

        def v_body(v):
            for k in range(nb):
                off = (k % 2) * 2 * TOKS
                if k >= 2:
                    v.wait_ge(b.sem_t, t0 + k - 1)
                for i, c in enumerate([2 * k, 2 * k + 1]):
                    ins = v.tensor_copy(b.xb[:, off + i * TOKS:off + (i + 1) * TOKS],
                                        xsrc[c][:, :])
                b.c_v += 1
                ins.then_inc(b.sem_v, 1)

        def t_body(t):
            for k in range(nb):
                off = (k % 2) * 2 * TOKS
                t.wait_ge(b.sem_s, s0 + k + 1)
                t.wait_ge(b.sem_v, v0 + k + 1)
                for i, c in enumerate([2 * k, 2 * k + 1]):
                    st = (c == 0)
                    sp = (c == NCH - 1)
                    xbv = b.xb[:, off + i * TOKS:off + (i + 1) * TOKS]
                    sqv = b.sq[:, off + i * TOKS:off + (i + 1) * TOKS]
                    t.matmul(b.pV0[0:1, 0:512], b.onesP_bf[:, :], xbv[:, 0:512],
                             start=st, stop=sp, skip_group_check=True)
                    t.matmul(b.pV0[0:1, 512:1024], b.onesP_bf[:, :], xbv[:, 512:1024],
                             start=st, stop=sp, skip_group_check=True)
                    t.matmul(b.pU[0:1, 0:512], b.onesP_bf[:, :], sqv[:, 0:512],
                             start=st, stop=sp, skip_group_check=True)
                    ins = t.matmul(b.pQK[0:1, 0:512], b.onesP_bf[:, :], sqv[:, 512:1024],
                                   start=st, stop=sp, skip_group_check=True)
                b.c_t += 1
                ins.then_inc(b.sem_t, 1)

        self.block(scalar=s_body, vector=v_body, tensor=t_body)

        def s_nm(s):
            s.activation(b.nm_r[0:1, :], b.pV0[0:1, :], AF.Copy, scale=-1.0 / 2048.0)
        self.block(scalar=s_nm)

        def s_m2(s):
            s.activation(b.m2_r[0:1, :], b.nm_r[0:1, :], AF.Square)
        self.block(scalar=s_m2)

        def v_var(v):
            v.scalar_tensor_tensor(b.rstd_r[0:1, 0:512], b.pU[0:1, 0:512],
                                   1.0 / 2048.0, b.m2_r[0:1, 0:512],
                                   ALU.mult, ALU.subtract)
            v.scalar_tensor_tensor(b.rstd_r[0:1, 512:1024], b.pQK[0:1, 0:512],
                                   1.0 / 2048.0, b.m2_r[0:1, 512:1024],
                                   ALU.mult, ALU.subtract)
        self.block(vector=v_var)

        def s_rstd(s):
            s.activation(b.rstd_r[0:1, :], b.rstd_r[0:1, :], AF.Ln,
                         bias=b.eps_ln[0:1, 0:1])
            s.activation(b.rstd_r[0:1, :], b.rstd_r[0:1, :], AF.Exp, scale=-0.5)
        self.block(scalar=s_rstd)

        def v_nmr(v):
            v.tensor_tensor(b.nmr_r[0:1, :], b.nm_r[0:1, :], b.rstd_r[0:1, :],
                            ALU.mult)
        self.block(vector=v_nmr)

        def t_bcast(t):
            t.matmul(b.pA[:, :], b.ones_row_f[:, :], b.rstd_r[0:1, 0:512],
                     start=True, stop=True)
            t.matmul(b.pB[:, :], b.ones_row_f[:, :], b.rstd_r[0:1, 512:1024],
                     start=True, stop=True)
            t.matmul(b.pQK[:, :], b.ones_row_f[:, :], b.nmr_r[0:1, 0:512],
                     start=True, stop=True)
            t.matmul(b.pO[:, :], b.ones_row_f[:, :], b.nmr_r[0:1, 512:1024],
                     start=True, stop=True)
        self.block(tensor=t_bcast)

        def v_apply(v):
            for c in range(NCH):
                x = b.xs(c)
                xnv = b.xn[:, c * TOKS:(c + 1) * TOKS]
                v.tensor_tensor(b.tmp[:, 0:512], x[:, 0:512], b.pA[:, :], ALU.mult)
                v.tensor_tensor(b.tmp[:, 512:1024], x[:, 512:1024], b.pB[:, :], ALU.mult)
                v.tensor_tensor(xnv[:, 0:512], b.tmp[:, 0:512], b.pQK[:, :], ALU.add)
                v.tensor_tensor(xnv[:, 512:1024], b.tmp[:, 512:1024], b.pO[:, :], ALU.add)
        hook = getattr(self, "post_apply_g", None)
        if hook is not None:
            self.post_apply_g = None
            self.block(vector=v_apply, gpsimd=hook)
        else:
            self.block(vector=v_apply)
        self.tap(f"xn_{tag}", b.xn, P, NCH * TOKS)

    def emit_ffn(self, tag):
        b = self
        banks4 = [b.pA, b.pB, b.pQK, b.pO]

        def t_lg(t):
            for tt in range(8):
                for c in range(NCH):
                    t.matmul(b.pA[:, tt * 16:tt * 16 + 8],
                             b.xn[:, c * TOKS + tt * P:c * TOKS + (tt + 1) * P],
                             b.wr_bf[:, c * 8:(c + 1) * 8],
                             start=(c == 0), stop=(c == NCH - 1), skip_group_check=True)
        self.block(tensor=t_lg)

        def v_sm1(v):
            for tt in range(8):
                v.tensor_tensor(b.lgs[:, tt * 8:(tt + 1) * 8],
                                b.pA[:, tt * 16:tt * 16 + 8], b.br_b[:, :], ALU.add)
            for tt in range(8):
                v.reduce_max(b.mxb[:, tt:tt + 1], b.lgs[:, tt * 8:(tt + 1) * 8], AX)
            for tt in range(8):
                v.tensor_scalar(b.exb[:, tt * 8:(tt + 1) * 8], b.lgs[:, tt * 8:(tt + 1) * 8],
                                b.mxb[:, tt:tt + 1], None, ALU.subtract)
        self.block(vector=v_sm1)

        def s_exp(s):
            s.activation(b.exb[:, :], b.exb[:, :], AF.Exp)
        self.block(scalar=s_exp)

        def v_sm2a(v):
            for tt in range(8):
                v.reduce_sum(b.seb[:, tt:tt + 1], b.exb[:, tt * 8:(tt + 1) * 8], AX)
        self.block(vector=v_sm2a)

        def s_recip(s):
            s.activation(b.wb[:, :], b.seb[:, :], AF.Ln)
            s.activation(b.wb[:, :], b.wb[:, :], AF.Exp, scale=-1.0)
        self.block(scalar=s_recip)

        def v_sm2(v):
            for tt in range(8):
                v.tensor_scalar(b.ohb[:, tt * 8:(tt + 1) * 8], b.lgs[:, tt * 8:(tt + 1) * 8],
                                b.mxb[:, tt:tt + 1], None, ALU.is_equal)
            for tt in range(8):
                v.tensor_scalar(b.mw_tm[:, tt * 8:(tt + 1) * 8],
                                b.ohb[:, tt * 8:(tt + 1) * 8],
                                b.wb[:, tt:tt + 1], FFN_SCALE, ALU.mult, ALU.mult)
        self.block(vector=v_sm2)
        self.tap(f"seb_{tag}", b.seb, P, 8)
        self.tap(f"lgs_{tag}", b.lgs, P, 64)
        self.tap(f"exb_{tag}", b.exb, P, 64)
        self.tap(f"ohb_{tag}", b.ohb, P, 64)
        self.tap(f"mxb_{tag}", b.mxb, P, 8)
        self.tap(f"wb_{tag}", b.wb, P, 8)
        self.tap(f"mwtm_{tag}", b.mw_tm, P, 64)

        def t_tr(t):
            for tt in range(8):
                dst = b.pO if tt < 4 else b.pU
                t.transpose(dst[0:8, (tt % 4) * P:(tt % 4 + 1) * P],
                            b.mw_tm[:, tt * 8:(tt + 1) * 8], b.idF32[:, :])
        self.block(tensor=t_tr)

        def v_mw(v):
            v.tensor_copy(b.mw_fm[0:8, 0:512], b.pO[0:8, :])
            v.tensor_copy(b.mw_fm[0:8, 512:1024], b.pU[0:8, :])
        self.block(vector=v_mw)

        for wave in range(2):
            def t_ex(t, wave=wave):
                for i in range(2):
                    rc = wave * 2 + i
                    for tt in range(2):
                        bank = banks4[i * 2 + tt]
                        t.matmul(bank[:, :], b.eexp[:, rc * P:(rc + 1) * P],
                                 b.mw_fm[0:8, tt * 512:(tt + 1) * 512],
                                 start=True, stop=True)
            self.block(tensor=t_ex)

            def v_ex(v, wave=wave):
                for i in range(2):
                    rc = wave * 2 + i
                    for tt in range(2):
                        bank = banks4[i * 2 + tt]
                        v.tensor_copy(
                            b.mask_b[:, rc * TOKS + tt * 512:rc * TOKS + (tt + 1) * 512],
                            bank[:, :])
            self.block(vector=v_ex)

        htiles = [(rc, tt) for rc in range(4) for tt in range(2)]
        bsets = [(b.pA, b.pB), (b.pQK, b.pO)]
        t0, v0 = self.c_t, self.c_v

        def t_h(t):
            for w in range(0, len(htiles), 2):
                if w >= 4:
                    t.wait_ge(b.sem_v, v0 + w // 2 - 1)
                for i, (rc, tt) in enumerate(htiles[w:w + 2]):
                    bank = bsets[(w // 2) % 2][i]
                    for c in range(NCH):
                        ins = t.matmul(bank[:, :],
                                       b.a_bf[:, c * 512 + rc * P:c * 512 + (rc + 1) * P],
                                       b.xn[:, c * TOKS + tt * 512:c * TOKS + (tt + 1) * 512],
                                       start=(c == 0), stop=(c == NCH - 1))
                b.c_t += 1
                ins.then_inc(b.sem_t, 1)

        def v_h(v):
            for w in range(0, len(htiles), 2):
                v.wait_ge(b.sem_t, t0 + w // 2 + 1)
                for i, (rc, tt) in enumerate(htiles[w:w + 2]):
                    bank = bsets[(w // 2) % 2][i]
                    sl = slice(rc * TOKS + tt * 512, rc * TOKS + (tt + 1) * 512)
                    ins = v.tensor_tensor(b.hm[:, sl], bank[:, :], b.mask_b[:, sl], ALU.mult)
                b.c_v += 1
                ins.then_inc(b.sem_v, 1)

        self.block(tensor=t_h, vector=v_h)
        self.tap(f"hm_{tag}", b.hm, P, 4 * TOKS)

        store = (tag == "ffn2")
        t0, v0 = self.c_t, self.c_v
        rotplan = []
        sp = b.spare
        sm = list(b.slotmap)
        for oc in range(NCH):
            rotplan.append((sm[oc], sp))
            sm[oc], sp = sp, sm[oc]

        def t_y(t):
            for oc in range(NCH):
                if oc >= 2:
                    t.wait_ge(b.sem_v, v0 + oc - 1)
                for tt in range(2):
                    bank = bsets[oc % 2][tt]
                    for rc in range(4):
                        ins = t.matmul(bank[:, :],
                                       b.b_bf[:, rc * 2048 + oc * P:rc * 2048 + (oc + 1) * P],
                                       b.hm[:, rc * TOKS + tt * 512:rc * TOKS + (tt + 1) * 512],
                                       start=(rc == 0), stop=(rc == 3))
                b.c_t += 1
                ins.then_inc(b.sem_t, 1)

        def v_y(v):
            for oc in range(NCH):
                v.wait_ge(b.sem_t, t0 + oc + 1)
                olds, news = rotplan[oc]
                for tt in range(2):
                    bank = bsets[oc % 2][tt]
                    ins = v.tensor_tensor(b.xslot[news][:, tt * 512:(tt + 1) * 512],
                                          bank[:, :],
                                          b.xslot[olds][:, tt * 512:(tt + 1) * 512], ALU.add)
                b.c_v += 1
                ins.then_inc(b.sem_v, 1)

        bodies = dict(tensor=t_y, vector=v_y)
        if store:
            def y_st(y):
                for oc in range(NCH):
                    y.wait_ge(b.sem_v, v0 + oc + 1)
                    self.ydma(y, b.out_t[oc * P:(oc + 1) * P, :],
                              b.xslot[rotplan[oc][1]][:, :])
                y.wait_ge(b.sem_y, b.exp_y)
            bodies["sync"] = y_st
        self.block(**bodies)
        for oc in range(NCH):
            b.rotate(oc)

    def emit_attn(self):
        b = self
        spill_map = list(self.slotmap)

        def s_spill(s):
            self.ydma(s, b.xspill.ap(), b.xall[:, :])
        self.block(scalar=s_spill)
        spill_cnt = b.exp_y

        self.emit_ln("attn")

        self.mark_phase("ln2_done")
        inproj = (self.cc_mode == "inproj")
        self.emit_proj(0, 0, extra_wait=spill_cnt)
        self.mark_phase("proj0_done")
        self.emit_amath(0, 0, pre_issue=1)
        if not inproj:
            self.emit_cc_issue(0)
        for g in range(NG):
            gb = g % 2
            if g + 1 < NG:
                self.emit_proj(g + 1, (g + 1) % 2, cc_pre=(g if inproj else None),
                               preissued=True)
                self.mark_phase(f"proj{g+1}_done")
            elif inproj:
                self.emit_cc_issue(g)
            self.emit_cc_fixup(g, gb)
            self.mark_phase(f"fix{g}_done")
            self.emit_bmath(g, gb)
            self.mark_phase(f"bmath{g}_done")
            if g + 1 < NG:
                self.emit_amath(g + 1, (g + 1) % 2,
                                pre_issue=(g + 2 if g + 2 < NG else None))
                if not inproj:
                    self.emit_cc_issue(g + 1)
                self.mark_phase(f"amath{g+1}_done")

        self.tap("o", b.o_bf, P, NH * TOKS)

        def s_rv(s):
            s.activation(b.rstd_r[0:1, :], b.pV0[0:1, :], AF.Ln,
                         bias=b.eps_rms[0:1, 0:1], scale=1.0 / 2048.0)
        self.block(scalar=s_rv)

        def s_rv2(s):
            s.activation(b.rstd_r[0:1, :], b.rstd_r[0:1, :], AF.Exp, scale=-0.5)
        self.block(scalar=s_rv2)

        def t_rvb(t):
            t.matmul(b.pQK[:, :], b.ones_row_f[:, :], b.rstd_r[0:1, 0:512],
                     start=True, stop=True)
            t.matmul(b.pU[:, :], b.ones_row_f[:, :], b.rstd_r[0:1, 512:1024],
                     start=True, stop=True)
        self.block(tensor=t_rvb)

        def v_rvb(v):
            v.tensor_copy(b.rstd_sb[:, 0:512], b.pQK[:, :])
            v.tensor_copy(b.rstd_sb[:, 512:1024], b.pU[:, :])
        self.block(vector=v_rvb)

        def y_rest(y):
            self.ydma(y, b.xall[:, :], b.xspill.ap())
        self.block(sync=y_rest)
        restore_cnt = b.exp_y
        self.slotmap = spill_map
        self.spare = [i for i in range(NSLOT) if i not in spill_map][0]

        self.emit_gate()
        self.mark_phase("gate_done")
        self.tap("os", b.o_bf, P, NH * TOKS)

        self.emit_wout(restore_cnt)
        self.mark_phase("wout_done")
        self.tap("x2", self.xs(0), P, TOKS)

    def emit_proj(self, g, gb, extra_wait=0, cc_pre=None, preissued=False):
        b = self
        kg, vg, qg = b.k_g[gb], b.v_g[gb], b.q_g[gb]
        pieces = []
        for which, colbase in (("k", 2048 + g * 512), ("v", 4096 + g * 512),
                               ("q", 0 + g * 512)):
            for half in range(2):
                pieces.append((which, colbase + half * 256, half))

        waves = []
        for p, (which, cb, half) in enumerate(pieces):
            ws = b.wstr[p % 2]
            if which in ("k", "q"):
                dst = kg if which == "k" else qg
                for hi2 in range(2):
                    hi = half * 2 + hi2

                    def mm(t, banks, ws=ws, hi2=hi2):
                        for tt in range(2):
                            for c in range(NCH):
                                ins = t.matmul(
                                    banks[tt][:, :],
                                    ws[:, c * 256 + hi2 * P:c * 256 + (hi2 + 1) * P],
                                    b.xn[:, c * TOKS + tt * 512:c * TOKS + (tt + 1) * 512],
                                    start=(c == 0), stop=(c == NCH - 1))
                        return ins

                    def ev(s, banks, dst=dst, hi=hi):
                        for tt in range(2):
                            ins = s.activation(
                                dst[:, hi * TOKS + tt * 512:hi * TOKS + (tt + 1) * 512],
                                banks[tt][:, :], AF.Silu)
                        return ins
                    waves.append((p, mm, ev))
            else:
                for wv in range(4):
                    def mm(t, banks, ws=ws, wv=wv):
                        for i in range(2):
                            tt = wv * 2 + i
                            for c in range(NCH):
                                ins = t.matmul(
                                    banks[i][:, 0:256],
                                    b.xn[:, c * TOKS + tt * P:c * TOKS + (tt + 1) * P],
                                    ws[:, c * 256:(c + 1) * 256],
                                    start=(c == 0), stop=(c == NCH - 1))
                        return ins

                    def ev(s, banks, vg=vg, wv=wv, half=half):
                        for i in range(2):
                            tt = wv * 2 + i
                            ins = s.activation(
                                vg[:, tt * 512 + half * 256:tt * 512 + (half + 1) * 256],
                                banks[i][:, 0:256], AF.Silu)
                        return ins
                    waves.append((p, mm, ev))

        nw = len(waves)
        bsets = [(b.pA, b.pB), (b.pQK, b.pO)]
        t0, s0 = self.c_t, self.c_s
        piece_first = {}
        piece_last = {}
        for w, (p, _, _) in enumerate(waves):
            piece_first.setdefault(p, w)
            piece_last[p] = w
        if preissued:
            piece_cnt = [self.exp_w[p % 2] + 16 * (p // 2) for p in range(len(pieces))]
        else:
            piece_cnt = [self.exp_w[p % 2] + 16 * (p // 2 + 1) for p in range(len(pieces))]

        skip = 2 if preissued else 0

        def g_body(gq):
            if b.precast:
                gq.wait_ge(b.sem_cast, 48)
            if extra_wait:
                gq.wait_ge(b.sem_y, extra_wait)
            for p, (which, cb, half) in enumerate(pieces):
                if p < skip:
                    continue
                if p >= 2:
                    gq.wait_ge(b.sem_t, t0 + piece_last[p - 1] + 1)
                pid = cb // 256
                srcap = bass.AP(b.wqkv, pid * 4096, [[24 * 4096, P], [1, 4096]])
                b.wdma(gq, p % 2, b.wstr[p % 2][:, :], srcap)

        def t_body(t):
            for w, (p, mm, _) in enumerate(waves):
                if w == piece_first[p]:
                    t.wait_ge(b.sem_w[p % 2], piece_cnt[p])
                if w >= 2:
                    t.wait_ge(b.sem_s, s0 + w - 1)
                ins = mm(t, bsets[w % 2])
                b.c_t += 1
                ins.then_inc(b.sem_t, 1)

        def s_body(s):
            for w, (p, _, ev) in enumerate(waves):
                s.wait_ge(b.sem_t, t0 + w + 1)
                ins = ev(s, bsets[w % 2])
                b.c_s += 1
                ins.then_inc(b.sem_s, 1)

        def g_cc(gq):
            if cc_pre is not None:
                gq.collective_compute(
                    "AllGather", ALU.bypass,
                    replica_groups=[[0, 1], [2, 3], [4, 5], [6, 7]],
                    ins=[b.cc_in[cc_pre].ap()], outs=[b.cc_out[cc_pre].ap()],
                ).then_inc(b.sem_cc)
                b.exp_cc += 1

        def g_all(gq):
            g_cc(gq)
            g_body(gq)
        self.block(gpsimd=g_all, tensor=t_body, scalar=s_body)
        if g == 0:
            self.tap("k_g0", kg, P, HG * TOKS)
            self.tap("v_g0", vg, P, HG * TOKS)
            self.tap("q_g0", qg, P, HG * TOKS)

    def emit_amath(self, g, gb, pre_issue=None):
        b = self
        kg, vg = b.k_g[gb], b.v_g[gb]
        Sg = b.S_g[gb]
        ubank = [b.pU, b.pA]
        TTR = [0] * HG
        TU = [0] * HG
        VKS = [0] * HG
        VSC = [0] * HG
        ct = self.c_t
        for hi in range(HG):
            ct += 1; TTR[hi] = ct
            if hi >= 1:
                ct += 1; TU[hi - 1] = ct
        ct += 1; TU[HG - 1] = ct
        cv = self.c_v
        for hi in range(HG):
            cv += 1; VKS[hi] = cv
            if hi >= 1:
                cv += 1; VSC[hi - 1] = cv
        cv += 1; VSC[HG - 1] = cv

        def emit_tr(t, hi):
            for tt in range(8):
                ins = t.transpose(b.pT[:, tt * P:(tt + 1) * P],
                                  kg[:, hi * TOKS + tt * P:hi * TOKS + (tt + 1) * P],
                                  b.idBF[:, :])
            return ins

        def emit_u(t, hi):
            for j in range(NBLK):
                for mc in range(2):
                    tt = 2 * j + mc
                    ins = t.matmul(ubank[hi % 2][:, j * DH:(j + 1) * DH],
                                   b.ks_h[hi % 2][:, tt * P:(tt + 1) * P],
                                   vg[:, tt * 512 + hi * DH:tt * 512 + (hi + 1) * DH],
                                   start=(mc == 0), stop=(mc == 1))
            return ins

        def t_body(t):
            for hi in range(HG):
                if hi >= 1:
                    t.wait_ge(b.sem_v, VKS[hi - 1])
                ins = emit_tr(t, hi)
                ins.then_inc(b.sem_t, 1)
                if hi >= 1:
                    ins = emit_u(t, hi - 1)
                    ins.then_inc(b.sem_t, 1)
            t.wait_ge(b.sem_v, VKS[HG - 1])
            ins = emit_u(t, HG - 1)
            ins.then_inc(b.sem_t, 1)
            b.c_t = ct

        def emit_ks(v, hi):
            hh = g * HG + hi
            for tt in range(8):
                col = (tt % 2) * 16 + hh
                ins = v.tensor_scalar(b.ks_h[hi % 2][:, tt * P:(tt + 1) * P],
                                      b.pT[:, tt * P:(tt + 1) * P],
                                      b.kd_colsT[:, col:col + 1], None, ALU.mult)
            return ins

        def emit_scan(v, hi):
            hh = g * HG + hi
            base = hi * 4 * DH
            v.tensor_copy(Sg[:, base:base + DH], ubank[hi % 2][:, 0:DH])
            for j in range(1, NBLK):
                ins = v.scalar_tensor_tensor(Sg[:, base + j * DH:base + (j + 1) * DH],
                                             Sg[:, base + (j - 1) * DH:base + j * DH],
                                             b.bdp_b[:, hh * NBLK + 1:hh * NBLK + 2],
                                             ubank[hi % 2][:, j * DH:(j + 1) * DH],
                                             ALU.mult, ALU.add)
            return ins

        def v_body(v):
            for hi in range(HG):
                v.wait_ge(b.sem_t, TTR[hi])
                ins = emit_ks(v, hi)
                ins.then_inc(b.sem_v, 1)
                if hi >= 1:
                    v.wait_ge(b.sem_t, TU[hi - 1])
                    ins = emit_scan(v, hi - 1)
                    ins.then_inc(b.sem_v, 1)
            v.wait_ge(b.sem_t, TU[HG - 1])
            ins = emit_scan(v, HG - 1)
            ins.then_inc(b.sem_v, 1)
            b.c_v = cv

        def g_body(gq):
            if pre_issue is not None:
                ng = pre_issue
                for p in range(2):
                    pid = (2048 + p * 256 + ng * 512) // 256
                    srcap = bass.AP(b.wqkv, pid * 4096, [[24 * 4096, P], [1, 4096]])
                    b.wdma(gq, p, b.wstr[p][:, :], srcap)
            for hi in range(HG):
                gq.wait_ge(b.sem_v, VSC[hi])
                base = hi * 4 * DH
                b.gdma(gq, b.cc_in[g][hi * P:(hi + 1) * P, :],
                       Sg[:, base + 3 * DH:base + 4 * DH])
            gq.wait_ge(b.sem_g, b.exp_g)

        self.block(tensor=t_body, vector=v_body, gpsimd=g_body)
        if g == 0:
            self.tap("S_g0", Sg, P, HG * 4 * DH)

    def emit_cc_issue(self, g):
        b = self

        def g_cc(gq):
            gq.collective_compute(
                "AllGather", ALU.bypass,
                replica_groups=[[0, 1], [2, 3], [4, 5], [6, 7]],
                ins=[b.cc_in[g].ap()], outs=[b.cc_out[g].ap()],
            ).then_inc(b.sem_cc)
            b.exp_cc += 1
        self.block(gpsimd=g_cc)

    def emit_cc_fixup(self, g, gb):
        b = self
        Sg, Tg, kvi = b.S_g[gb], b.T_g[gb], b.kvi_g[gb]

        def g_cc(gq):
            gq.wait_ge(b.sem_cc, g + 1)
            b.gdma(gq, b.gtmp.ap().rearrange("p (h e) -> p h e", h=HG),
                   bass.AP(b.cc_out[g], 0, [[DH, P], [P * DH, HG], [1, DH]]))
            gq.wait_ge(b.sem_g, b.exp_g)
        self.block(gpsimd=g_cc)

        def v_kvi(v):
            v.tensor_scalar(kvi[:, :], b.gtmp[:, :], b.flag_c[:, 0:1], None, ALU.mult)
        self.block(vector=v_kvi)

        def v_fix(v):
            for hi in range(HG):
                hh = g * HG + hi
                base = hi * 4 * DH
                v.tensor_copy(Tg[:, base:base + DH], kvi[:, hi * DH:(hi + 1) * DH])
                for j in range(1, NBLK):
                    v.scalar_tensor_tensor(Tg[:, base + j * DH:base + (j + 1) * DH],
                                           kvi[:, hi * DH:(hi + 1) * DH],
                                           b.bdp_b[:, hh * NBLK + j:hh * NBLK + j + 1],
                                           Sg[:, base + (j - 1) * DH:base + j * DH],
                                           ALU.mult, ALU.add)
        self.block(vector=v_fix)
        if g == 0:
            self.tap("T_g0", Tg, P, HG * 4 * DH)

    def emit_bmath(self, g, gb):
        b = self
        kg, vg, qg = b.k_g[gb], b.v_g[gb], b.q_g[gb]
        Tg = b.T_g[gb]

        def g_qd(gq):
            b.gdma(gq, b.qdB_g.ap().rearrange("p (h l) -> p h l", h=HG),
                   bass.AP(b.qd_dram, g * HG * BLK, [[0, P], [BLK, HG], [1, BLK]]))
            gq.wait_ge(b.sem_g, b.exp_g)
        self.block(gpsimd=g_qd)

        def v_d(v):
            for hi in range(HG):
                hh = g * HG + hi
                v.tensor_scalar(b.D_g[:, hi * 512:(hi + 1) * 512], b.idx_c[:, :],
                                b.s_bc[:, hh:hh + 1], None, ALU.mult)
        self.block(vector=v_d)

        def s_d(s):
            for hi in range(HG):
                s.activation(b.D_g[:, hi * 512:(hi + 1) * 512],
                             b.D_g[:, hi * 512:(hi + 1) * 512], AF.Exp)
        self.block(scalar=s_d)
        if g == 0:
            self.tap("D_g0", b.D_g, P, HG * 512)
            self.tap("qdB_g0", b.qdB_g, P, HG * BLK)

        NI = HG * NBLK
        qkbank = [b.pQK, b.pU]
        obank = [b.pA, b.pO]
        TQK = [0] * NI
        TO = [0] * NI
        TV = [0] * NI
        VOE = [0] * NI
        VQK = [0] * NI
        SQ = [0] * NI
        ct = self.c_t
        for i in range(NI):
            ct += 1; TQK[i] = ct
            if i >= 1:
                ct += 1; TO[i - 1] = ct
            if i >= 4:
                ct += 1; TV[i - 4] = ct
        ct += 1; TO[NI - 1] = ct
        for j in range(NI - 4, NI):
            ct += 1; TV[j] = ct
        cv = self.c_v
        for i in range(NI):
            if i >= 1:
                cv += 1; VOE[i - 1] = cv
            cv += 1; VQK[i] = cv
        cv += 1; VOE[NI - 1] = cv
        cs = self.c_s
        for j in range(NI):
            cs += 1; SQ[j] = cs

        def mm_qk(t, i):
            hi, blk = i // NBLK, i % NBLK
            for mc in range(2):
                tt = 2 * blk + mc
                ins = t.matmul(qkbank[i % 2][:, mc * 256:(mc + 1) * 256],
                               kg[:, hi * TOKS + tt * P:hi * TOKS + (tt + 1) * P],
                               qg[:, hi * TOKS + blk * BLK:hi * TOKS + (blk + 1) * BLK],
                               start=True, stop=True)
            return ins

        def mm_o(t, j):
            hi, blk = j // NBLK, j % NBLK
            for mc in range(2):
                tt = 2 * blk + mc
                t.matmul(obank[j % 2][:, 0:256],
                         vg[:, tt * 512 + hi * DH:tt * 512 + (hi + 1) * DH],
                         b.qkD[j % 2][:, mc * 256:(mc + 1) * 256],
                         start=(mc == 0), stop=False)
            base = hi * 4 * DH
            return t.matmul(obank[j % 2][:, 0:256],
                            Tg[:, base + blk * DH:base + (blk + 1) * DH],
                            b.qs_b[j % 2][:, :], start=False, stop=True)

        def mm_var(t, j):
            hi, blk = j // NBLK, j % NBLK
            hh = g * HG + hi
            return t.matmul(b.pV0[0:1, blk * BLK:(blk + 1) * BLK],
                            b.onesP_bf[:, :], b.osq[j % 2][:, :],
                            start=(hh == 0 and blk in (0, 2)),
                            stop=(hh == NH - 1 and blk in (1, 3)),
                            skip_group_check=True)

        def t_body(t):
            def do_o(j):
                t.wait_ge(b.sem_v, VQK[j])
                if j >= 2:
                    t.wait_ge(b.sem_v, VOE[j - 2])
                mm_o(t, j).then_inc(b.sem_t, 1)

            def do_var(j):
                t.wait_ge(b.sem_s, SQ[j])
                mm_var(t, j).then_inc(b.sem_t, 1)

            for i in range(NI):
                if i >= 2:
                    t.wait_ge(b.sem_v, VQK[i - 2])
                mm_qk(t, i).then_inc(b.sem_t, 1)
                if i >= 1:
                    do_o(i - 1)
                if i >= 4:
                    do_var(i - 4)
            do_o(NI - 1)
            for j in range(NI - 4, NI):
                do_var(j)
            b.c_t = ct

        def v_body(v):
            def do_oe(j):
                hi, blk = j // NBLK, j % NBLK
                hh = g * HG + hi
                v.wait_ge(b.sem_t, TO[j])
                ins = v.tensor_copy(
                    b.o_bf[:, hh * TOKS + blk * BLK:hh * TOKS + (blk + 1) * BLK],
                    obank[j % 2][:, 0:256])
                ins.then_inc(b.sem_v, 1)

            for i in range(NI):
                if i >= 1:
                    do_oe(i - 1)
                hi, blk = i // NBLK, i % NBLK
                v.wait_ge(b.sem_t, TQK[i])
                v.tensor_tensor(b.qkD[i % 2][:, :], qkbank[i % 2][:, :],
                                b.D_g[:, hi * 512:(hi + 1) * 512], ALU.mult)
                ins = v.tensor_tensor(
                    b.qs_b[i % 2][:, :],
                    qg[:, hi * TOKS + blk * BLK:hi * TOKS + (blk + 1) * BLK],
                    b.qdB_g[:, hi * BLK:(hi + 1) * BLK], ALU.mult)
                ins.then_inc(b.sem_v, 1)
            do_oe(NI - 1)
            b.c_v = cv

        def s_body(s):
            for j in range(NI):
                hi, blk = j // NBLK, j % NBLK
                hh = g * HG + hi
                s.wait_ge(b.sem_v, VOE[j])
                if j >= 2:
                    s.wait_ge(b.sem_t, TV[j - 2])
                ins = s.activation(
                    b.osq[j % 2][:, :],
                    b.o_bf[:, hh * TOKS + blk * BLK:hh * TOKS + (blk + 1) * BLK],
                    AF.Square)
                ins.then_inc(b.sem_s, 1)
            b.c_s = cs

        if g == NG - 1:
            def g_pre(gq):
                for p in range(2):
                    srcap = bass.AP(b.wgate, p * 4096, [[8 * 4096, P], [1, 4096]])
                    b.wdma(gq, p, b.gst[p][:, :], srcap)
            self.block(tensor=t_body, vector=v_body, scalar=s_body, gpsimd=g_pre)
        else:
            self.block(tensor=t_body, vector=v_body, scalar=s_body)

    def emit_gate(self):
        b = self
        bset = [(b.pA, b.pB), (b.pQK, b.pO)]
        t0, s0, v0 = self.c_t, self.c_s, self.c_v
        piece_cnt = [self.exp_w[p % 2] + 16 * (p // 2) for p in range(8)]

        def g_body(gq):
            for p in range(2, 8):
                gq.wait_ge(b.sem_t, t0 + 2 * (p - 1))
                srcap = bass.AP(b.wgate, p * 4096, [[8 * 4096, P], [1, 4096]])
                b.wdma(gq, p % 2, b.gst[p % 2][:, :], srcap)
            gq.wait_ge(b.sem_t, t0 + 14)
            srcap = bass.AP(b.woutw, 0, [[8 * 4096, P], [1, 4096]])
            b.wdma(gq, 0, b.gst[0][:, :], srcap)
            gq.wait_ge(b.sem_t, t0 + 16)
            srcap = bass.AP(b.woutw, 4096, [[8 * 4096, P], [1, 4096]])
            b.wdma(gq, 1, b.gst[1][:, :], srcap)

        def t_body(t):
            for oc in range(NCH):
                p, oc2 = oc // 2, oc % 2
                ws = b.gst[p % 2]
                if oc2 == 0:
                    t.wait_ge(b.sem_w[p % 2], piece_cnt[p])
                if oc >= 2:
                    t.wait_ge(b.sem_s, s0 + oc - 1)
                for tt in range(2):
                    bank = bset[oc % 2][tt]
                    for c in range(NCH):
                        ins = t.matmul(bank[:, :],
                                       ws[:, c * 256 + oc2 * P:c * 256 + (oc2 + 1) * P],
                                       b.xn[:, c * TOKS + tt * 512:c * TOKS + (tt + 1) * 512],
                                       start=(c == 0), stop=(c == NCH - 1))
                b.c_t += 1
                ins.then_inc(b.sem_t, 1)

        def s_body(s):
            for oc in range(NCH):
                s.wait_ge(b.sem_t, t0 + oc + 1)
                if oc >= 2:
                    s.wait_ge(b.sem_v, v0 + oc - 1)
                for tt in range(2):
                    ins = s.activation(b.sg_b[oc % 2][:, tt * 512:(tt + 1) * 512],
                                       bset[oc % 2][tt][:, :], AF.Sigmoid)
                b.c_s += 1
                ins.then_inc(b.sem_s, 1)

        def v_body(v):
            for oc in range(NCH):
                v.wait_ge(b.sem_s, s0 + oc + 1)
                for tt in range(2):
                    v.scalar_tensor_tensor(
                        b.t2_b[0][:, tt * 512:(tt + 1) * 512],
                        b.sg_b[oc % 2][:, tt * 512:(tt + 1) * 512], b.rmsw[:, oc:oc + 1],
                        b.o_bf[:, oc * TOKS + tt * 512:oc * TOKS + (tt + 1) * 512],
                        ALU.mult, ALU.mult)
                for tt in range(2):
                    ins = v.tensor_tensor(
                        b.o_bf[:, oc * TOKS + tt * 512:oc * TOKS + (tt + 1) * 512],
                        b.t2_b[0][:, tt * 512:(tt + 1) * 512],
                        b.rstd_sb[:, tt * 512:(tt + 1) * 512], ALU.mult)
                b.c_v += 1
                ins.then_inc(b.sem_v, 1)

        self.block(gpsimd=g_body, tensor=t_body, scalar=s_body, vector=v_body)

    def emit_wout(self, restore_cnt=0):
        b = self
        bset = [(b.pA, b.pB), (b.pQK, b.pO)]
        t0, v0 = self.c_t, self.c_v
        piece_cnt = [self.exp_w[p % 2] + 16 * (p // 2) for p in range(8)]

        def g_body(gq):
            for p in range(2, 8):
                gq.wait_ge(b.sem_t, t0 + 2 * (p - 1))
                srcap = bass.AP(b.woutw, p * 4096, [[8 * 4096, P], [1, 4096]])
                b.wdma(gq, p % 2, b.gst[p % 2][:, :], srcap)

        def t_body(t):
            for oc in range(NCH):
                p, oc2 = oc // 2, oc % 2
                ws = b.gst[p % 2]
                if oc2 == 0:
                    t.wait_ge(b.sem_w[p % 2], piece_cnt[p])
                if oc >= 2:
                    t.wait_ge(b.sem_v, v0 + oc - 1)
                for tt in range(2):
                    bank = bset[oc % 2][tt]
                    for c in range(NCH):
                        ins = t.matmul(bank[:, :],
                                       ws[:, c * 256 + oc2 * P:c * 256 + (oc2 + 1) * P],
                                       b.o_bf[:, c * TOKS + tt * 512:c * TOKS + (tt + 1) * 512],
                                       start=(c == 0), stop=(c == NCH - 1))
                b.c_t += 1
                ins.then_inc(b.sem_t, 1)

        def v_body(v):
            for oc in range(NCH):
                v.wait_ge(b.sem_t, t0 + oc + 1)
                if oc == 0 and restore_cnt:
                    v.wait_ge(b.sem_y, restore_cnt)
                new = b.xslot[b.spare]
                old = b.xs(oc)
                for tt in range(2):
                    bank = bset[oc % 2][tt]
                    ins = v.tensor_tensor(new[:, tt * 512:(tt + 1) * 512], bank[:, :],
                                          old[:, tt * 512:(tt + 1) * 512], ALU.add)
                b.rotate(oc)
                b.c_v += 1
                ins.then_inc(b.sem_v, 1)

        self.block(gpsimd=g_body, tensor=t_body, vector=v_body)

    def emit_store(self):
        b = self

        def y_out(y):
            for c in range(NCH):
                self.ydma(y, b.out_t[c * P:(c + 1) * P, :], b.xs(c)[:, :])
            y.wait_ge(b.sem_y, b.exp_y)
        self.block(sync=y_out)

    def mark_phase(self, name):
        self.phase_marks.append((name, self.nc.n_instructions()))

    def build(self, upto="all"):
        self.phase_marks = []
        self.mark_phase("start")
        self.emit_setup()
        self.mark_phase("setup_done")
        self.load_ffn_weights()
        self.emit_ln("ffn1")
        self.mark_phase("ln1_done")
        if upto == "ln1":
            self.emit_store(); self.es.close(); return self.nc
        self.emit_ffn("ffn1")
        self.mark_phase("ffn1_done")
        self.tap("x1", self.xs(0), P, TOKS)
        if upto == "ffn1":
            self.emit_store(); self.es.close(); return self.nc
        self.emit_attn()
        self.mark_phase("attn_done")
        if upto == "attn":
            self.emit_store(); self.es.close(); return self.nc
        self.load_ffn_weights()
        self.emit_ln("ffn2")
        self.mark_phase("ln3_done")
        self.emit_ffn("ffn2")
        self.mark_phase("ffn2_done")
        self.es.close()
        return self.nc



def _host_consts():
    m = np.arange(BLK)
    l = np.arange(BLK)
    idx = (m[:, None] - l[None, :]).astype(np.float32)
    idx[m[:, None] > l[None, :]] = NEG_BIG
    idx_c = np.concatenate([idx[0:128, :], idx[128:256, :]], axis=1).astype(np.float32)
    gq = np.broadcast_to(-(l + 1).astype(np.float32), (NH, BLK)).copy()
    gk = np.broadcast_to(-(255 - m).astype(np.float32), (NH, BLK)).copy()
    gbd = np.broadcast_to((-256.0 * np.arange(NBLK)).astype(np.float32), (NH, NBLK)).copy()
    eexp = np.zeros((8, 512), np.float32)
    for j in range(512):
        eexp[j // 64, j] = 1.0
    return idx_c, gq, gk, gbd, eexp


def build_inputs(inputs):
    x = np.asarray(inputs["L0_attn_hidden_states"], np.float32)
    def stream_layout(w):
        o = w.shape[1]
        t = w.reshape(NCH, P, o // 256, 256).transpose(1, 2, 0, 3)
        return np.ascontiguousarray(t.reshape(P, -1))

    wqkv = stream_layout(np.asarray(inputs["L0_attn_w_qkv"], np.float32))
    wgate = stream_layout(np.asarray(inputs["L0_attn_w_gate"], np.float32))
    woutw = stream_layout(np.asarray(inputs["L0_attn_w_out"], np.float32))
    rmsw = np.asarray(inputs["L0_attn_rms_weight"], np.float32)
    slope = np.asarray(inputs["L0_attn_slope_rate"], np.float32).reshape(NH)
    A = np.asarray(inputs["L0_ffn_A"], np.float32)
    B = np.asarray(inputs["L0_ffn_B"], np.float32)
    Wr = np.asarray(inputs["L0_ffn_Wr"], np.float32)
    br = np.asarray(inputs["L0_ffn_br"], np.float32)

    a_t = np.ascontiguousarray(A.reshape(512, 2048).T)
    b_t = np.ascontiguousarray(B.transpose(0, 2, 1).reshape(512, 2048))
    wr_t = np.ascontiguousarray(Wr.T)
    rmsw_fm = np.ascontiguousarray(rmsw.reshape(NCH, P).T)
    idx_c, gq, gk, gbd, eexp = _host_consts()

    common = dict(
        wqkv=wqkv, wgate=wgate, woutw=woutw, a_t=a_t, b_t=b_t, wr_t=wr_t,
        br_in=np.ascontiguousarray(br.reshape(1, 8)), rmsw_in=rmsw_fm,
        s_col_in=np.ascontiguousarray(slope.reshape(NH, 1)),
        idx_in=idx_c, gq_in=gq, gk_in=gk, gbd_in=gbd, eexp_in=eexp,
    )
    in_maps = []
    for c in range(8):
        bidx, half = c // 2, c % 2
        xc = np.ascontiguousarray(x[bidx, half * TOKS:(half + 1) * TOKS, :].T)
        flag = np.full((P, 1), 1.0 if half == 1 else 0.0, np.float32)
        m = dict(common)
        m["x_in"] = xc
        m["flag_in"] = flag
        in_maps.append(m)
    return in_maps


def kernel(**inputs):
    in_maps = build_inputs(inputs)
    bld = Builder()
    nc = bld.build()
    res = run_bass_kernel_spmd(nc, in_maps, core_ids=list(range(8)))
    out = np.empty((4, 2048, 2048), np.float32)
    for c in range(8):
        bidx, half = c // 2, c % 2
        out[bidx, half * TOKS:(half + 1) * TOKS, :] = res.results[c]["out_t"].T
    return out
